# version 07
# baseline (speedup 1.0000x reference)
import numpy as np
import ml_dtypes
from contextlib import ExitStack

import concourse.bass as bass
import concourse.mybir as mybir
from concourse.bass_utils import run_bass_kernel_spmd
from concourse.bass_core_rust import IndirectOffsetOnAxis

F32 = mybir.dt.float32
BF16 = mybir.dt.bfloat16
U32 = mybir.dt.uint32
AF = mybir.ActivationFunctionType
ALU = mybir.AluOpType
AXL = mybir.AxisListType

NCORE = 8
B, S, D = 4, 4096, 1024
T = 2048
NT = 16
NH = 16
HD = 64
C = 128
NCH = 16
CLW = 129
CLTOT = NCH * CLW
NEG = -1e30
EPS_LN = 1e-5
CAPL = 384
CAP = 2560

ENGINES = ("sync", "gpsimd", "vector", "scalar", "tensor")


class Node:
    __slots__ = ("i", "eng", "fn", "deps", "w", "signal", "sig", "tag",
                 "psb", "coll", "csem")

    def __init__(self, i, eng, fn, deps, w, tag=""):
        self.i = i
        self.eng = eng
        self.fn = fn
        self.deps = deps
        self.w = w
        self.signal = False
        self.sig = 0
        self.tag = tag


class Em:

    def __init__(self, nc):
        self.nc = nc
        self.nodes = []
        self.barrier = None

    def set_barrier(self):
        last = {}
        colls = []
        for nd in self.nodes:
            last[(nd.eng, nd.w)] = nd
            if nd.coll:
                colls.append(nd)
        self.barrier = list(last.values()) + colls

    def n(self, eng, fn, deps=(), w=1, tag="", psb=None, coll=False):
        deps = [d for d in deps if d is not None]
        if self.barrier is not None:
            deps = deps + self.barrier
        nd = Node(len(self.nodes), eng, fn, deps, w, tag)
        nd.psb = psb or []
        nd.coll = coll
        nd.csem = None
        for d in deps:
            assert d.i < nd.i, f"dep order violated at {tag}"
        self.nodes.append(nd)
        return nd

    def dma(self, eng, fn, deps=(), tag=""):
        return self.n(eng, fn, deps, w=16, tag=tag)

    def build(self, block, stack, limit=None):
        if limit is not None:
            self.nodes = self.nodes[:limit]
        last_ps = {}
        for nd in self.nodes:
            for h in getattr(nd, "psb", []):
                k = id(h)
                p = last_ps.get(k)
                if p is not None and p.eng != nd.eng and p not in nd.deps:
                    nd.deps.append(p)
                last_ps[k] = nd
        sems = {e: stack.enter_context(self.nc.semaphore(f"S_{e}"))
                for e in ENGINES}
        dsems = {e: stack.enter_context(self.nc.semaphore(f"D_{e}"))
                 for e in ("sync", "gpsimd", "scalar", "vector")}

        ncoll = sum(1 for nd in self.nodes if getattr(nd, "coll", False))
        csems = [stack.enter_context(self.nc.semaphore(f"C_{i}"))
                 for i in range(ncoll)]
        ci = [0]
        for nd in self.nodes:
            if getattr(nd, "coll", False):
                nd.csem = csems[ci[0]]
                ci[0] += 1

        def semof(nd):
            if nd.csem is not None:
                return nd.csem
            if nd.w == 16:
                return dsems[nd.eng]
            return sems[nd.eng]

        def semkey(nd):
            if nd.csem is not None:
                return ("coll", nd.i)
            return (nd.eng, "d" if nd.w == 16 else "c")

        for nd in self.nodes:
            if nd.w == 16:
                nd.signal = True
            for d in nd.deps:
                if d.eng != nd.eng or nd.eng != "tensor" or d.w == 16:
                    d.signal = True
        last = {}
        for nd in self.nodes:
            last[semkey(nd)] = nd
        for nd in last.values():
            nd.signal = True
        cnt = {}
        for nd in self.nodes:
            if nd.signal:
                k = semkey(nd)
                cnt[k] = cnt.get(k, 0) + nd.w
                nd.sig = cnt[k]
        totals = dict(cnt)
        consumed = set()
        for nd in self.nodes:
            for d in nd.deps:
                consumed.add(d.i)

        def emit_stream(e_obj, ename):
            hwm = {}
            for nd in self.nodes:
                if nd.eng != ename:
                    continue
                for d in nd.deps:
                    if not d.signal:
                        continue
                    k = semkey(d)
                    if k == (ename, "c") and ename == "tensor":
                        continue
                    if d.sig > hwm.get(k, 0):
                        e_obj.wait_ge(semof(d), d.sig)
                        hwm[k] = d.sig
                ins = nd.fn(e_obj)
                assert ins is not None, f"emit fn returned None: {nd.tag}"
                if nd.signal:
                    ins.then_inc(semof(nd), nd.w)
                    if nd.w == 16 and nd.i in consumed:
                        e_obj.wait_ge(semof(nd), nd.sig)
            if ename == "sync":
                for k, v in totals.items():
                    if v > 0 and k != ("sync", "d") and k[0] != "coll":
                        e_obj.wait_ge(sems[k[0]] if k[1] == "c"
                                      else dsems[k[0]], v)
                if totals.get(("sync", "d"), 0) > 0:
                    e_obj.wait_ge(dsems["sync"], totals[("sync", "d")])

        @block.sync
        def _(e):
            emit_stream(e, "sync")

        @block.gpsimd
        def _(e):
            emit_stream(e, "gpsimd")

        @block.vector
        def _(e):
            emit_stream(e, "vector")

        @block.scalar
        def _(e):
            emit_stream(e, "scalar")

        @block.tensor
        def _(e):
            emit_stream(e, "tensor")


class NCProxy:

    def __init__(self, nc, pf):
        self._nc = nc
        self._pf = pf

    def __getattr__(self, a):
        return getattr(self._nc, a)

    def sbuf_tensor(self, name, *a, **k):
        return self._nc.sbuf_tensor(self._pf + name, *a, **k)

    def psum_tensor(self, name, *a, **k):
        return self._nc.psum_tensor(self._pf + name, *a, **k)

    def inline_tensor(self, arr, name):
        return self._nc.inline_tensor(arr, self._pf + name)



def _t(a):
    return np.ascontiguousarray(np.asarray(a, np.float32).T)


def stage_inputs(inp):
    x = np.asarray(inp["L0_attn_x"], np.float32)
    maps = []
    r = np.arange(128)[:, None]
    j = np.arange(144)[None, :]
    band = np.where((j >= r) & (j <= r + 16), 0.0, NEG).astype(np.float32)
    Wab64 = np.zeros((D, 64), np.float32)
    Wab64[:, 0:16] = np.asarray(inp["L0_attn_Wa"], np.float32)
    Wab64[:, 32:48] = np.asarray(inp["L0_attn_Wb"], np.float32)
    bab64 = np.zeros((64, 1), np.float32)
    bab64[0:16, 0] = np.asarray(inp["L0_attn_ba"], np.float32)
    bab64[32:48, 0] = np.asarray(inp["L0_attn_bb"], np.float32)
    Wg = np.asarray(inp["L1_attn_Wg"], np.float32)
    P = np.asarray(inp["L1_attn_P"], np.float32)
    P16 = np.zeros((128, D), np.float32)
    P16[:10] = P
    b1e = np.asarray(inp["L1_ffn_b1"], np.float32)
    gm = np.zeros((1, 2176), np.float32)
    gm[0, :2048] = 1.0
    gm[0, 2048:2058] = 0.5
    for c in range(NCORE):
        b, hc = c // 2, c % 2
        t0 = hc * T
        m = {}
        xctx = np.zeros((2176, D), np.float32)
        if hc == 1:
            xctx[127] = x[b, t0 - 1]
        xctx[128:] = x[b, t0:t0 + T]
        m["xctx"] = xctx
        m["Wq"] = np.asarray(inp["L0_attn_Wq"], np.float32)
        m["Wk"] = np.asarray(inp["L0_attn_Wk"], np.float32)
        m["Wv"] = np.asarray(inp["L0_attn_Wv"], np.float32)
        m["Wab"] = Wab64
        m["bab"] = bab64
        m["Wg0"] = np.asarray(inp["L0_attn_Wg"], np.float32)
        m["Wo0"] = np.asarray(inp["L0_attn_Wo"], np.float32)
        m["gn_cols"] = _t(np.asarray(inp["L0_attn_g_norm"],
                                     np.float32).reshape(8, 128))
        m["WrT"] = _t(inp["L0_ffn_W_r"])
        m["WkT"] = _t(inp["L0_ffn_W_k"])
        m["WvT"] = _t(inp["L0_ffn_W_v"])
        m["mur_c"] = _t(np.asarray(inp["L0_ffn_mu_r"], np.float32).reshape(8, 128))
        m["muk_c"] = _t(np.asarray(inp["L0_ffn_mu_k"], np.float32).reshape(8, 128))
        m["state_mask"] = np.full((1, 1), float(hc), np.float32)
        m["memWT"] = _t(inp["L1_attn_mem_W"])
        m["WiT"] = _t(inp["L1_attn_Wi"])
        m["Wl1T"] = _t(inp["L1_attn_Wl1"])
        m["Wl2T"] = _t(inp["L1_attn_Wl2"])
        m["WoT1"] = _t(inp["L1_attn_Wo"])
        m["WgAT"] = _t(Wg[:, :D])
        m["WgBT"] = _t(Wg[:, D:])
        m["W1T"] = _t(inp["L1_attn_W1"])
        m["W2T"] = _t(inp["L1_attn_W2"])
        m["PT16"] = _t(P16)
        m["bi_c"] = _t(np.asarray(inp["L1_attn_bi"], np.float32).reshape(8, 128))
        m["bl1_c"] = _t(np.asarray(inp["L1_attn_bl1"], np.float32).reshape(8, 128))
        m["bl2_c"] = _t(np.asarray(inp["L1_attn_bl2"], np.float32).reshape(8, 128))
        m["memb_c"] = _t(np.asarray(inp["L1_attn_mem_b"], np.float32).reshape(8, 128))
        m["b1_c"] = _t(np.asarray(inp["L1_attn_b1"], np.float32).reshape(32, 128))
        m["rows6"] = np.stack([
            np.asarray(inp["L1_attn_g1"], np.float32),
            np.asarray(inp["L1_attn_be1"], np.float32),
            np.asarray(inp["L1_attn_g2"], np.float32),
            np.asarray(inp["L1_attn_be2"], np.float32),
            np.asarray(inp["L1_attn_g3"], np.float32),
            np.asarray(inp["L1_attn_be3"], np.float32)])
        m["bg_c"] = _t(np.asarray(inp["L1_attn_bg"], np.float32).reshape(8, 128))
        m["bo_c"] = _t(np.asarray(inp["L1_attn_bo"], np.float32).reshape(8, 128))
        m["b2f_c"] = _t(np.asarray(inp["L1_attn_b2"], np.float32).reshape(8, 128))
        m["fg_lr"] = np.array(
            [[float(np.asarray(inp["L1_attn_fgate"]).ravel()[0]),
              float(np.asarray(inp["L1_attn_lrate"]).ravel()[0])]],
            np.float32)
        m["gradmask"] = gm
        wm0 = band.copy()
        if hc == 0:
            wm0[:, :16] = NEG
        m["winmask0"] = wm0
        m["band"] = band
        m["gwT"] = _t(inp["L1_ffn_gate_w"])
        m["w1e"] = np.asarray(inp["L1_ffn_w1"], np.float32)
        m["w2e"] = np.asarray(inp["L1_ffn_w2"], np.float32)
        m["b1e_c"] = np.stack([_t(b1e[e].reshape(32, 128)) for e in range(8)])
        m["b2e_r"] = np.asarray(inp["L1_ffn_b2"], np.float32)
        pm = np.zeros((8, 1), np.float32)
        pm[:c] = 1.0
        m["prior_mask"] = pm
        maps.append(m)
    return maps



L0_INPUTS = [
    ("xctx", (2176, D), F32),
    ("Wq", (D, D), F32), ("Wk", (D, D), F32), ("Wv", (D, D), F32),
    ("Wab", (D, 64), F32), ("bab", (64, 1), F32),
    ("Wg0", (D, D), F32), ("Wo0", (D, D), F32), ("gn_cols", (128, 8), F32),
    ("WrT", (D, D), F32), ("WkT", (D, D), F32), ("WvT", (D, D), F32),
    ("mur_c", (128, 8), F32), ("muk_c", (128, 8), F32),
    ("state_mask", (1, 1), F32),
]


def build_l0(dbg=None, limit=None, ctx=None):
    dbg = dbg or []
    if ctx is None:
        nc = bass.Bass("TRN2", target_bir_lowering=False)
    else:
        nc, em, block = ctx
    ins = {n: nc.dram_tensor(n, list(s), d, kind="ExternalInput")
           for n, s, d in L0_INPUTS}
    ident16 = nc.inline_tensor(np.eye(16, dtype=np.float32), "ident16")
    r_ = np.arange(128)[:, None]
    c_ = np.arange(128)[None, :]
    mlo_np = np.where(c_ < r_, 0.0, NEG).astype(np.float32)
    mup_np = np.where(c_ > r_, 0.0, NEG).astype(np.float32)
    mui_np = np.where(c_ >= r_, 0.0, NEG).astype(np.float32)
    t_mlo = nc.inline_tensor(mlo_np, "t_mlo")
    t_mup = nc.inline_tensor(mup_np, "t_mup")
    t_mui = nc.inline_tensor(mui_np, "t_mui")
    t_ones1f = nc.inline_tensor(np.ones((65, 128), np.float32), "t_ones1f")
    t_id64f = nc.inline_tensor(np.eye(64, dtype=np.float32), "t_id64f")
    t_bd = []
    for jj in range(8):
        bd = np.zeros((128, 16), ml_dtypes.bfloat16)
        bd[0:64, 2 * jj] = 1
        bd[64:128, 2 * jj + 1] = 1
        t_bd.append(nc.inline_tensor(bd, f"t_bd{jj}"))

    if ctx is None:
        x1_out = nc.dram_tensor("x1", [T, D], F32, kind="ExternalOutput")
    else:
        x1_out = nc.dram_tensor("x1", [T, D], F32)
    d_xn = nc.dram_tensor("d_xn", [2176, D], BF16)
    d_qf = nc.dram_tensor("d_qf", [D, T], BF16)
    d_kf = nc.dram_tensor("d_kf", [D, T], BF16)
    d_ktm = nc.dram_tensor("d_ktm", [T, D], BF16)
    d_vtm = nc.dram_tensor("d_vtm", [T, D], BF16)
    d_cl = nc.dram_tensor("d_cl", [16, CLTOT], F32)
    d_nclk = nc.dram_tensor("d_nclk", [16, CLTOT], F32)
    d_clpb = nc.dram_tensor("d_clpb", [16, CLTOT], F32)
    d_clq = nc.dram_tensor("d_clq", [16, CLTOT], F32)
    d_bet = nc.dram_tensor("d_bet", [16, T], F32)
    d_attn = nc.dram_tensor("d_attn", [T, D], F32)
    d_f0 = nc.dram_tensor("d_f0", [T, D], F32)
    d_rtm = nc.dram_tensor("d_rtm", [T, D], BF16)
    d_pk = [nc.dram_tensor(f"d_pk{i}", [128, 16 * 384], BF16)
            for i in range(NCH)]
    d_sx = nc.dram_tensor("d_sx", [64, 1024], BF16)
    d_sxo = nc.dram_tensor("d_sxo", [128, 1024], BF16)
    dbg_t = {}
    for name in dbg:
        shp = {"cl": (16, CLTOT), "clpb": (16, CLTOT), "clq": (16, CLTOT),
               "nclk": (16, CLTOT), "ab": (64, T), "y": (D, T),
               "s1": (64, 1024), "z": (D, T)}[name]
        dbg_t[name] = nc.dram_tensor("dbg_" + name, list(shp), F32,
                                     kind="ExternalOutput")

    if ctx is None:
        em = Em(nc)
    stack = ExitStack()
    with stack:
        sb = lambda name, shape, dt: stack.enter_context(
            nc.sbuf_tensor(name, list(shape), dt))
        if ctx is None:
            block = stack.enter_context(nc.Block())

        s1 = sb("s1", (64, 1024), BF16)
        s2 = sb("s2", (64, 1024), BF16)
        sG = sb("sG", (64, 1024), BF16)
        yf = [sb(f"yf{j}", (128, T), BF16) for j in range(8)]
        clc = sb("clc", (1, 256), F32)
        mcol = sb("mcol", (64, 1), F32)
        m_lo = sb("m_lo", (128, 128), F32)
        m_up = sb("m_up", (128, 128), F32)
        m_ui = sb("m_ui", (128, 128), F32)
        one1f = sb("one1f", (65, 128), F32)
        id16 = sb("id16", (16, 16), F32)
        id64f = sb("id64f", (64, 64), F32)
        bd16 = [sb(f"bd16_{j}", (128, 16), BF16) for j in range(8)]
        bab_s = sb("bab_s", (64, 1), F32)
        gn_s = sb("gn_s", (128, 8), F32)
        mur_s = sb("mur_s", (128, 8), F32)
        muk_s = sb("muk_s", (128, 8), F32)
        smk = sb("smk", (1, 1), F32)

        cld = []
        for tile, src in ([(m_lo, t_mlo), (m_up, t_mup), (m_ui, t_mui),
                           (one1f, t_ones1f), (id16, ident16),
                           (id64f, t_id64f), (bab_s, ins["bab"]),
                           (gn_s, ins["gn_cols"]), (mur_s, ins["mur_c"]),
                           (muk_s, ins["muk_c"]), (smk, ins["state_mask"])] +
                          [(bd16[j], t_bd[j]) for j in range(8)]):
            cld.append(em.dma("sync", lambda e, t=tile, s=src: e.dma_start(
                out=t[:], in_=s.ap()), tag="cload"))
        cldall = cld[-1]

        wbuf = [[sb(f"wbuf{i}_{k}", (128, D), BF16) for k in range(8)]
                for i in range(2)]
        pkt = [sb(f"pkt{i}", (128, 512), BF16) for i in range(4)]
        sqs = sb("sqs", (128, 512), BF16)
        wabt = sb("wabt", (128, 64 * 8), BF16)
        xt = [sb(f"xt{i}", (128, D), F32) for i in range(2)]
        xnt = [sb(f"xnt{i}", (128, D), BF16) for i in range(2)]
        st6 = sb("st6", (128, 12), F32)
        mv = sb("mv", (128, 2), F32)
        rstd = sb("rstd", (128, 1), F32)
        eps_c = sb("eps_c", (128, 1), F32)
        stack_x1 = ExitStack()
        xnf = [stack_x1.enter_context(
            nc.sbuf_tensor(f"xnf{j}", [128, 2176], BF16)) for j in range(8)]
        epn = em.n("vector", lambda e: e.memset(eps_c[:], EPS_LN), tag="eps")
        xn_st = []
        prev_use = [None, None]
        prev_ln = None
        for t in range(17):
            bi = t % 2
            ld = em.dma("sync", lambda e, t=t, bi=bi: e.dma_start(
                out=xt[bi][:], in_=ins["xctx"][128 * t:128 * (t + 1), :]),
                deps=[prev_use[bi]], tag=f"xld{t}")
            b1 = em.n("vector", lambda e, bi=bi: e.bn_stats(
                st6[:, 0:6], xt[bi][:, 0:512]), deps=[ld, prev_ln], tag="bn1")
            b2 = em.n("vector", lambda e, bi=bi: e.bn_stats(
                st6[:, 6:12], xt[bi][:, 512:1024]), deps=[ld], tag="bn2")
            b3 = em.n("vector", lambda e: e.bn_aggr(mv[:], st6[:]),
                      deps=[b1, b2], tag="bn3")
            sq = em.n("scalar", lambda e: e.activation(
                rstd[:], mv[:, 1:2], AF.Sqrt, bias=eps_c[:]),
                deps=[b3, epn], tag="sqrt")
            rc = em.n("vector", lambda e: e.reciprocal(rstd[:], rstd[:]),
                      deps=[sq], tag="recip")
            xn = em.n("vector", lambda e, bi=bi: e.tensor_scalar(
                xnt[bi][:], xt[bi][:], mv[:, 0:1], rstd[:],
                ALU.subtract, ALU.mult), deps=[rc], tag="ln")
            prev_use[bi] = xn
            prev_ln = xn
            xn_st.append(em.dma("sync", lambda e, t=t, bi=bi: e.dma_start(
                out=d_xn[128 * t:128 * (t + 1), :], in_=xnt[bi][:]),
                deps=[xn], tag=f"xst{t}"))

        xnf_nd = []
        for j in range(8):
            xnf_nd.append(em.dma("sync", lambda e, j=j: e.dma_start_transpose(
                xnf[j][:], d_xn[:, 128 * j:128 * (j + 1)]),
                deps=xn_st, tag=f"xnf{j}"))

        stack_rows = ExitStack()
        sbr = lambda name, shape, dt: stack_rows.enter_context(
            nc.sbuf_tensor(name, list(shape), dt))
        ab = sbr("ab", (64, T), F32)
        cl = sbr("cl", (16, CLTOT), F32)
        nclk = sbr("nclk", (16, CLTOT), F32)
        clpb = sbr("clpb", (16, CLTOT), F32)
        clq = sbr("clq", (16, CLTOT), F32)
        lnq = sbr("lnq", (16, T), F32)
        lnk = sbr("lnk", (16, T), F32)
        lnb = sbr("lnb", (16, T), F32)
        lna = sbr("lna", (16, T), F32)
        nacc = sbr("nacc", (16, T), F32)
        zz = sbr("zz", (16, 128), F32)
        stack_pj = ExitStack()
        psA = stack_pj.enter_context(nc.psum_tensor("psA", [128, 512], F32))
        psB = stack_pj.enter_context(nc.psum_tensor("psB", [128, 512], F32))
        psN4 = [stack_pj.enter_context(nc.psum_tensor(f"psN{i}", [16, 512], F32))
                for i in range(4)]
        ps64 = stack_pj.enter_context(nc.psum_tensor("ps64", [64, 512], F32))

        wl_free = [[None] * 8, [None] * 8]

        def wload(idx, src, extra=(), fd=D):
            nds = []
            for k in range(8):
                nds.append(em.dma("gpsimd", lambda e, k=k, idx=idx, src=src,
                                  fd=fd: e.dma_start(
                    out=wbuf[idx][k][:, 0:fd],
                    in_=src[128 * k:128 * (k + 1), :]),
                    deps=[wl_free[idx][k]] + list(extra), tag=f"wl{idx}_{k}"))
                wl_free[idx][k] = nds[-1]
            return nds

        ps_free = {0: None, 1: None}
        pk_free = [None] * 4
        pidx = [0]

        wf_norm = {}
        last_na = {}
        norm_rd = [None]

        def fm_proj(widx, wnd, dst, normps, parks, extra_dep=None):
            last_nmm = []
            last_na.clear()
            for jj in range(8):
                for g in range(4):
                    pi = pidx[0] % 2
                    pidx[0] += 1
                    ps = psA if pi == 0 else psB
                    mms = []
                    for k in range(8):
                        dep = [wnd[k], xnf_nd[k]]
                        if k == 0:
                            pf = ps_free[pi]
                            dep += (pf if isinstance(pf, list) else [pf])
                            dep.append(extra_dep)
                        mms.append(em.n("tensor",
                            lambda e, jj=jj, g=g, k=k, ps=ps, widx=widx:
                            e.matmul(ps[:],
                                     wbuf[widx][k][:, 128 * jj:128 * (jj + 1)],
                                     xnf[k][:, 128 + 512 * g:128 + 512 * (g + 1)],
                                     start=(k == 0), stop=(k == 7)),
                            deps=dep, tag=f"fm{jj}_{g}_{k}", psb=[ps]))
                    pki = pidx[0] % 4
                    ev = em.n("vector", lambda e, ps=ps, pki=pki:
                        e.tensor_copy(pkt[pki][:], ps[:]),
                        deps=[mms[-1], pk_free[pki]], tag="fmev",
                        psb=[ps])
                    frees = [ev]
                    if normps is not None:
                        sqn = em.n("scalar", lambda e, ps=ps: e.activation(
                            sqs[:], ps[:], AF.Square), deps=[mms[-1]],
                            tag="fsq", psb=[ps])
                        noff = pidx[0] % 4
                        psn = psN4[noff]
                        nmm = em.n("tensor", lambda e, jj=jj, psn=psn:
                            e.matmul(psn[:], bd16[jj][:], sqs[:],
                                     start=True, stop=True),
                            deps=[sqn, wf_norm.get(noff)] + cld[:1],
                            tag="fnm", psb=[psn])
                        if jj == 0:
                            na = em.n("vector", lambda e, g=g, psn=psn:
                                e.tensor_copy(nacc[:, 512 * g:512 * (g + 1)],
                                              psn[:]),
                                deps=[nmm, norm_rd[0]], tag="fna0",
                                psb=[psn])
                        else:
                            na = em.n("vector", lambda e, g=g, psn=psn:
                                e.tensor_add(nacc[:, 512 * g:512 * (g + 1)],
                                             nacc[:, 512 * g:512 * (g + 1)],
                                             psn[:]),
                                deps=[nmm, last_na.get(g)], tag="fna",
                                psb=[psn])
                        last_na[g] = na
                        wf_norm[noff] = na
                        if jj == 7:
                            last_nmm.append(na)
                        frees.append(sqn)
                    ps_free[pi] = frees
                    st = em.dma("sync", lambda e, jj=jj, g=g, pki=pki,
                                dst=dst: e.dma_start(
                        out=dst[128 * jj:128 * (jj + 1),
                                512 * g:512 * (g + 1)],
                        in_=pkt[pki][:]), deps=[ev], tag="fpark")
                    pk_free[pki] = st
                    parks.append(st)
            return last_nmm

        def recip_norm(normps, out_row, dep_mms, tagp):
            n1 = em.n("scalar", lambda e: e.activation(
                out_row[:], nacc[:], AF.Sqrt), deps=dep_mms,
                tag=tagp + "sq")
            norm_rd[0] = n1
            n2 = em.n("vector", lambda e: e.tensor_scalar_max(
                out_row[:], out_row[:], 1e-12), deps=[n1], tag=tagp + "mx")
            n3 = em.n("vector", lambda e: e.reciprocal(out_row[:], out_row[:]),
                      deps=[n2], tag=tagp + "rc")
            n4 = em.n("scalar", lambda e: e.activation(
                out_row[:], out_row[:], AF.Ln), deps=[n3], tag=tagp + "ln")
            return n4

        q_parks, k_parks = [], []
        wq_nd = wload(0, ins["Wq"])
        qn_mm = fm_proj(0, wq_nd, d_qf, psN4, q_parks)
        lnq_nd = recip_norm(None, lnq, qn_mm, "rq")
        wk_nd = wload(1, ins["Wk"])
        kn_mm = fm_proj(1, wk_nd, d_kf, psN4, k_parks, extra_dep=lnq_nd)
        lnk_nd = recip_norm(None, lnk, kn_mm, "rk")

        wv_nd = wload(0, ins["Wv"], extra=[qn_mm[-1]])
        tm_parks = []
        for t in range(NT):
            for h2 in range(2):
                for wsel, dst in ((1, d_ktm), (0, d_vtm)):
                    pi = pidx[0] % 2
                    pidx[0] += 1
                    ps = psA if pi == 0 else psB
                    mms = []
                    wnd = wk_nd if wsel == 1 else wv_nd
                    for k in range(8):
                        dep = [wnd[k], xnf_nd[k]]
                        if k == 0:
                            pf = ps_free[pi]
                            dep += (pf if isinstance(pf, list) else [pf])
                        mms.append(em.n("tensor",
                            lambda e, t=t, h2=h2, k=k, ps=ps, wsel=wsel:
                            e.matmul(ps[:],
                                     xnf[k][:, 128 + 128 * t:256 + 128 * t],
                                     wbuf[wsel][k][:, 512 * h2:512 * (h2 + 1)],
                                     start=(k == 0), stop=(k == 7)),
                            deps=dep, tag=f"tm{t}_{h2}", psb=[ps]))
                    pki = pidx[0] % 4
                    ev = em.n("vector", lambda e, ps=ps, pki=pki:
                        e.tensor_copy(pkt[pki][:], ps[:]),
                        deps=[mms[-1], pk_free[pki]], tag="tmev", psb=[ps])
                    ps_free[pi] = ev
                    st = em.dma("sync", lambda e, t=t, h2=h2, dst=dst,
                                pki=pki: e.dma_start(
                        out=dst[128 * t:128 * (t + 1),
                                512 * h2:512 * (h2 + 1)],
                        in_=pkt[pki][:]), deps=[ev], tag="tmpark")
                    pk_free[pki] = st
                    tm_parks.append(st)

        wab_nd = []
        for k in range(8):
            wab_nd.append(em.dma("gpsimd", lambda e, k=k: e.dma_start(
                out=wabt[:, 64 * k:64 * (k + 1)],
                in_=ins["Wab"][128 * k:128 * (k + 1), :]), tag="wabl"))
        ab_nd = []
        prev_sig = None
        for g in range(4):
            mms = []
            for k in range(8):
                dep = [wab_nd[k], xnf_nd[k]]
                if k == 0:
                    dep.append(prev_sig)
                mms.append(em.n("tensor", lambda e, g=g, k=k: e.matmul(
                    ps64[:, 0:512], wabt[:, 64 * k:64 * (k + 1)],
                    xnf[k][:, 128 + 512 * g:128 + 512 * (g + 1)],
                    start=(k == 0), stop=(k == 7)), deps=dep, tag="abmm",
                    psb=[ps64]))
            prev_sig = em.n("scalar", lambda e, g=g: e.activation(
                ab[:, 512 * g:512 * (g + 1)], ps64[:, 0:512], AF.Sigmoid,
                bias=bab_s[:]), deps=[mms[-1]] + cld[:1], tag="absig",
                psb=[ps64])
            ab_nd.append(prev_sig)

        zzn = em.n("vector", lambda e: e.memset(zz[:], 0.0), tag="zz")
        cl0 = em.n("vector", lambda e: e.memset(cl[:], 0.0), tag="cl0")
        nk0 = em.n("vector", lambda e: e.memset(nclk[:], 0.0), tag="nk0")
        pb0 = em.n("vector", lambda e: e.memset(clpb[:], 0.0), tag="pb0")
        cq0 = em.n("vector", lambda e: e.memset(clq[:], 0.0), tag="cq0")
        la1 = em.n("vector", lambda e: e.tensor_scalar_max(
            lna[:], ab[0:16, :], 1e-30), deps=ab_nd, tag="lamax")
        la2 = em.n("scalar", lambda e: e.activation(
            lna[:], lna[:], AF.Ln), deps=[la1], tag="laln")
        lb1 = em.n("vector", lambda e: e.tensor_scalar_max(
            lnb[:], ab[32:48, :], 1e-30), deps=ab_nd, tag="lbmax")
        lb2 = em.n("scalar", lambda e: e.activation(
            lnb[:], lnb[:], AF.Ln), deps=[lb1], tag="lbln")
        cl_nd = []
        for i in range(NCH):
            cl_nd.append(em.n("vector", lambda e, i=i: e.tensor_tensor_scan(
                cl[:, CLW * i + 1:CLW * (i + 1)],
                lna[:, 128 * i:128 * (i + 1)], zz[:], 0.0, ALU.add, ALU.add),
                deps=[la2, cl0, zzn], tag=f"clscan{i}"))
        row_nd = []
        for i in range(NCH):
            a = em.n("vector", lambda e, i=i: e.scalar_tensor_tensor(
                nclk[:, CLW * i + 1:CLW * (i + 1)],
                cl[:, CLW * i + 1:CLW * (i + 1)], -1.0,
                lnk[:, 128 * i:128 * (i + 1)], ALU.mult, ALU.add),
                deps=[cl_nd[i], lnk_nd], tag=f"nclk{i}")
            b = em.n("vector", lambda e, i=i: e.tensor_add(
                clpb[:, CLW * i + 1:CLW * (i + 1)],
                cl[:, CLW * i:CLW * i + 128],
                lnb[:, 128 * i:128 * (i + 1)]), deps=[cl_nd[i], lb2, pb0],
                tag=f"clpb_a{i}")
            b2_ = em.n("vector", lambda e, i=i: e.tensor_add(
                clpb[:, CLW * i + 1:CLW * (i + 1)],
                clpb[:, CLW * i + 1:CLW * (i + 1)],
                lnk[:, 128 * i:128 * (i + 1)]), deps=[b, lnk_nd],
                tag=f"clpb_b{i}")
            cq = em.n("vector", lambda e, i=i: e.tensor_add(
                clq[:, CLW * i + 1:CLW * (i + 1)],
                cl[:, CLW * i + 1:CLW * (i + 1)],
                lnq[:, 128 * i:128 * (i + 1)]), deps=[cl_nd[i], lnq_nd, cq0],
                tag=f"clq{i}")
            row_nd += [a, b2_, cq]

        bounce = []
        for dst, src in ((d_cl, cl), (d_nclk, nclk), (d_clpb, clpb),
                         (d_clq, clq)):
            bounce.append(em.dma("sync", lambda e, dst=dst, src=src:
                e.dma_start(out=dst.ap(), in_=src[:]),
                deps=cl_nd + row_nd, tag="bounce"))
        bounce.append(em.dma("sync", lambda e: e.dma_start(
            out=d_bet.ap(), in_=ab[32:48, :]), deps=ab_nd + row_nd,
            tag="bounceb"))
        clc_nd = em.dma("sync", lambda e: e.dma_start(
            out=clc[:].rearrange("p (a b) -> p a b", a=16), in_=bass.AP(
                tensor=d_cl, offset=128,
                ap=[[1, 1], [CLTOT, 16], [CLW, 16]]),
            allow_slow_non_contiguous=True), deps=bounce[:1], tag="clc")

        for name, src in (("cl", cl), ("clpb", clpb), ("clq", clq),
                          ("nclk", nclk), ("ab", ab)):
            if name in dbg_t:
                em.dma("sync", lambda e, s=src, dd=dbg_t[name]: e.dma_start(
                    out=dd.ap(), in_=s[:]), deps=cl_nd + row_nd + ab_nd,
                    tag="dbg" + name)

        stack_pj.close()
        stack_rows.close()
        stack_x1.close()

        stack_scan = ExitStack()
        sb2 = lambda name, shape, dt: stack_scan.enter_context(
            nc.sbuf_tensor(name, list(shape), dt))
        fl = [sb2(f"fl{i}", (1, 6144), F32) for i in range(2)]
        rsl = [sb2(f"rsl{i}", (16, 384), F32) for i in range(2)]
        ktm_t = [sb2(f"ktm{i}", (128, D), BF16) for i in range(2)]
        vtm_t = [sb2(f"vtm{i}", (128, D), BF16) for i in range(2)]
        qfp = [[sb2(f"qfs{i}_{p}", (64, 1024), BF16) for p in range(2)]
               for i in range(2)]
        kfp = [[sb2(f"kfs{i}_{p}", (64, 1024), BF16) for p in range(2)]
               for i in range(2)]
        tmc = sb2("tmc", (128, 48), F32)
        kgx = sb2("kgx", (128, 16), F32)
        pbx = sb2("pbx", (128, 16), F32)
        gct = sb2("gct", (64, 16), F32)
        gram = sb2("gram", (128, 256), BF16)
        eSa = sb2("eSa", (128, 128), F32)
        eEa = sb2("eEa", (128, 128), BF16)
        Lt_s = sb2("Lt_s", (128, 128), BF16)
        Lu_s = sb2("Lu_s", (128, 128), BF16)
        Mt_s = sb2("Mt_s", (128, 128), BF16)
        gqE = sb2("gqE", (128, 128), BF16)
        R_s = sb2("R_s", (128, 128), BF16)
        Xb = [sb2(f"Xb{i}", (128, 128), BF16) for i in range(2)]
        Tp = [sb2(f"Tp{i}", (128, 128), BF16) for i in range(2)]
        Up = [sb2(f"Up{i}", (128, 128), BF16) for i in range(2)]
        qh0 = sb2("qh0", (64, 128), BF16)
        qgb = sb2("qgb", (64, 128), BF16)
        Kg_all = sb2("Kg_all", (128, 1024), BF16)
        X_all = sb2("X_all", (128, 2048), BF16)
        At_all = sb2("At_all", (64, 1024), BF16)
        G_all = sb2("G_all", (64, 2048), BF16)
        ihg = sb2("ihg", (64, 64), F32)
        pkr = [sb2(f"pkr{i}", (128, 3072), BF16) for i in range(2)]

        stack_sc = ExitStack()
        psE = stack_sc.enter_context(nc.psum_tensor("psE", [128, 512], F32))
        psG = stack_sc.enter_context(nc.psum_tensor("psG", [128, 512], F32))
        psT = stack_sc.enter_context(nc.psum_tensor("psT", [128, 128], F32))
        psU = stack_sc.enter_context(nc.psum_tensor("psU", [128, 128], F32))
        psX = stack_sc.enter_context(nc.psum_tensor("psX", [128, 128], F32))
        psO1 = stack_sc.enter_context(nc.psum_tensor("psO1", [64, 512], F32))
        psO2 = stack_sc.enter_context(nc.psum_tensor("psO2", [64, 512], F32))
        psM = stack_sc.enter_context(nc.psum_tensor("psM", [128, 512], F32))

        s1z = em.n("vector", lambda e: e.memset(s1[:], 0.0), tag="s1z")
        wf = {}
        chain1_prev = [s1z] * NH
        fl_free = [None, None]
        str_free = [None, None]
        park_nodes = {}

        for i in range(NCH):
            bi = i % 2
            l_fl = []
            for row, src in ((0, d_nclk), (1, d_clpb), (2, d_clq)):
                l_fl.append(em.dma("sync", lambda e, row=row, src=src, i=i,
                                   bi=bi: e.dma_start(
                    out=fl[bi][0:1, 2048 * row:2048 * (row + 1)].rearrange(
                        "p (h t) -> p h t", h=16),
                    in_=bass.AP(tensor=src, offset=CLW * i + 1,
                                ap=[[1, 1], [CLTOT, 16], [1, 128]])),
                    deps=[fl_free[bi]] + bounce, tag=f"flld{i}"))
            l_k = em.dma("sync", lambda e, i=i, bi=bi: e.dma_start(
                out=ktm_t[bi][:], in_=d_ktm[128 * i:128 * (i + 1), :]),
                deps=[str_free[bi], wf.get("gb_last"), tm_parks[-1]],
                tag=f"ktmld{i}")
            l_v = em.dma("sync", lambda e, i=i, bi=bi: e.dma_start(
                out=vtm_t[bi][:], in_=d_vtm[128 * i:128 * (i + 1), :]),
                deps=[str_free[bi], tm_parks[-1]], tag=f"vtmld{i}")
            l_q = None
            l_kf = None
            for p in range(2):
                l_q = em.dma("sync", lambda e, i=i, bi=bi, p=p: e.dma_start(
                    out=qfp[bi][p][:].rearrange("p2 (c t) -> p2 c t", c=8),
                    in_=bass.AP(tensor=d_qf, offset=128 * i + p * 64 * 2048,
                                ap=[[2048, 64], [128 * 2048, 8], [1, 128]])),
                    deps=[str_free[bi], q_parks[-1], l_q], tag=f"qfld{i}")
                l_kf = em.dma("sync", lambda e, i=i, bi=bi, p=p: e.dma_start(
                    out=kfp[bi][p][:].rearrange("p2 (c t) -> p2 c t", c=8),
                    in_=bass.AP(tensor=d_kf, offset=128 * i + p * 64 * 2048,
                                ap=[[2048, 64], [128 * 2048, 8], [1, 128]])),
                    deps=[str_free[bi], k_parks[-1], l_kf], tag=f"kfld{i}")

            rld_nd = []
            for ci, (dsrc, off) in enumerate(
                    ((d_nclk, CLW * i + 1), (d_clpb, CLW * i + 1),
                     (d_bet, 128 * i))):
                rld_nd.append(em.dma("sync", lambda e, ci=ci, dsrc=dsrc,
                                     off=off, bi=bi: e.dma_start(
                    out=rsl[bi][:, 128 * ci:128 * (ci + 1)],
                    in_=bass.AP(tensor=dsrc, offset=off,
                                ap=[[CLTOT if dsrc is not d_bet else T, 16],
                                    [1, 128]])),
                    deps=[wf.get(("rsl", bi)), bounce[ci + 1]
                          if ci < 2 else bounce[4], bounce[0]],
                    tag=f"rld{ci}_{i}"))
            tr_ev = []
            for ci in range(3):
                tr = em.n("tensor", lambda e, ci=ci, bi=bi: e.transpose(
                    psM[:, 16 * ci:16 * (ci + 1)],
                    rsl[bi][:, 128 * ci:128 * (ci + 1)], id16[:]),
                          deps=[rld_nd[ci], cld[4]],
                          tag=f"tr{ci}_{i}", psb=[psM])
                ev = em.n("vector", lambda e, ci=ci: e.tensor_copy(
                    tmc[:, 16 * ci:16 * (ci + 1)],
                    psM[:, 16 * ci:16 * (ci + 1)]),
                    deps=[tr, wf.get(("tmc", ci))], tag=f"trev{ci}_{i}",
                    psb=[psM])
                tr_ev.append(ev)
            wf[("rsl", bi)] = tr_ev[2]

            kg1 = em.n("tensor", lambda e, i=i: e.matmul(
                psM[:, 48:64], one1f[0:1, :],
                clc[:].rearrange("p (h i2) -> p h i2", h=16)[:, :, i:i + 1],
                start=True, stop=True),
                deps=[clc_nd, cld[3]], tag=f"kg1_{i}", psb=[psM])
            kg2 = em.n("vector", lambda e: e.tensor_add(
                kgx[:], psM[:, 48:64], tmc[:, 0:16]),
                deps=[kg1, tr_ev[0], wf.get("kgx")], tag="kg2", psb=[psM])
            kg3 = em.n("scalar", lambda e: e.activation(
                kgx[:], kgx[:], AF.Exp), deps=[kg2], tag="kg3")
            pb1 = em.n("scalar", lambda e: e.activation(
                pbx[:], tmc[:, 16:32], AF.Exp),
                deps=[tr_ev[1], wf.get("pbx")], tag="pb1")
            gc1 = em.n("tensor", lambda e, i=i: e.matmul(
                psM[0:64, 64:80], one1f[0:1, 0:64],
                clc[:].rearrange("p (h i2) -> p h i2", h=16)[:, :, i:i + 1],
                start=True, stop=True),
                deps=[clc_nd, cld[3]], tag=f"gc1_{i}", psb=[psM])
            gc2 = em.n("scalar", lambda e: e.activation(
                gct[:], psM[0:64, 64:80], AF.Exp),
                deps=[gc1, wf.get("gct")], tag="gc2", psb=[psM])

            kg_nodes, x_nodes, at_nodes, g_nodes = [], [], [], []
            pp = park_nodes.get(i - 1, [None] * 4)
            for h in range(NH):
                jj, r0 = h // 2, 64 * (h % 2)
                bc = []
                for bidx in range(3):
                    bc.append(em.n("tensor", lambda e, bidx=bidx,
                                   h=h, bi=bi: e.matmul(
                        psE[:, 128 * bidx:128 * (bidx + 1)],
                        one1f[0:1, :],
                        fl[bi][0:1, 2048 * bidx + 128 * h:
                               2048 * bidx + 128 * (h + 1)],
                        start=True, stop=True),
                        deps=l_fl + [wf.get(("psE", bidx)), cld[3]],
                        tag=f"bc{bidx}_{i}_{h}", psb=[psE]))
                hp = h % 2
                gkk = em.n("tensor", lambda e, jj=jj, hp=hp, bi=bi: e.matmul(
                    psG[:, 0:128],
                    kfp[bi][hp][:, 128 * jj:128 * (jj + 1)],
                    kfp[bi][hp][:, 128 * jj:128 * (jj + 1)],
                    start=True, stop=True),
                    deps=[l_kf, wf.get("psG")], tag="gkk", psb=[psG])
                gkq = em.n("tensor", lambda e, jj=jj, hp=hp, bi=bi: e.matmul(
                    psG[:, 128:256],
                    kfp[bi][hp][:, 128 * jj:128 * (jj + 1)],
                    qfp[bi][hp][:, 128 * jj:128 * (jj + 1)],
                    start=True, stop=True), deps=[l_kf, l_q, gkk],
                    tag="gkq", psb=[psG])
                grm = em.n("vector", lambda e: e.tensor_copy(
                    gram[:], psG[:, 0:256]),
                    deps=[gkq, wf.get("gram")], tag="grmev", psb=[psG])
                wf["psG"] = grm
                e_nodes = []
                prev_exp = wf.get("eEa")
                for which, bidx, col, msk, dst in (
                        ("lt", 1, 0, m_up, Lt_s),
                        ("lu", 0, 16, m_lo, Lu_s),
                        ("mt", 2, 0, m_ui, Mt_s)):
                    st_ = em.n("vector", lambda e, bidx=bidx, col=col, h=h,
                               msk=msk: e.scalar_tensor_tensor(
                        eSa[:], psE[:, 128 * bidx:128 * (bidx + 1)],
                        tmc[:, col + h:col + h + 1], msk[:],
                        ALU.add, ALU.add),
                        deps=[bc[bidx], tr_ev[0], tr_ev[1], prev_exp,
                              cld[0]], tag=f"es_{which}", psb=[psE])
                    ex_ = em.n("scalar", lambda e: e.activation(
                        eEa[:], eSa[:], AF.Exp),
                        deps=[st_, wf.get("eEa_rd")], tag=f"ex_{which}")
                    gcol = 0 if which != "mt" else 128
                    mu_ = em.n("gpsimd", lambda e, dst=dst, gcol=gcol:
                        e.tensor_tensor(dst[:], eEa[:],
                                        gram[:, gcol:gcol + 128], ALU.mult),
                        deps=[ex_, grm,
                              wf.get("LtLu_rd") if which != "mt"
                              else wf.get("Mt_rd")],
                        tag=f"em_{which}")
                    prev_exp = ex_
                    wf["eEa_rd"] = mu_
                    e_nodes.append((st_, ex_, mu_))
                wf[("psE", 0)] = e_nodes[1][0]
                wf[("psE", 1)] = e_nodes[0][0]
                wf[("psE", 2)] = e_nodes[2][0]
                wf["eEa"] = e_nodes[2][1]
                lt_nd, lu_nd, mt_nd = (e_nodes[0][2], e_nodes[1][2],
                                       e_nodes[2][2])
                gq_ = em.n("scalar", lambda e: e.activation(
                    gqE[:], psE[:, 256:384], AF.Exp),
                    deps=[bc[2], wf.get("gqE")], tag="gqE", psb=[psE])
                wf[("psE", 2)] = gq_
                rv = em.n("vector", lambda e, h=h, bi=bi: e.tensor_scalar_mul(
                    R_s[:, 0:64], vtm_t[bi][:, 64 * h:64 * (h + 1)],
                    tmc[:, 32 + h:33 + h]),
                    deps=[l_v, tr_ev[2], wf.get("R_s")], tag="rv")
                rk = em.n("vector", lambda e, h=h, bi=bi: e.tensor_scalar_mul(
                    R_s[:, 64:128], ktm_t[bi][:, 64 * h:64 * (h + 1)],
                    pbx[:, h:h + 1]), deps=[l_k, pb1], tag="rk")
                kg_ = em.n("vector", lambda e, h=h, bi=bi: e.tensor_scalar_mul(
                    Kg_all[:, 64 * h:64 * (h + 1)],
                    ktm_t[bi][:, 64 * h:64 * (h + 1)],
                    kgx[:, h:h + 1]),
                    deps=[l_k, kg3, pp[0], wf.get(("Kg", h))], tag="kg")
                kg_nodes.append(kg_)
                a0 = em.n("tensor", lambda e: e.matmul(
                    psX[:], Lt_s[:], R_s[:], start=True, stop=True),
                    deps=[lt_nd, rv, rk], tag="a0", psb=[psX])
                x0 = em.n("vector", lambda e: e.tensor_sub(
                    Xb[0][:], R_s[:], psX[:]),
                    deps=[a0, wf.get(("Xb", 0))], tag="x0", psb=[psX])
                wf["R_s"] = x0
                tprev, uprev = Lt_s, Lu_s
                tprev_nd, uprev_nd = lt_nd, lu_nd
                xcur, xcur_nd = Xb[0], x0
                first_usq = [None]
                for k in range(1, 7):
                    tsq = em.n("tensor", lambda e, tprev=tprev, uprev=uprev:
                        e.matmul(psT[:], uprev[:], tprev[:],
                                 start=True, stop=True),
                        deps=[tprev_nd, uprev_nd], tag="tsq", psb=[psT])
                    tnew = Tp[k % 2]
                    tev = em.n("scalar", lambda e, tnew=tnew: e.copy(
                        tnew[:], psT[:]),
                        deps=[tsq, wf.get(("Tp", k % 2))], tag="tev",
                        psb=[psT])
                    if k < 6:
                        usq = em.n("tensor", lambda e, tprev=tprev,
                                   uprev=uprev: e.matmul(
                            psU[:], tprev[:], uprev[:],
                            start=True, stop=True),
                            deps=[tprev_nd, uprev_nd],
                            tag="usq", psb=[psU])
                        unew = Up[k % 2]
                        uev = em.n("scalar", lambda e, unew=unew: e.copy(
                            unew[:], psU[:]),
                            deps=[usq, wf.get(("Up", k % 2))], tag="uev",
                            psb=[psU])
                        if k == 1:
                            first_usq[0] = usq
                        uprev, uprev_nd = unew, uev
                    app = em.n("tensor", lambda e, tnew=tnew, xcur=xcur:
                        e.matmul(psX[:], tnew[:], xcur[:],
                                 start=True, stop=True),
                        deps=[tev, xcur_nd], tag="app", psb=[psX])
                    if k < 6:
                        xnew = Xb[(k + 1) % 2]
                        xadd = em.n("vector", lambda e, xnew=xnew, xcur=xcur:
                            e.tensor_add(xnew[:], xcur[:], psX[:]),
                            deps=[app, wf.get(("Xb", (k + 1) % 2))],
                            tag="xadd", psb=[psX])
                        wf[("Xb", (k + 1) % 2)] = None
                        xcur, xcur_nd = xnew, xadd
                    else:
                        xadd = em.n("vector", lambda e, h=h, xcur=xcur:
                            e.tensor_add(X_all[:, 128 * h:128 * (h + 1)],
                                         xcur[:], psX[:]),
                            deps=[app, pp[1], wf.get(("X", h))], tag="xfin",
                            psb=[psX])
                    wf[("Tp", k % 2)] = app
                    tprev, tprev_nd = tnew, tev
                xfin = xadd
                x_nodes.append(xfin)
                wf["LtLu_rd"] = first_usq[0]
                po1 = em.n("tensor", lambda e, h=h: e.matmul(
                    psO1[0:64, 0:128], X_all[:, 128 * h:128 * h + 64],
                    Mt_s[:], start=True, stop=True),
                    deps=[xfin, mt_nd], tag="po1", psb=[psO1])
                po2 = em.n("tensor", lambda e, h=h: e.matmul(
                    psO2[0:64, 0:128],
                    X_all[:, 128 * h + 64:128 * (h + 1)],
                    Mt_s[:], start=True, stop=True),
                    deps=[xfin, mt_nd], tag="po2", psb=[psO2])
                wf["Mt_rd"] = po2
                yev = em.n("scalar", lambda e, jj=jj, r0=r0, i=i: e.copy(
                    yf[jj][r0:r0 + 64, 128 * i:128 * (i + 1)],
                    psO1[0:64, 0:128]), deps=[po1], tag="yev", psb=[psO1])
                wf[("yfw", jj, r0, i)] = yev
                qcp = em.n("vector", lambda e, jj=jj, hp=h % 2, bi=bi:
                    e.tensor_copy(qh0[:],
                                  qfp[bi][hp][:, 128 * jj:128 * (jj + 1)]),
                    deps=[l_q, wf.get("qh0")], tag="qcp")
                qg_ = em.n("vector", lambda e, h=h: e.tensor_mul(
                    qgb[:], qh0[:], gqE[0:64, :]),
                    deps=[qcp, gq_, wf.get("qgb")], tag="qgb")
                wf["qh0"] = qg_
                wf["gqE"] = qg_
                gb = em.n("vector", lambda e, h=h: e.scalar_tensor_tensor(
                    G_all[0:64, 128 * h:128 * (h + 1)], psO2[0:64, 0:128],
                    -1.0, qgb[:], ALU.mult, ALU.add),
                    deps=[po2, qg_, pp[3], wf.get(("G", h))], tag="gb",
                    psb=[psO2])
                wf["gb_last"] = gb
                wf["qgb"] = gb
                g_nodes.append(gb)
                pc = em.n("tensor", lambda e, h=h: e.matmul(
                    psO2[0:64, 128:192],
                    X_all[:, 128 * h + 64:128 * (h + 1)],
                    Kg_all[:, 64 * h:64 * (h + 1)], start=True, stop=True),
                    deps=[xfin, kg_], tag="pc", psb=[psO2])
                ih = em.n("vector", lambda e, h=h: e.tensor_scalar_mul(
                    ihg[:], id64f[:], gct[:, h:h + 1]),
                    deps=[gc2, cld[5], wf.get("ihg")], tag="ih")
                at = em.n("vector", lambda e, h=h: e.tensor_sub(
                    At_all[0:64, 64 * h:64 * (h + 1)], ihg[:],
                    psO2[0:64, 128:192]),
                    deps=[pc, ih, pp[2], wf.get(("At", h))], tag="at",
                    psb=[psO2])
                wf["ihg"] = at
                at_nodes.append(at)
                c1a = em.n("tensor", lambda e, h=h: e.matmul(
                    psM[0:64, 128:192], At_all[0:64, 64 * h:64 * (h + 1)],
                    s1[:, 64 * h:64 * (h + 1)], start=True, stop=False),
                    deps=[at, chain1_prev[h]], tag="c1a", psb=[psM])
                c1b = em.n("tensor", lambda e, h=h: e.matmul(
                    psM[0:64, 128:192], Kg_all[:, 64 * h:64 * (h + 1)],
                    X_all[:, 128 * h:128 * h + 64], start=False, stop=True),
                    deps=[c1a, kg_, xfin], tag="c1b", psb=[psM])
                c1e = em.n("vector", lambda e, h=h: e.tensor_copy(
                    s1[:, 64 * h:64 * (h + 1)], psM[0:64, 128:192]),
                    deps=[c1b], tag="c1e", psb=[psM])
                chain1_prev[h] = c1e
                wf[("Kg", h)] = None
                wf[("X", h)] = None
                wf[("At", h)] = None
                wf[("G", h)] = None
            pks = []
            pks.append(em.dma("sync", lambda e, i=i: e.dma_start(
                out=bass.AP(tensor=d_pk[i], offset=0,
                            ap=[[6144, 128], [384, 16], [1, 64]]),
                in_=Kg_all[:].rearrange("p (h w) -> p h w", h=16)),
                deps=kg_nodes, tag=f"pkKg{i}"))
            pks.append(em.dma("sync", lambda e, i=i: e.dma_start(
                out=bass.AP(tensor=d_pk[i], offset=64,
                            ap=[[6144, 128], [384, 16], [1, 128]]),
                in_=X_all[:].rearrange("p (h w) -> p h w", h=16)),
                deps=x_nodes, tag=f"pkX{i}"))
            pks.append(em.dma("sync", lambda e, i=i: e.dma_start(
                out=bass.AP(tensor=d_pk[i], offset=192,
                            ap=[[6144, 64], [384, 16], [1, 64]]),
                in_=At_all[:].rearrange("p (h w) -> p h w", h=16)),
                deps=at_nodes, tag=f"pkAt{i}"))
            pks.append(em.dma("sync", lambda e, i=i: e.dma_start(
                out=bass.AP(tensor=d_pk[i], offset=256,
                            ap=[[6144, 64], [384, 16], [1, 128]]),
                in_=G_all[:].rearrange("p (h w) -> p h w", h=16)),
                deps=g_nodes, tag=f"pkG{i}"))
            park_nodes[i] = pks
            fl_free[bi] = bc[2]
            str_free[bi] = c1e

        sxw = em.dma("sync", lambda e: e.dma_start(
            out=d_sx.ap(), in_=s1[:]), deps=[chain1_prev[h] for h in
                                             range(NH)], tag="sxw")
        agc = em.n("gpsimd", lambda e: e.collective_compute(
            "AllGather", ALU.bypass,
            replica_groups=[[0, 1], [2, 3], [4, 5], [6, 7]],
            ins=[d_sx.ap().opt()], outs=[d_sxo.ap().opt()]),
            w=1, tag="agc", deps=[sxw], coll=True)
        sgr = em.dma("sync", lambda e: e.dma_start(
            out=sG[:], in_=d_sxo[0:64, :]), deps=[agc], tag="sgr")
        mc1 = em.n("tensor", lambda e: e.matmul(
            psM[0:64, 192:193], one1f[0:1, 0:64], smk[:],
            start=True, stop=True), deps=[cld[3], cld[10]],
            tag="mc1", psb=[psM])
        mc2 = em.n("vector", lambda e: e.tensor_copy(
            mcol[:], psM[0:64, 192:193]), deps=[mc1], tag="mc2", psb=[psM])
        s2i = []
        for h in range(NH):
            s2i.append(em.n("vector", lambda e, h=h: e.tensor_scalar_mul(
                s2[:, 64 * h:64 * (h + 1)], sG[:, 64 * h:64 * (h + 1)],
                mcol[:]), deps=[sgr, mc2], tag=f"s2i{h}"))

        pkr_free = [None, None]
        chain2_prev = list(s2i)
        y_upd = {(jj, i): None for jj in range(8) for i in range(NCH)}
        yev_all = []
        for i in range(NCH):
            lpk_h = [None, None]
            for hh in range(2):
                la_ = em.dma("sync", lambda e, i=i, hh=hh: e.dma_start(
                    out=pkr[hh][0:64, :],
                    in_=d_pk[i][0:64, 3072 * hh:3072 * (hh + 1)]),
                    deps=[pkr_free[hh]] + park_nodes[i], tag=f"lpkA{i}_{hh}")
                lpk_h[hh] = em.dma("sync", lambda e, i=i, hh=hh: e.dma_start(
                    out=pkr[hh][64:128, :].rearrange(
                        "p (h w) -> p h w", h=8)[:, :, 0:192],
                    in_=bass.AP(tensor=d_pk[i], offset=64 * 6144 + 3072 * hh,
                                ap=[[6144, 64], [384, 8], [1, 192]])),
                    deps=[pkr_free[hh], la_] + park_nodes[i],
                    tag=f"lpkB{i}_{hh}")
            for h in range(NH):
                jj, r0 = h // 2, 64 * (h % 2)
                bi = h // 8
                lpk = lpk_h[bi]
                hm = h % 8
                psA_ = psX if h % 2 == 0 else psT
                psB_ = psM if h % 2 == 0 else psU
                oc = em.n("tensor", lambda e, h=h, bi=bi, hm=hm,
                          psA_=psA_: e.matmul(
                    psA_[0:64, 0:128],
                    s2[:, 64 * h:64 * (h + 1)],
                    pkr[bi][0:64, 384 * hm + 256:384 * (hm + 1)],
                    start=True, stop=True),
                    deps=[lpk, chain2_prev[h]], tag="c2oc", psb=[psA_])
                ya = em.n("vector", lambda e, jj=jj, r0=r0, i=i,
                          psA_=psA_: e.tensor_add(
                    yf[jj][r0:r0 + 64, 128 * i:128 * (i + 1)],
                    yf[jj][r0:r0 + 64, 128 * i:128 * (i + 1)],
                    psA_[0:64, 0:128]),
                    deps=[oc, wf.get(("yfw", jj, r0, i))], tag="c2ya",
                    psb=[psA_])
                yev_all.append(ya)
                c2a = em.n("tensor", lambda e, h=h, bi=bi, hm=hm,
                           psB_=psB_: e.matmul(
                    psB_[0:64, 0:64],
                    pkr[bi][0:64, 384 * hm + 192:384 * hm + 256],
                    s2[:, 64 * h:64 * (h + 1)], start=True, stop=False),
                    deps=[lpk, chain2_prev[h]], tag="c2a", psb=[psB_])
                c2b = em.n("tensor", lambda e, h=h, bi=bi, hm=hm,
                           psB_=psB_: e.matmul(
                    psB_[0:64, 0:64],
                    pkr[bi][:, 384 * hm:384 * hm + 64],
                    pkr[bi][:, 384 * hm + 64:384 * hm + 128],
                    start=False, stop=True), deps=[c2a], tag="c2b",
                    psb=[psB_])
                c2e = em.n("vector", lambda e, h=h, psB_=psB_: e.tensor_copy(
                    s2[:, 64 * h:64 * (h + 1)], psB_[0:64, 0:64]),
                    deps=[c2b, oc, c2a], tag="c2e", psb=[psB_])
                chain2_prev[h] = c2e
                pkr_free[bi] = c2e

        if "y" in dbg_t:
            for j in range(8):
                yev_all.append(em.dma("gpsimd", lambda e, j=j: e.dma_start(
                    out=dbg_t["y"][128 * j:128 * (j + 1), :], in_=yf[j][:]),
                    deps=list(yev_all), tag="dbgy"))
        if "s1" in dbg_t:
            em.dma("gpsimd", lambda e: e.dma_start(
                out=dbg_t["s1"].ap(), in_=s1[:]),
                deps=[chain1_prev[h] for h in range(NH)], tag="dbgs1")
        stack_sc.close()

        stack_scan.close()

        xnf2 = [sb(f"xnf2_{j}", (128, 2176), BF16) for j in range(8)]
        xnf2_nd = []
        for j in range(8):
            xnf2_nd.append(em.dma("sync", lambda e, j=j: e.dma_start_transpose(
                xnf2[j][:], d_xn[:, 128 * j:128 * (j + 1)]),
                deps=xn_st, tag=f"xnf2_{j}"))
        t_ones128b = nc.inline_tensor(np.ones((128, 1), ml_dtypes.bfloat16),
                                      "t_ones128b")
        t_ones1b = nc.inline_tensor(np.ones((1, 128), ml_dtypes.bfloat16),
                                    "t_ones1b")
        one128b = sb("one128b", (128, 1), BF16)
        one1b = sb("one1b", (1, 128), BF16)
        ones1f2 = sb("ones1f2", (1, 128), F32)
        c128b = em.dma("sync", lambda e: e.dma_start(
            out=one128b[:], in_=t_ones128b.ap()), tag="c128b")
        c1b = em.dma("sync", lambda e: e.dma_start(
            out=one1b[:], in_=t_ones1b.ap()), tag="c1b")
        sqY = sb("sqY", (128, T), BF16)
        rms_r = sb("rms_r", (1, T), F32)
        rms_b = sb("rms_b", (1, T), BF16)
        gsil = sb("gsil", (128, 512), BF16)
        t1g = sb("t1g", (128, 512), BF16)
        pkf = [sb(f"pkf{i}", (128, 512), F32) for i in range(2)]

        stack_p5 = ExitStack()
        psY4 = [stack_p5.enter_context(nc.psum_tensor(f"psY{i}", [1, 512], F32))
                for i in range(4)]
        psC = stack_p5.enter_context(nc.psum_tensor("psC", [128, 512], F32))
        psD = stack_p5.enter_context(nc.psum_tensor("psD", [128, 512], F32))
        psR = stack_p5.enter_context(nc.psum_tensor("psR", [128, 512], F32))

        yacc = sb("yacc", (1, T), F32)
        ysq_mm = []
        yps_wf = {}
        ylast = {}
        prev_sq = None
        for j in range(8):
            sq = em.n("scalar", lambda e, j=j: e.activation(
                sqY[:], yf[j][:], AF.Square),
                deps=yev_all + [prev_sq], tag=f"ysq{j}")
            for g in range(4):
                off = (j * 4 + g) % 4
                psy = psY4[off]
                nm = em.n("tensor", lambda e, g=g, psy=psy: e.matmul(
                    psy[:], one128b[:],
                    sqY[:, 512 * g:512 * (g + 1)],
                    start=True, stop=True),
                    deps=[sq, c128b, yps_wf.get(off)], tag=f"ynm{j}_{g}",
                    psb=[psy])
                if j == 0:
                    na = em.n("vector", lambda e, g=g, psy=psy: e.tensor_copy(
                        yacc[0:1, 512 * g:512 * (g + 1)],
                        psy[:]), deps=[nm], tag="yna0", psb=[psy])
                else:
                    na = em.n("vector", lambda e, g=g, psy=psy: e.tensor_add(
                        yacc[0:1, 512 * g:512 * (g + 1)],
                        yacc[0:1, 512 * g:512 * (g + 1)],
                        psy[:]),
                        deps=[nm, ylast.get(g)], tag="yna", psb=[psy])
                ylast[g] = na
                yps_wf[off] = na
                if j == 7:
                    ysq_mm.append(na)
            prev_sq = nm
        rr1 = em.n("vector", lambda e: e.tensor_scalar(
            rms_r[:], yacc[0:1, :], 1.0 / D, EPS_LN, ALU.mult, ALU.add),
            deps=ysq_mm, tag="rr1")
        rr2 = em.n("scalar", lambda e: e.activation(
            rms_r[:], rms_r[:], AF.Sqrt), deps=[rr1], tag="rr2")
        rr3 = em.n("vector", lambda e: e.reciprocal(rms_r[:], rms_r[:]),
                   deps=[rr2], tag="rr3")
        rr4 = em.n("vector", lambda e: e.tensor_copy(rms_b[:], rms_r[:]),
                   deps=[rr3], tag="rr4")

        wg_nd = wload(1, ins["Wg0"], extra=[tm_parks[-1]])
        z_nd = {}
        prev_ps = {0: None, 1: None}
        for j in range(8):
            for g in range(4):
                mms = []
                for k in range(8):
                    dep = [wg_nd[k], xnf2_nd[k]]
                    if k == 0:
                        dep.append(prev_ps[0])
                    mms.append(em.n("tensor", lambda e, j=j, g=g, k=k:
                        e.matmul(psC[:],
                                 wbuf[1][k][:, 128 * j:128 * (j + 1)],
                                 xnf2[k][:, 128 + 512 * g:128 + 512 * (g + 1)],
                                 start=(k == 0), stop=(k == 7)),
                        deps=dep, tag=f"gmm{j}_{g}_{k}", psb=[psC]))
                sil = em.n("scalar", lambda e: e.activation(
                    gsil[:], psC[:], AF.Silu), deps=[mms[-1],
                    z_nd.get((j, g - 1) if g else (j - 1, 3))], tag="gsil",
                    psb=[psC])
                prev_ps[0] = sil
                rb = em.n("tensor", lambda e, g=g: e.matmul(
                    psR[:], one1b[:], rms_b[0:1, 512 * g:512 * (g + 1)],
                    start=True, stop=True),
                    deps=[rr4, c1b], tag="rbz", psb=[psR])
                tz = em.n("vector", lambda e, j=j: e.scalar_tensor_tensor(
                    t1g[:], gsil[:], gn_s[:, j:j + 1], psR[:],
                    ALU.mult, ALU.mult), deps=[sil, rb, cld[7],
                    wf.get("t1g")], tag="tz", psb=[psR])
                zz_ = em.n("vector", lambda e, j=j, g=g: e.tensor_mul(
                    yf[j][:, 512 * g:512 * (g + 1)],
                    yf[j][:, 512 * g:512 * (g + 1)], t1g[:]),
                    deps=[tz] + yev_all[-1:], tag="zmul")
                wf["t1g"] = zz_
                z_nd[(j, g)] = zz_
        if "z" in dbg_t:
            for j in range(8):
                em.dma("gpsimd", lambda e, j=j: e.dma_start(
                    out=dbg_t["z"][128 * j:128 * (j + 1), :], in_=yf[j][:]),
                    deps=[z_nd[(j, g)] for g in range(4)], tag="dbgz")

        wo_nd = wload(0, ins["Wo0"], extra=[tm_parks[-1], z_nd[(7, 3)]])
        attn_st = []
        pkf_free = [None, None]
        prev_wops = None
        for t in range(NT):
            for h2 in range(2):
                mms = []
                for k in range(8):
                    dep = [wo_nd[k], z_nd[(k, t // 4)]]
                    if k == 0:
                        dep.append(prev_wops)
                    mms.append(em.n("tensor", lambda e, t=t, h2=h2, k=k:
                        e.matmul(psD[:],
                                 yf[k][:, 128 * t:128 * (t + 1)],
                                 wbuf[0][k][:, 512 * h2:512 * (h2 + 1)],
                                 start=(k == 0), stop=(k == 7)),
                        deps=dep, tag=f"womm{t}_{h2}_{k}", psb=[psD]))
                fi = (2 * t + h2) % 2
                ev = em.n("vector", lambda e, fi=fi: e.tensor_copy(
                    pkf[fi][:], psD[:]), deps=[mms[-1], pkf_free[fi]],
                    tag="woev", psb=[psD])
                prev_wops = ev
                st = em.dma("sync", lambda e, t=t, h2=h2, fi=fi: e.dma_start(
                    out=d_attn[128 * t:128 * (t + 1),
                               512 * h2:512 * (h2 + 1)], in_=pkf[fi][:]),
                    deps=[ev], tag="wopark")
                pkf_free[fi] = st
                attn_st.append(st)

        mur1m = sb("mur1m", (128, 8), F32)
        muk1m = sb("muk1m", (128, 8), F32)
        n1m_r = em.n("vector", lambda e: e.tensor_scalar(
            mur1m[:], mur_s[:], -1.0, 1.0, ALU.mult, ALU.add),
            deps=[cld[8]], tag="mur1m")
        n1m_k = em.n("vector", lambda e: e.tensor_scalar(
            muk1m[:], muk_s[:], -1.0, 1.0, ALU.mult, ALU.add),
            deps=[cld[9]], tag="muk1m")
        xr = [sb(f"xr{j}", (128, T), BF16) for j in range(8)]
        xr_t1 = sb("xr_t1", (128, T), BF16)

        def mix(mu_s, mu1m, mdep, extra_deps):
            nds = []
            for j in range(8):
                a = em.n("vector", lambda e, j=j, mu_s=mu_s: e.tensor_scalar_mul(
                    xr_t1[:], xnf2[j][:, 128:2176], mu_s[:, j:j + 1]),
                    deps=[xnf_nd[j], mdep, wf.get("xr_t1")] + extra_deps,
                    tag=f"mixa{j}")
                b = em.n("vector", lambda e, j=j, mu1m=mu1m:
                    e.scalar_tensor_tensor(
                        xr[j][:], xnf2[j][:, 127:2175], mu1m[:, j:j + 1],
                        xr_t1[:], ALU.mult, ALU.add),
                    deps=[a, wf.get(("xrw", j))], tag=f"mixb{j}")
                wf["xr_t1"] = b
                nds.append(b)
            return nds

        xr_nd = mix(mur_s, mur1m, n1m_r, [])
        wr_nd = wload(1, ins["WrT"], extra=[z_nd[(7, 3)], prev_wops])
        r_st = {}
        prev_rps = None
        for t in range(NT):
            for h2 in range(2):
                mms = []
                for k in range(8):
                    dep = [wr_nd[k], xr_nd[k]]
                    if k == 0:
                        dep.append(prev_rps)
                    mms.append(em.n("tensor", lambda e, t=t, h2=h2, k=k:
                        e.matmul(psC[:],
                                 xr[k][:, 128 * t:128 * (t + 1)],
                                 wbuf[1][k][:, 512 * h2:512 * (h2 + 1)],
                                 start=(k == 0), stop=(k == 7)),
                        deps=dep, tag=f"rmm{t}_{h2}", psb=[psC]))
                pki = (2 * t + h2) % 4
                sg = em.n("scalar", lambda e, pki=pki: e.activation(
                    pkt[pki][:], psC[:], AF.Sigmoid),
                    deps=[mms[-1], pk_free[pki]], tag="rsig", psb=[psC])
                prev_rps = sg
                st = em.dma("sync", lambda e, t=t, h2=h2, pki=pki:
                    e.dma_start(out=d_rtm[128 * t:128 * (t + 1),
                                          512 * h2:512 * (h2 + 1)],
                                in_=pkt[pki][:]), deps=[sg], tag="rpark")
                pk_free[pki] = st
                r_st[(t, h2)] = st

        xk_nd = mix(muk_s, muk1m, n1m_k, [prev_rps])
        wk2_nd = wload(0, ins["WkT"], extra=[prev_wops])
        rel_s = sb("rel_s", (128, 512), BF16)
        ka_nd = {}
        prev_kps = None
        for j in range(8):
            for g in range(4):
                mms = []
                for k in range(8):
                    dep = [wk2_nd[k], xk_nd[k]]
                    if k == 0:
                        dep.append(prev_kps)
                    mms.append(em.n("tensor", lambda e, j=j, g=g, k=k:
                        e.matmul(psC[:],
                                 wbuf[0][k][:, 128 * j:128 * (j + 1)],
                                 xr[k][:, 512 * g:512 * (g + 1)],
                                 start=(k == 0), stop=(k == 7)),
                        deps=dep, tag=f"kamm{j}_{g}", psb=[psC]))
                rl = em.n("vector", lambda e: e.tensor_relu(
                    rel_s[:], psC[:]), deps=[mms[-1], wf.get("rel_s")],
                    tag="karelu", psb=[psC])
                prev_kps = rl
                kq = em.n("scalar", lambda e, j=j, g=g: e.activation(
                    xnf2[j][:, 512 * g:512 * (g + 1)], rel_s[:], AF.Square),
                    deps=[rl, wf.get(("kaw", j, g))], tag="kasq")
                wf["rel_s"] = kq
                ka_nd[(j, g)] = kq
        wv2_nd = wload(1, ins["WvT"], extra=[prev_rps])
        rld = sb("rld", (128, 512), BF16)
        f0_st = []
        prev_fps = None
        for t in range(NT):
            for h2 in range(2):
                lr = em.dma("sync", lambda e, t=t, h2=h2: e.dma_start(
                    out=rld[:], in_=d_rtm[128 * t:128 * (t + 1),
                                          512 * h2:512 * (h2 + 1)]),
                    deps=[r_st[(t, h2)], wf.get("rld")], tag="rldld")
                mms = []
                for k in range(8):
                    dep = [wv2_nd[k], ka_nd[(k, t // 4)]]
                    if k == 0:
                        dep.append(prev_fps)
                    mms.append(em.n("tensor", lambda e, t=t, h2=h2, k=k:
                        e.matmul(psD[:],
                                 xnf2[k][:, 128 * t:128 * (t + 1)],
                                 wbuf[1][k][:, 512 * h2:512 * (h2 + 1)],
                                 start=(k == 0), stop=(k == 7)),
                        deps=dep, tag=f"fmm{t}_{h2}", psb=[psD]))
                fi = (2 * t + h2) % 2
                fz = em.n("vector", lambda e, fi=fi: e.tensor_mul(
                    pkf[fi][:], psD[:], rld[:]),
                    deps=[mms[-1], lr, pkf_free[fi]], tag="f0mul",
                    psb=[psD])
                wf["rld"] = fz
                prev_fps = fz
                st = em.dma("sync", lambda e, t=t, h2=h2, fi=fi: e.dma_start(
                    out=d_f0[128 * t:128 * (t + 1),
                             512 * h2:512 * (h2 + 1)], in_=pkf[fi][:]),
                    deps=[fz], tag="f0park")
                pkf_free[fi] = st
                f0_st.append(st)

        res_a = sb("res_a", (128, D), F32)
        res_b = sb("res_b", (128, D), F32)
        for t in range(NT):
            la = em.dma("sync", lambda e, t=t: e.dma_start(
                out=res_a[:], in_=ins["xctx"][128 + 128 * t:256 + 128 * t, :]),
                deps=[wf.get("res_st")], tag="resx")
            lb = em.dma("sync", lambda e, t=t: e.dma_start(
                out=res_b[:], in_=d_attn[128 * t:128 * (t + 1), :]),
                deps=attn_st + [wf.get("res_st")], tag="resat")
            s1_ = em.n("vector", lambda e: e.tensor_add(
                res_a[:], res_a[:], res_b[:]), deps=[la, lb], tag="resadd1")
            lb2 = em.dma("sync", lambda e, t=t: e.dma_start(
                out=res_b[:], in_=d_f0[128 * t:128 * (t + 1), :]),
                deps=f0_st + [s1_], tag="resf0")
            s2_ = em.n("vector", lambda e: e.tensor_add(
                res_a[:], res_a[:], res_b[:]), deps=[lb2], tag="resadd2")
            st = em.dma("sync", lambda e, t=t: e.dma_start(
                out=x1_out[128 * t:128 * (t + 1), :], in_=res_a[:]),
                deps=[s2_], tag="x1st")
            wf["res_st"] = st

        stack_p5.close()
        if ctx is None:
            em.build(block, stack, limit=limit)
    if ctx is not None:
        return x1_out
    return nc



L1_INPUTS = [
    ("x1in", (T, D), F32),
    ("memWT", (D, D), F32), ("WiT", (D, D), F32), ("Wl1T", (D, D), F32),
    ("Wl2T", (D, D), F32), ("WoT1", (D, D), F32),
    ("WgAT", (D, D), F32), ("WgBT", (D, D), F32),
    ("W1T", (D, 4096), F32), ("W2T", (4096, D), F32),
    ("PT16", (D, 128), F32),
    ("bi_c", (128, 8), F32), ("bl1_c", (128, 8), F32),
    ("bl2_c", (128, 8), F32), ("memb_c", (128, 8), F32),
    ("b1_c", (128, 32), F32), ("rows6", (6, D), F32),
    ("bg_c", (128, 8), F32), ("bo_c", (128, 8), F32),
    ("b2f_c", (128, 8), F32),
    ("fg_lr", (1, 2), F32), ("gradmask", (1, 2176), F32),
    ("winmask0", (128, 144), F32), ("band", (128, 144), F32),
    ("gwT", (D, 8), F32), ("w1e", (8, D, 4096), F32),
    ("w2e", (8, 4096, D), F32), ("b1e_c", (8, 128, 32), F32),
    ("b2e_r", (8, D), F32), ("prior_mask", (8, 1), F32),
]
NKV = 2176
GRPS = [(0, 512), (512, 512), (1024, 512), (1536, 512), (2048, 128)]


def build_l1(dbg=None, ctx=None, x1_src=None):
    dbg = dbg or []
    if ctx is None:
        nc = bass.Bass("TRN2", target_bir_lowering=False)
    else:
        nc0, em, block = ctx
        nc = NCProxy(nc0, "q")
        em.set_barrier()
    ins = {n: nc.dram_tensor(n, list(s), d, kind="ExternalInput")
           for n, s, d in L1_INPUTS
           if ctx is None or n != "x1in"}
    if ctx is not None:
        ins["x1in"] = x1_src
    yout = nc.dram_tensor("yout", [T, D], F32, kind="ExternalOutput")
    ident16 = nc.inline_tensor(np.eye(16, dtype=np.float32), "ident16")
    t_id128b = nc.inline_tensor(np.eye(128, dtype=ml_dtypes.bfloat16),
                                "t_id128b")
    t_ones1b = nc.inline_tensor(np.ones((1, 128), ml_dtypes.bfloat16),
                                "t_ones1b")
    t_ones128b = nc.inline_tensor(np.ones((128, 1), ml_dtypes.bfloat16),
                                  "t_ones128b")
    t_iota8 = nc.inline_tensor(
        np.tile(np.arange(8, dtype=np.float32), (128, 1)), "t_iota8")
    t_ones1f2 = nc.inline_tensor(np.ones((1, 128), np.float32), "t_ones1f2")

    d_hx = nc.dram_tensor("d_hx", [16, D], F32)
    d_hxo = nc.dram_tensor("d_hxo", [32, D], F32)
    d_xe = nc.dram_tensor("d_xe", [NKV, D], BF16)
    d_x1s = nc.dram_tensor("d_x1s", [T, D], F32)
    d_x1sb = nc.dram_tensor("d_x1sb", [T, D], BF16)
    d_kp = nc.dram_tensor("d_kp", [D, NKV], BF16)
    d_hdd = nc.dram_tensor("d_hdd", [D, NKV], BF16)
    d_diff = nc.dram_tensor("d_diff", [D, NKV], BF16)
    d_grad = nc.dram_tensor("d_grad", [D, D], F32)
    d_grado = nc.dram_tensor("d_grado", [D, D], F32)
    d_s = nc.dram_tensor("d_s", [D, T], BF16)
    d_x2 = nc.dram_tensor("d_x2", [T, D], F32)
    d_x2b = nc.dram_tensor("d_x2b", [T, D], BF16)
    d_h = nc.dram_tensor("d_h", [4096, T], BF16)
    d_ffb = nc.dram_tensor("d_ffb", [D, T], BF16)
    d_a1 = nc.dram_tensor("d_a1", [T, D], F32)
    d_disp = nc.dram_tensor("d_disp", [8 * CAPL, D], BF16)
    d_eout = nc.dram_tensor("d_eout", [8 * CAPL, D], BF16)
    d_cnt = nc.dram_tensor("d_cnt", [8, 1], F32)
    d_cnto = nc.dram_tensor("d_cnto", [64, 1], F32)
    dbg_t = {}
    for name in dbg:
        shp = {"x1s": (T, D), "hdd": (D, NKV), "grad": (D, D),
               "mo": (D, T), "x2": (T, D), "a1": (T, D),
               "moe": (T, D), "idx": (T, 1)}[name]
        dt_ = U32 if name == "idx" else F32
        dbg_t[name] = nc.dram_tensor("dbg_" + name, list(shp), dt_,
                                     kind="ExternalOutput")

    if ctx is None:
        em = Em(nc)
    stack = ExitStack()
    with stack:
        sb = lambda name, shape, dt: stack.enter_context(
            nc.sbuf_tensor(name, list(shape), dt))
        if ctx is None:
            block = stack.enter_context(nc.Block())

        wm0 = sb("wm0", (128, 144), F32)
        wmb = sb("wmb", (128, 144), F32)
        rows6f = sb("rows6f", (1, D), BF16)
        gB = [sb(f"gB{r}", (128, D), BF16) for r in range(6)]
        gmB = sb("gmB", (128, NKV), BF16)
        gmrow = sb("gmrow", (1, NKV), BF16)
        bi_s = sb("bi_s", (128, 8), F32)
        bl1_s = sb("bl1_s", (128, 8), F32)
        bl2_s = sb("bl2_s", (128, 8), F32)
        memb_s = sb("memb_s", (128, 8), F32)
        dfb_s = sb("dfb_s", (128, 8), F32)
        bg_s = sb("bg_s", (128, 8), F32)
        bo_s = sb("bo_s", (128, 8), F32)
        b2f_s = sb("b2f_s", (128, 8), F32)
        b1_s = sb("b1_s", (128, 32), F32)
        fglr = sb("fglr", (1, 2), F32)
        fgc = sb("fgc", (128, 2), F32)
        pri_s = sb("pri_s", (8, 1), F32)
        one1b = sb("one1b", (1, 128), BF16)
        ones1f2 = sb("ones1f2", (1, 128), F32)
        one128b = sb("one128b", (128, 1), BF16)
        id16 = sb("id16", (16, 16), F32)
        id128b = sb("id128b", (128, 128), BF16)
        iota8 = sb("iota8", (128, 8), F32)
        eps12 = sb("eps12", (128, 1), F32)
        wbuf = [[sb(f"wbuf{i}_{k}", (128, D), BF16) for k in range(8)]
                for i in range(2)]
        pkt = [sb(f"pkt{i}", (128, 512), BF16) for i in range(4)]
        xa = [sb(f"xa{i}", (128, D), F32) for i in range(2)]
        xb_ = [sb(f"xb{i}", (128, D), F32) for i in range(2)]
        xc = [sb(f"xc{i}", (128, D), BF16) for i in range(2)]
        st6 = sb("st6", (128, 12), F32)
        mv = sb("mv", (128, 2), F32)
        rstd = sb("rstd", (128, 1), F32)

        ps1 = stack.enter_context(nc.psum_tensor("ps1", [128, 512], F32))
        ps2 = stack.enter_context(nc.psum_tensor("ps2", [128, 512], F32))
        ps3 = stack.enter_context(nc.psum_tensor("ps3", [128, 512], F32))
        ps4 = stack.enter_context(nc.psum_tensor("ps4", [128, 512], F32))
        ps5 = stack.enter_context(nc.psum_tensor("ps5", [128, 512], F32))
        ps6 = stack.enter_context(nc.psum_tensor("ps6", [128, 512], F32))

        wf = {}
        cl0 = []
        for tile, srcd in ((wm0, ins["winmask0"]), (wmb, ins["band"]),
                           (bi_s, ins["bi_c"]), (bl1_s, ins["bl1_c"]),
                           (bl2_s, ins["bl2_c"]), (memb_s, ins["memb_c"]),
                           (bg_s, ins["bg_c"]), (bo_s, ins["bo_c"]),
                           (b2f_s, ins["b2f_c"]), (b1_s, ins["b1_c"]),
                           (fglr, ins["fg_lr"]), (pri_s, ins["prior_mask"]),
                           (one1b, t_ones1b), (one128b, t_ones128b),
                           (id16, ident16), (id128b, t_id128b),
                           (iota8, t_iota8), (ones1f2, t_ones1f2)):
            cl0.append(em.dma("sync", lambda e, t=tile, s=srcd: e.dma_start(
                out=t[:], in_=s.ap()), tag="cl0"))
        r6l = None
        gml = em.dma("gpsimd", lambda e: e.dma_start(
            out=gmrow[:], in_=ins["gradmask"].ap()), tag="gml")
        ep0 = em.n("vector", lambda e: e.memset(eps12[:], 1e-12), tag="ep0")
        dfb = em.n("vector", lambda e: e.tensor_sub(
            dfb_s[:], memb_s[:], bi_s[:]), deps=[cl0[5], cl0[2]], tag="dfb")
        fg1 = em.n("tensor", lambda e: e.matmul(
            ps3[:, 0:2], ones1f2[:], fglr[0:1, :],
            start=True, stop=True), deps=[cl0[10], cl0[-1]], tag="fg1",
            psb=[ps3])
        fg2 = em.n("vector", lambda e: e.tensor_scalar(
            fgc[:, 0:1], ps3[:, 0:1], -1.0, 1.0, ALU.mult, ALU.add),
            deps=[fg1], tag="fg2", psb=[ps3])
        fg3 = em.n("vector", lambda e: e.tensor_copy(
            fgc[:, 1:2], ps3[:, 1:2]), deps=[fg1], tag="fg3", psb=[ps3])
        gB_nd = []
        prev_r6 = None
        for r in range(6):
            r6l = em.dma("gpsimd", lambda e, r=r: e.dma_start(
                out=rows6f[:], in_=ins["rows6"][r:r + 1, :]),
                deps=[prev_r6], tag=f"r6l{r}")
            mm1 = em.n("tensor", lambda e, r=r: e.matmul(
                ps3[:, 0:512], one1b[:], rows6f[0:1, 0:512],
                start=True, stop=True), deps=[r6l, cl0[12], fg2, fg3],
                tag=f"gB{r}a", psb=[ps3])
            c1 = em.n("vector", lambda e, r=r: e.tensor_copy(
                gB[r][:, 0:512], ps3[:, 0:512]), deps=[mm1], tag="gBc",
                psb=[ps3])
            mm2 = em.n("tensor", lambda e, r=r: e.matmul(
                ps3[:, 0:512], one1b[:], rows6f[0:1, 512:1024],
                start=True, stop=True), deps=[c1], tag=f"gB{r}b", psb=[ps3])
            c2 = em.n("vector", lambda e, r=r: e.tensor_copy(
                gB[r][:, 512:1024], ps3[:, 0:512]), deps=[mm2], tag="gBc2",
                psb=[ps3])
            prev_r6 = mm2
            gB_nd.append(c2)
        gm_nd = []
        prev = gB_nd[-1]
        for off, gsz in GRPS:
            mm1 = em.n("tensor", lambda e, off=off, gsz=gsz: e.matmul(
                ps3[:, 0:gsz], one1b[:], gmrow[0:1, off:off + gsz],
                start=True, stop=True), deps=[gml, cl0[12], prev],
                tag="gmBa", psb=[ps3])
            c1 = em.n("vector", lambda e, off=off, gsz=gsz: e.tensor_copy(
                gmB[:, off:off + gsz], ps3[:, 0:gsz]), deps=[mm1],
                tag="gmBc", psb=[ps3])
            gm_nd.append(c1)
            prev = c1

        hxw = em.dma("sync", lambda e: e.dma_start(
            out=d_hx.ap(), in_=ins["x1in"][T - 16:T, :]), tag="hxw")
        agh = em.n("gpsimd", lambda e: e.collective_compute(
            "AllGather", ALU.bypass,
            replica_groups=[[0, 1], [2, 3], [4, 5], [6, 7]],
            ins=[d_hx.ap().opt()], outs=[d_hxo.ap().opt()]),
            w=1, tag="agh", deps=[hxw], coll=True)
        hl = em.dma("sync", lambda e: e.dma_start(
            out=xa[0][0:16, :], in_=d_hxo[0:16, :]), deps=[agh], tag="hl")
        hb1 = em.n("vector", lambda e: e.bn_stats(
            st6[0:16, 0:6], xa[0][0:16, 0:512]), deps=[hl], tag="hb1")
        hb2 = em.n("vector", lambda e: e.bn_stats(
            st6[0:16, 6:12], xa[0][0:16, 512:1024]), deps=[hl], tag="hb2")
        hb3 = em.n("vector", lambda e: e.bn_aggr(
            mv[0:16, :], st6[0:16, :]), deps=[hb1, hb2], tag="hb3")
        epsln = sb("epsln", (128, 1), F32)
        epl = em.n("vector", lambda e: e.memset(epsln[:], EPS_LN), tag="epl")
        hsq = em.n("scalar", lambda e: e.activation(
            rstd[0:16, :], mv[0:16, 1:2], AF.Sqrt, bias=epsln[0:16, :]),
            deps=[hb3, epl], tag="hsq")
        hrc = em.n("vector", lambda e: e.reciprocal(
            rstd[0:16, :], rstd[0:16, :]), deps=[hsq], tag="hrc")
        hxn = em.n("vector", lambda e: e.tensor_scalar(
            xc[0][0:16, :], xa[0][0:16, :], mv[0:16, 0:1], rstd[0:16, :],
            ALU.subtract, ALU.mult), deps=[hrc], tag="hxn")
        hst = em.dma("sync", lambda e: e.dma_start(
            out=d_xe[0:16, :], in_=xc[0][0:16, :]), deps=[hxn], tag="hst")
        xe_st = [hst]
        zxe0 = em.n("vector", lambda e: e.memset(xc[1][:], 0.0), tag="zxe0")
        zxe1 = em.dma("sync", lambda e: e.dma_start(
            out=d_xe[2064:2176, :], in_=xc[1][0:112, :]), deps=[zxe0],
            tag="zxe1")
        xe_st.append(zxe1)
        prevs = {("xc", 1): zxe1}
        for t in range(NT):
            bi = t % 2
            ld = em.dma("sync", lambda e, t=t, bi=bi: e.dma_start(
                out=xa[bi][:], in_=ins["x1in"][128 * t:128 * (t + 1), :]),
                deps=[prevs.get(("xa", bi)), hxn if t < 2 else None],
                tag=f"xld{t}")
            b1 = em.n("vector", lambda e, bi=bi: e.bn_stats(
                st6[:, 0:6], xa[bi][:, 0:512]), deps=[ld, prevs.get("ln")],
                tag="b1")
            b2 = em.n("vector", lambda e, bi=bi: e.bn_stats(
                st6[:, 6:12], xa[bi][:, 512:1024]), deps=[ld], tag="b2")
            b3 = em.n("vector", lambda e: e.bn_aggr(mv[:], st6[:]),
                      deps=[b1, b2], tag="b3")
            sq = em.n("scalar", lambda e: e.activation(
                rstd[:], mv[:, 1:2], AF.Sqrt, bias=epsln[:]),
                deps=[b3, epl], tag="sq")
            rc = em.n("vector", lambda e: e.reciprocal(rstd[:], rstd[:]),
                      deps=[sq], tag="rc")
            xn = em.n("vector", lambda e, bi=bi: e.tensor_scalar(
                xc[bi][:], xa[bi][:], mv[:, 0:1], rstd[:],
                ALU.subtract, ALU.mult), deps=[rc, prevs.get(("xc", bi))],
                tag="xn")
            prevs["ln"] = xn
            prevs[("xa", bi)] = xn
            st_ = em.dma("sync", lambda e, t=t, bi=bi: e.dma_start(
                out=d_xe[16 + 128 * t:16 + 128 * (t + 1), :],
                in_=xc[bi][:]), deps=[xn], tag=f"xest{t}")
            prevs[("xc", bi)] = st_
            xe_st.append(st_)

        stack_b = ExitStack()
        sbB = lambda n, s, d: stack_b.enter_context(
            nc.sbuf_tensor(n, list(s), d))
        xef = [sbB(f"xef{j}", (128, NKV), BF16) for j in range(8)]
        xef_nd = []
        for j in range(8):
            xef_nd.append(em.dma("sync", lambda e, j=j: e.dma_start_transpose(
                xef[j][:], d_xe[:, 128 * j:128 * (j + 1)]),
                deps=xe_st, tag=f"xef{j}"))
        ka_t = [sbB(f"ka{i}", (128, D), BF16) for i in range(2)]
        kb_t = [sbB(f"kb{i}", (16, D), BF16) for i in range(2)]
        praw = sbB("praw", (128, 144), F32)
        pexp = sbB("pexp", (128, 144), BF16)
        pat = sbB("pat", (128, 128), BF16)
        pbt = sbB("pbt", (16, 128), BF16)
        nmx = sbB("nmx", (128, 1), F32)
        dnm = sbB("dnm", (128, 1), F32)
        x1s_st = []
        for t in range(NT):
            bi = t % 2
            mms = []
            for k in range(8):
                mms.append(em.n("tensor", lambda e, t=t, k=k: e.matmul(
                    ps1[:, 0:144],
                    xef[k][:, 16 + 128 * t:16 + 128 * (t + 1)],
                    xef[k][:, 128 * t:128 * t + 144],
                    start=(k == 0), stop=(k == 7)),
                    deps=[xef_nd[k], wf.get("ps1w")], tag=f"ws{t}_{k}",
                    psb=[ps1]))
            msk = wm0 if t == 0 else wmb
            sc = em.n("vector", lambda e, msk=msk: e.scalar_tensor_tensor(
                praw[:], ps1[:, 0:144], 0.03125, msk[:], ALU.mult, ALU.add),
                deps=[mms[-1], cl0[0], cl0[1], wf.get("praw")], tag="wmask",
                psb=[ps1])
            wf["ps1w"] = sc
            mx = em.n("vector", lambda e: e.reduce_max(
                nmx[:], praw[:], axis=AXL.X, negate=True),
                deps=[sc], tag="wmax")
            ex = em.n("scalar", lambda e: e.activation(
                pexp[:], praw[:], AF.Exp, bias=nmx[:], accum_out=dnm[:]),
                deps=[mx, wf.get("pexp")], tag="wexp")
            wf["praw"] = ex
            rcp = em.n("vector", lambda e: e.reciprocal(dnm[:], dnm[:]),
                       deps=[ex], tag="wrcp")
            tp1 = em.n("tensor", lambda e: e.transpose(
                ps2[:, 0:64].bitcast(BF16), pexp[:, 0:128], id128b[:]),
                deps=[ex, wf.get("ps2w")], tag="wtp1", psb=[ps2])
            ev1 = em.n("vector", lambda e: e.tensor_copy(
                pat[:], ps2[:, 0:64].bitcast(BF16)), deps=[tp1],
                tag="wev1", psb=[ps2])
            tp2 = em.n("tensor", lambda e: e.transpose(
                ps2[0:16, 64:128].bitcast(BF16), pexp[:, 128:144],
                id128b[:]), deps=[ev1], tag="wtp2", psb=[ps2])
            ev2 = em.n("vector", lambda e: e.tensor_copy(
                pbt[:], ps2[0:16, 64:128].bitcast(BF16)), deps=[tp2],
                tag="wev2", psb=[ps2])
            wf["pexp"] = tp2
            wf["ps2w"] = ev2
            lka = em.dma("sync", lambda e, t=t, bi=bi: e.dma_start(
                out=ka_t[bi][:], in_=d_xe[128 * t:128 * (t + 1), :]),
                deps=[xe_st[-1], wf.get(("ka", bi))], tag="lka")
            lkb = em.dma("sync", lambda e, t=t, bi=bi: e.dma_start(
                out=kb_t[bi][0:16, :],
                in_=d_xe[128 * t + 128:128 * t + 144, :]),
                deps=[xe_st[-1], wf.get(("ka", bi))], tag="lkb")
            avn = []
            for h2 in range(2):
                m1 = em.n("tensor", lambda e, h2=h2, bi=bi: e.matmul(
                    ps4[:, 0:512], pat[:],
                    ka_t[bi][:, 512 * h2:512 * (h2 + 1)],
                    start=True, stop=False),
                    deps=[ev1, lka, wf.get("ps4w")], tag="av1", psb=[ps4])
                m2 = em.n("tensor", lambda e, h2=h2, bi=bi: e.matmul(
                    ps4[:, 0:512], pbt[:],
                    kb_t[bi][0:16, 512 * h2:512 * (h2 + 1)],
                    start=False, stop=True),
                    deps=[ev2, lkb, m1], tag="av2", psb=[ps4])
                a1_ = em.n("vector", lambda e, h2=h2, bi=bi:
                    e.tensor_scalar_mul(
                        xb_[bi][:, 512 * h2:512 * (h2 + 1)], ps4[:, 0:512],
                        dnm[:]), deps=[m2, rcp, wf.get(("xb", bi))],
                    tag="avs", psb=[ps4])
                wf["ps4w"] = a1_
                avn.append(a1_)
            wf[("ka", bi)] = avn[-1]
            lx = em.dma("sync", lambda e, t=t, bi=bi: e.dma_start(
                out=xa[bi][:], in_=ins["x1in"][128 * t:128 * (t + 1), :]),
                deps=[wf.get(("xaB", bi))], tag="lx1")
            ad = em.n("vector", lambda e, bi=bi: e.tensor_add(
                xb_[bi][:], xb_[bi][:], xa[bi][:]), deps=[lx] + avn,
                tag="wadd")
            wf[("xaB", bi)] = ad
            b1 = em.n("vector", lambda e, bi=bi: e.bn_stats(
                st6[:, 0:6], xb_[bi][:, 0:512]), deps=[ad, wf.get("lnb")],
                tag="wb1")
            b2 = em.n("vector", lambda e, bi=bi: e.bn_stats(
                st6[:, 6:12], xb_[bi][:, 512:1024]), deps=[ad], tag="wb2")
            b3 = em.n("vector", lambda e: e.bn_aggr(mv[:], st6[:]),
                      deps=[b1, b2], tag="wb3")
            sq = em.n("scalar", lambda e: e.activation(
                rstd[:], mv[:, 1:2], AF.Sqrt, bias=epsln[:]),
                deps=[b3, epl], tag="wsq")
            rc = em.n("vector", lambda e: e.reciprocal(rstd[:], rstd[:]),
                      deps=[sq], tag="wrc")
            n1 = em.n("vector", lambda e, bi=bi: e.tensor_scalar(
                xb_[bi][:], xb_[bi][:], mv[:, 0:1], rstd[:],
                ALU.subtract, ALU.mult), deps=[rc], tag="wn1")
            n2 = em.n("vector", lambda e, bi=bi: e.tensor_mul(
                xb_[bi][:], xb_[bi][:], gB[0][:]), deps=[n1, gB_nd[0]],
                tag="wn2")
            n3 = em.n("vector", lambda e, bi=bi: e.tensor_add(
                xb_[bi][:], xb_[bi][:], gB[1][:]), deps=[n2, gB_nd[1]],
                tag="wn3")
            wf["lnb"] = n3
            c_ = em.n("vector", lambda e, bi=bi: e.tensor_copy(
                xc[bi][:], xb_[bi][:]), deps=[n3, wf.get(("xcB", bi))],
                tag="wcb")
            s1_ = em.dma("sync", lambda e, t=t, bi=bi: e.dma_start(
                out=d_x1s[128 * t:128 * (t + 1), :], in_=xb_[bi][:]),
                deps=[n3], tag="x1sst")
            s2_ = em.dma("sync", lambda e, t=t, bi=bi: e.dma_start(
                out=d_x1sb[128 * t:128 * (t + 1), :], in_=xc[bi][:]),
                deps=[c_], tag="x1sbst")
            wf[("xb", bi)] = s1_
            wf[("xcB", bi)] = s2_
            x1s_st.append(s1_)
            x1s_st.append(s2_)
        if "x1s" in dbg_t:
            em.dma("gpsimd", lambda e: e.dma_start(
                out=dbg_t["x1s"].ap(), in_=d_x1s.ap()), deps=x1s_st,
                tag="dbgx1s")

        pfm = [sbB(f"pfm{j}", (128, 128), BF16) for j in range(8)]
        nrm = sbB("nrm", (1, NKV), F32)
        recb = sbB("recb", (1, NKV), BF16)
        sqv = sbB("sqv", (128, 512), BF16)
        knt = sbB("knt", (128, 512), BF16)

        pf_nd = []
        for j in range(8):
            pf_nd.append(em.dma("gpsimd", lambda e, j=j: e.dma_start(
                out=pfm[j][:], in_=ins["PT16"][128 * j:128 * (j + 1), :]),
                tag="pfl"))

        def kv_src(k, off, gsz):
            if off >= T:
                return pfm[k][:, 0:128]
            return xef[k][:, 16 + off:16 + off + gsz]

        nz = em.n("vector", lambda e: e.memset(nrm[:], 0.0), tag="nz")
        nrm_nd = [nz]
        prev_sq = None
        for k in range(8):
            for off, gsz in GRPS:
                sq_ = em.n("scalar", lambda e, k=k, off=off, gsz=gsz:
                    e.activation(sqv[:, 0:gsz], kv_src(k, off, gsz),
                                 AF.Square),
                    deps=[xef_nd[k], pf_nd[k], prev_sq], tag="ksq")
                nm_ = em.n("tensor", lambda e, gsz=gsz: e.matmul(
                    ps3[0:1, 0:gsz], one128b[:], sqv[:, 0:gsz],
                    start=True, stop=True),
                    deps=[sq_, cl0[13], wf.get("ps3n")], tag="knm",
                    psb=[ps3])
                ad_ = em.n("vector", lambda e, off=off, gsz=gsz: e.tensor_add(
                    nrm[0:1, off:off + gsz], nrm[0:1, off:off + gsz],
                    ps3[0:1, 0:gsz]), deps=[nm_, nrm_nd[-1]], tag="kna",
                    psb=[ps3])
                wf["ps3n"] = ad_
                nrm_nd.append(ad_)
                prev_sq = nm_
        kr1 = em.n("scalar", lambda e: e.activation(
            nrm[:], nrm[:], AF.Sqrt), deps=nrm_nd[-1:], tag="kr1")
        kr2 = em.n("vector", lambda e: e.tensor_scalar_max(
            nrm[:], nrm[:], 1e-12), deps=[kr1], tag="kr2")
        kr3 = em.n("vector", lambda e: e.reciprocal(nrm[:], nrm[:]),
                   deps=[kr2], tag="kr3")
        kr4 = em.n("vector", lambda e: e.tensor_copy(recb[:], nrm[:]),
                   deps=[kr3], tag="kr4")
        d_kn = nc.dram_tensor("d_kn", [D, NKV], BF16)
        kn_nd = []
        for k in range(8):
            for off, gsz in GRPS:
                rb_ = em.n("tensor", lambda e, off=off, gsz=gsz: e.matmul(
                    ps3[:, 0:gsz], one1b[:], recb[0:1, off:off + gsz],
                    start=True, stop=True),
                    deps=[kr4, cl0[12], wf.get("ps3n")], tag="krb",
                    psb=[ps3])
                kn_ = em.n("vector", lambda e, k=k, off=off, gsz=gsz:
                    e.tensor_mul(knt[:, 0:gsz], kv_src(k, off, gsz),
                                 ps3[:, 0:gsz]),
                    deps=[rb_, wf.get("knt")], tag="knmul", psb=[ps3])
                wf["ps3n"] = kn_
                pk_ = em.dma("sync", lambda e, k=k, off=off, gsz=gsz:
                    e.dma_start(out=d_kn[128 * k:128 * (k + 1),
                                         off:off + gsz],
                                in_=knt[:, 0:gsz]), deps=[kn_], tag="knpk")
                wf["knt"] = pk_
                kn_nd.append(pk_)
        stack_b.close()

        csrc = [sb(f"csrc{k}", (128, 512), BF16) for k in range(8)]
        csc2 = [sb(f"csc2{k}", (128, 512), BF16) for k in range(8)]

        def wloadL(idx, srcd, dep=(), fd=D):
            nds = []
            for k in range(8):
                nds.append(em.dma("gpsimd", lambda e, k=k, idx=idx,
                                  srcd=srcd, fd=fd: e.dma_start(
                    out=wbuf[idx][k][:, 0:fd],
                    in_=srcd[128 * k:128 * (k + 1), :]),
                    deps=[wf.get(("wl", idx, k))] + list(dep),
                    tag=f"wlL{idx}_{k}"))
                wf[("wl", idx, k)] = nds[-1]
            return nds

        def fm_chain2(wnd, widx, src_d, dst_d, func, bias_s, tagp,
                      deps_extra=(), tok_only=False, wtiles=None,
                      src2_d=None, wnd2=None, widx2=None, sig=AF.Identity,
                      src_park_of=None):
            grps = GRPS[:4] if tok_only else GRPS
            last = []
            for gi, (off, gsz) in enumerate(grps):
                l_nd = []
                for k in range(8):
                    pdep = ([src_park_of(gi, k)] if src_park_of is not None
                            else [])
                    if isinstance(src_d, tuple):
                        srcT = src_d[1]
                        l_nd.append(em.dma("sync", lambda e, k=k, off=off,
                                           gsz=gsz, srcT=srcT:
                            e.dma_start_transpose(
                                csrc[k][:, 0:gsz],
                                srcT[off:off + gsz,
                                     128 * k:128 * (k + 1)]),
                            deps=[wf.get(("cs", k))] + list(deps_extra)
                                 + pdep,
                            tag=f"{tagp}lT"))
                    else:
                        l_nd.append(em.dma("sync", lambda e, k=k, off=off,
                                           gsz=gsz, src_d=src_d:
                            e.dma_start(
                                out=csrc[k][:, 0:gsz],
                                in_=src_d[128 * k:128 * (k + 1),
                                          off:off + gsz]),
                            deps=[wf.get(("cs", k))] + list(deps_extra)
                                 + pdep,
                            tag=f"{tagp}l"))
                l2_nd = []
                if src2_d is not None:
                    for k in range(8):
                        l2_nd.append(em.dma("sync", lambda e, k=k, off=off,
                                            gsz=gsz: e.dma_start(
                            out=csc2[k][:, 0:gsz],
                            in_=src2_d[128 * k:128 * (k + 1),
                                       off:off + gsz]),
                            deps=[wf.get(("cs2", k))] + list(deps_extra),
                            tag=f"{tagp}l2"))
                for j in range(8):
                    ps = ps1 if j % 2 == 0 else ps2
                    mms = []
                    for k in range(8):
                        wtile = (wtiles[k] if wtiles is not None
                                 else wbuf[widx][k])
                        dep = [wnd[k], l_nd[k]]
                        if k == 0:
                            dep.append(wf.get((tagp, id(ps))))
                        mms.append(em.n("tensor",
                            lambda e, j=j, k=k, gsz=gsz, ps=ps,
                            wtile=wtile: e.matmul(
                                ps[:, 0:gsz],
                                wtile[:, 128 * j:128 * (j + 1)],
                                csrc[k][:, 0:gsz],
                                start=(k == 0),
                                stop=(k == 7 and src2_d is None)),
                            deps=dep, tag=f"{tagp}mm", psb=[ps]))
                    if src2_d is not None:
                        for k in range(8):
                            mms.append(em.n("tensor",
                                lambda e, j=j, k=k, gsz=gsz, ps=ps,
                                widx2=widx2: e.matmul(
                                    ps[:, 0:gsz],
                                    wbuf[widx2][k][:, 128 * j:128 * (j + 1)],
                                    csc2[k][:, 0:gsz],
                                    start=False, stop=(k == 7)),
                                deps=[wnd2[k], l2_nd[k]], tag=f"{tagp}m2",
                                psb=[ps]))
                    pki = j % 4
                    ev = em.n("scalar", lambda e, j=j, gsz=gsz, ps=ps,
                              func=func, pki=pki: e.activation(
                        pkt[pki][:, 0:gsz], ps[:, 0:gsz], func,
                        bias=bias_s[:, j:j + 1]),
                        deps=[mms[-1], wf.get(("pk", pki))],
                        tag=f"{tagp}ev", psb=[ps])
                    wf[(tagp, id(ps))] = ev
                    pk_ = em.dma("sync", lambda e, j=j, off=off, gsz=gsz,
                                 dst_d=dst_d, pki=pki: e.dma_start(
                        out=dst_d[128 * j:128 * (j + 1), off:off + gsz],
                        in_=pkt[pki][:, 0:gsz]), deps=[ev], tag=f"{tagp}pk")
                    wf[("pk", pki)] = pk_
                    last.append(pk_)
                for k in range(8):
                    wf[("cs", k)] = mms[7] if src2_d is None else mms[7]
                    if src2_d is not None:
                        wf[("cs2", k)] = mms[-1]
            return last

        d_h1 = nc.dram_tensor("d_h1", [D, NKV], BF16)
        d_pred = nc.dram_tensor("d_pred", [D, NKV], BF16)
        d_mem = nc.dram_tensor("d_mem", [D, NKV], BF16)
        d_mo = nc.dram_tensor("d_mo", [D, T], BF16)
        d_gate = nc.dram_tensor("d_gate", [D, T], BF16)
        wi_nd = wloadL(0, ins["WiT"])
        kp_nd = fm_chain2(wi_nd, 0, d_kn, d_kp, AF.Identity, bi_s, "kp",
                          src_park_of=lambda gi, k: kn_nd[5 * k + gi])
        wl1_nd = wloadL(1, ins["Wl1T"])
        h1_nd = fm_chain2(wl1_nd, 1, d_kp, d_h1, AF.Silu, bl1_s, "h1",
                          src_park_of=lambda gi, k: kp_nd[8 * gi + k])
        wl2_nd = wloadL(0, ins["Wl2T"], dep=kp_nd[-1:])
        hdd_nd = fm_chain2(wl2_nd, 0, d_h1, d_hdd, AF.Silu, bl2_s, "hd",
                           src_park_of=lambda gi, k: h1_nd[8 * gi + k])
        wm_nd = wloadL(1, ins["memWT"], dep=h1_nd[-1:])
        pred_nd = fm_chain2(wm_nd, 1, d_hdd, d_pred, AF.Identity, memb_s,
                            "pr", deps_extra=[cl0[5]],
                            src_park_of=lambda gi, k: hdd_nd[8 * gi + k])
        diff_nd = []
        prev = None
        for j in range(8):
            for off, gsz in GRPS:
                gi_ = GRPS.index((off, gsz))
                l1_ = em.dma("sync", lambda e, j=j, off=off, gsz=gsz:
                    e.dma_start(out=csc2[0][:, 0:gsz],
                                in_=d_pred[128 * j:128 * (j + 1),
                                           off:off + gsz]),
                    deps=[pred_nd[8 * gi_ + j], wf.get("df0")], tag="dfl1")
                l2_ = em.dma("sync", lambda e, j=j, off=off, gsz=gsz:
                    e.dma_start(out=csc2[1][:, 0:gsz],
                                in_=d_kp[128 * j:128 * (j + 1),
                                         off:off + gsz]),
                    deps=[kp_nd[8 * gi_ + j], wf.get("df1")], tag="dfl2")
                d1 = em.n("vector", lambda e, gsz=gsz: e.tensor_sub(
                    csc2[0][:, 0:gsz], csc2[0][:, 0:gsz],
                    csc2[1][:, 0:gsz]), deps=[l1_, l2_, prev], tag="dfs")
                wf["df1"] = d1
                d2 = em.n("vector", lambda e, off=off, gsz=gsz:
                    e.tensor_mul(csc2[0][:, 0:gsz], csc2[0][:, 0:gsz],
                                 gmB[:, off:off + gsz]),
                    deps=[d1, gm_nd[-1]], tag="dfm")
                pk_ = em.dma("sync", lambda e, j=j, off=off, gsz=gsz:
                    e.dma_start(out=d_diff[128 * j:128 * (j + 1),
                                           off:off + gsz],
                                in_=csc2[0][:, 0:gsz]),
                    deps=[d2], tag="dfpk")
                wf["df0"] = pk_
                diff_nd.append(pk_)
                prev = d2
        if "hdd" in dbg_t:
            em.dma("gpsimd", lambda e: e.dma_start(
                out=dbg_t["hdd"].ap(), in_=d_hdd.ap()),
                deps=hdd_nd[-1:], tag="dbghdd")

        stack_g = ExitStack()
        sbG = lambda n, s, d: stack_g.enter_context(
            nc.sbuf_tensor(n, list(s), d))
        hddT = [sbG(f"hddT{t}", (128, D), BF16) for t in range(17)]
        difT = [sbG(f"difT{t}", (128, D), BF16) for t in range(17)]
        wnew = [sbG(f"wnew{k}", (128, D), BF16) for k in range(8)]
        grl = sbG("grl", (128, 512), F32)
        gev = [sbG(f"gev{i}", (128, 512), F32) for i in range(2)]
        ht_nd, dt_nd = [], []
        for t in range(17):
            rows = 128
            ht_nd.append(em.dma("sync", lambda e, t=t, rows=rows:
                e.dma_start_transpose(hddT[t][0:rows, :],
                                      d_hdd[:, 128 * t:128 * t + rows]),
                deps=hdd_nd[-1:], tag=f"htl{t}"))
            dt_nd.append(em.dma("sync", lambda e, t=t, rows=rows:
                e.dma_start_transpose(difT[t][0:rows, :],
                                      d_diff[:, 128 * t:128 * t + rows]),
                deps=diff_nd[-1:], tag=f"dtl{t}"))
        gw_st = []
        for jj in range(8):
            for hf in range(2):
                ps = ps5 if (jj * 2 + hf) % 2 == 0 else ps6
                bi4 = (jj * 2 + hf) % 2
                mms = []
                for t in range(17):
                    rows = 128
                    dep = [ht_nd[t], dt_nd[t]]
                    if t == 0:
                        dep.append(wf.get(("gps", id(ps))))
                    mms.append(em.n("tensor", lambda e, jj=jj, hf=hf, t=t,
                                    rows=rows, ps=ps: e.matmul(
                        ps[:, 0:512],
                        hddT[t][0:rows, 128 * jj:128 * (jj + 1)],
                        difT[t][0:rows, 512 * hf:512 * (hf + 1)],
                        start=(t == 0), stop=(t == 16)),
                        deps=dep, tag="grmm", psb=[ps]))
                ev = em.n("vector", lambda e, ps=ps, bi4=bi4:
                    e.tensor_scalar_mul(gev[bi4][:], ps[:, 0:512],
                                        2.0 / (16424.0 * 1024.0)),
                    deps=[mms[-1], wf.get(("gevb", bi4))], tag="grev",
                    psb=[ps])
                wf[("gps", id(ps))] = ev
                st_ = em.dma("sync", lambda e, jj=jj, hf=hf, bi4=bi4:
                    e.dma_start(out=d_grad[128 * jj:128 * (jj + 1),
                                           512 * hf:512 * (hf + 1)],
                                in_=gev[bi4][:]), deps=[ev], tag="gwst")
                wf[("gevb", bi4)] = st_
                gw_st.append(st_)
        agg = em.n("gpsimd", lambda e: e.collective_compute(
            "AllReduce", ALU.add, replica_groups=[list(range(8))],
            ins=[d_grad.ap().opt()], outs=[d_grado.ap().opt()]),
            w=1, tag="agg", deps=gw_st, coll=True)
        if "grad" in dbg_t:
            em.dma("gpsimd", lambda e: e.dma_start(
                out=dbg_t["grad"].ap(), in_=d_grado.ap()), deps=[agg],
                tag="dbggrad")
        wmw_nd = wloadL(0, ins["memWT"], dep=pred_nd[-1:])
        wn_nd = []
        for k in range(8):
            for hf in range(2):
                lg = em.dma("sync", lambda e, k=k, hf=hf: e.dma_start(
                    out=grl[:, 0:512],
                    in_=d_grado[128 * k:128 * (k + 1),
                                512 * hf:512 * (hf + 1)]),
                    deps=[agg, wf.get("grl")], tag="grll")
                t1 = em.n("vector", lambda e: e.tensor_scalar_mul(
                    grl[:, 0:512], grl[:, 0:512], fgc[:, 1:2]),
                    deps=[lg, fg3], tag="wnt1")
                t2 = em.n("vector", lambda e, k=k, hf=hf:
                    e.scalar_tensor_tensor(
                        wnew[k][:, 512 * hf:512 * (hf + 1)],
                        wbuf[0][k][:, 512 * hf:512 * (hf + 1)],
                        fgc[:, 0:1], grl[:, 0:512], ALU.mult, ALU.add),
                    deps=[t1, wmw_nd[k], fg2], tag="wnt2")
                wf["grl"] = t2
                wn_nd.append(t2)
        wnall = [wn_nd[2 * k + 1] for k in range(8)]
        mem_nd = fm_chain2(wnall, None, d_hdd, d_mem, AF.Identity, memb_s,
                           "mo0", wtiles=wnew,
                           src_park_of=lambda gi, k: hdd_nd[8 * gi + k])
        wo1_nd = wloadL(1, ins["WoT1"], dep=pred_nd[-1:] + [wn_nd[-1]])
        mo_nd = fm_chain2(wo1_nd, 1, d_mem, d_mo, AF.Identity, bo_s, "mo",
                          tok_only=True,
                          src_park_of=lambda gi, k: mem_nd[8 * gi + k])
        stack_g.close()
        if "mo" in dbg_t:
            em.dma("gpsimd", lambda e: e.dma_start(
                out=dbg_t["mo"].ap(), in_=d_mo.ap()), deps=mo_nd[-1:],
                tag="dbgmo")

        wga_nd = wloadL(0, ins["WgAT"], dep=mem_nd[-1:])
        wgb_nd = wloadL(1, ins["WgBT"], dep=mo_nd[-1:])
        gate_nd = fm_chain2(wga_nd, 0, ("T", d_x1sb), d_gate, AF.Sigmoid,
                            bg_s, "gt",
                            deps_extra=x1s_st[-1:] + mo_nd[-1:],
                            tok_only=True, src2_d=d_mo, wnd2=wgb_nd,
                            widx2=1)
        s_st = []
        prev = gate_nd[-1]
        for j in range(8):
            for g in range(4):
                off, gsz = 512 * g, 512
                lg_ = em.dma("sync", lambda e, j=j, off=off: e.dma_start(
                    out=csrc[0][:, 0:512],
                    in_=d_gate[128 * j:128 * (j + 1), off:off + 512]),
                    deps=[gate_nd[-1], wf.get("sg0")], tag="sgl")
                lxs = em.dma("sync", lambda e, j=j, off=off:
                    e.dma_start_transpose(
                        csrc[1][:, 0:512],
                        d_x1sb[off:off + 512, 128 * j:128 * (j + 1)]),
                    deps=[x1s_st[-1], wf.get("sg1")], tag="sxl")
                lmo = em.dma("sync", lambda e, j=j, off=off: e.dma_start(
                    out=csrc[2][:, 0:512],
                    in_=d_mo[128 * j:128 * (j + 1), off:off + 512]),
                    deps=[mo_nd[-1], wf.get("sg2")], tag="sml")
                v1 = em.n("vector", lambda e: e.tensor_mul(
                    csrc[3][:, 0:512], csrc[0][:, 0:512],
                    csrc[1][:, 0:512]), deps=[lg_, lxs, wf.get("sg3")],
                    tag="cu1")
                v2 = em.n("vector", lambda e: e.tensor_mul(
                    csrc[0][:, 0:512], csrc[0][:, 0:512],
                    csrc[2][:, 0:512]), deps=[v1, lmo], tag="cu2")
                wf["sg0"] = v2
                v3 = em.n("vector", lambda e: e.tensor_add(
                    csrc[1][:, 0:512], csrc[1][:, 0:512],
                    csrc[3][:, 0:512]), deps=[v1, v2], tag="cu3")
                wf["sg3"] = v3
                v4 = em.n("vector", lambda e: e.tensor_add(
                    csrc[1][:, 0:512], csrc[1][:, 0:512],
                    csrc[2][:, 0:512]), deps=[v3], tag="cu4")
                wf["sg2"] = v4
                v5 = em.n("vector", lambda e: e.tensor_sub(
                    csrc[1][:, 0:512], csrc[1][:, 0:512],
                    csrc[0][:, 0:512]), deps=[v4], tag="cu5")
                st_ = em.dma("sync", lambda e, j=j, off=off: e.dma_start(
                    out=d_s[128 * j:128 * (j + 1), off:off + 512],
                    in_=csrc[1][:, 0:512]), deps=[v5], tag="spark")
                wf["sg1"] = st_
                s_st.append(st_)
        x2_st = []
        for t in range(NT):
            bi = t % 2
            ls = em.dma("sync", lambda e, t=t, bi=bi: e.dma_start_transpose(
                xc[bi][:], d_s[:, 128 * t:128 * (t + 1)]),
                deps=[s_st[-1], wf.get(("x2c", bi))], tag="lstm")
            cp = em.n("vector", lambda e, bi=bi: e.tensor_copy(
                xb_[bi][:], xc[bi][:]), deps=[ls, wf.get(("x2b", bi))],
                tag="x2cp")
            b1 = em.n("vector", lambda e, bi=bi: e.bn_stats(
                st6[:, 0:6], xb_[bi][:, 0:512]), deps=[cp, wf.get("lnb")],
                tag="x2b1")
            b2 = em.n("vector", lambda e, bi=bi: e.bn_stats(
                st6[:, 6:12], xb_[bi][:, 512:1024]), deps=[cp], tag="x2b2")
            b3 = em.n("vector", lambda e: e.bn_aggr(mv[:], st6[:]),
                      deps=[b1, b2], tag="x2b3")
            sq = em.n("scalar", lambda e: e.activation(
                rstd[:], mv[:, 1:2], AF.Sqrt, bias=epsln[:]),
                deps=[b3], tag="x2sq")
            rc = em.n("vector", lambda e: e.reciprocal(rstd[:], rstd[:]),
                      deps=[sq], tag="x2rc")
            n1 = em.n("vector", lambda e, bi=bi: e.tensor_scalar(
                xb_[bi][:], xb_[bi][:], mv[:, 0:1], rstd[:],
                ALU.subtract, ALU.mult), deps=[rc], tag="x2n1")
            n2 = em.n("vector", lambda e, bi=bi: e.tensor_mul(
                xb_[bi][:], xb_[bi][:], gB[2][:]), deps=[n1, gB_nd[2]],
                tag="x2n2")
            n3 = em.n("vector", lambda e, bi=bi: e.tensor_add(
                xb_[bi][:], xb_[bi][:], gB[3][:]), deps=[n2, gB_nd[3]],
                tag="x2n3")
            wf["lnb"] = n3
            c_ = em.n("vector", lambda e, bi=bi: e.tensor_copy(
                xc[bi][:], xb_[bi][:]), deps=[n3], tag="x2cb")
            wf[("x2c", bi)] = c_
            s1_ = em.dma("sync", lambda e, t=t, bi=bi: e.dma_start(
                out=d_x2[128 * t:128 * (t + 1), :], in_=xb_[bi][:]),
                deps=[n3], tag="x2st")
            s2_ = em.dma("sync", lambda e, t=t, bi=bi: e.dma_start(
                out=d_x2b[128 * t:128 * (t + 1), :], in_=xc[bi][:]),
                deps=[c_], tag="x2bst")
            wf[("x2b", bi)] = s1_
            x2_st += [s1_, s2_]
        if "x2" in dbg_t:
            em.dma("gpsimd", lambda e: e.dma_start(
                out=dbg_t["x2"].ap(), in_=d_x2.ap()), deps=x2_st[-2:],
                tag="dbgx2")

        stack_f4 = ExitStack()
        sbF = lambda n, s, d: stack_f4.enter_context(
            nc.sbuf_tensor(n, list(s), d))
        wE1 = [sbF(f"wE1_{k}", (128, 4096), BF16) for k in range(8)]
        wE1_nd = []
        for k in range(8):
            wE1_nd.append(em.dma("gpsimd", lambda e, k=k: e.dma_start(
                out=wE1[k][:], in_=ins["W1T"][128 * k:128 * (k + 1), :]),
                deps=[wf.get(("wl", 0, k)), wf.get(("wl", 1, k))],
                tag="wE1l"))
        h_st = []
        for g in range(4):
            x2l = []
            for k in range(8):
                x2l.append(em.dma("sync", lambda e, g=g, k=k:
                    e.dma_start_transpose(
                        csrc[k][:, 0:512],
                        d_x2b[512 * g:512 * (g + 1),
                              128 * k:128 * (k + 1)]),
                    deps=[x2_st[-1], wf.get(("cs", k))], tag="ffxl"))
            for oc in range(32):
                ps = ps1 if oc % 2 == 0 else ps2
                mms = []
                for k in range(8):
                    dep = [wE1_nd[k], x2l[k]]
                    if k == 0:
                        dep.append(wf.get(("ff", id(ps))))
                    mms.append(em.n("tensor", lambda e, oc=oc, k=k,
                                    ps=ps: e.matmul(
                        ps[:, 0:512],
                        wE1[k][:, 128 * oc:128 * (oc + 1)],
                        csrc[k][:, 0:512],
                        start=(k == 0), stop=(k == 7)), deps=dep,
                        tag="h1mm", psb=[ps]))
                pki = oc % 4
                ev = em.n("scalar", lambda e, oc=oc, ps=ps, pki=pki:
                    e.activation(pkt[pki][:], ps[:, 0:512], AF.Silu,
                                 bias=b1_s[:, oc:oc + 1]),
                    deps=[mms[-1], cl0[9], wf.get(("pk", pki))],
                    tag="h1ev", psb=[ps])
                wf[("ff", id(ps))] = ev
                st_ = em.dma("sync", lambda e, g=g, oc=oc, pki=pki:
                    e.dma_start(out=d_h[128 * oc:128 * (oc + 1),
                                        512 * g:512 * (g + 1)],
                                in_=pkt[pki][:]), deps=[ev], tag="hpark")
                wf[("pk", pki)] = st_
                h_st.append(st_)
            for k in range(8):
                wf[("cs", k)] = mms[7]
        stack_f4.close()
        stack_f5 = ExitStack()
        sbF5 = lambda n, s, d: stack_f5.enter_context(
            nc.sbuf_tensor(n, list(s), d))
        wE2 = [sbF5(f"wE2_{oc}", (128, D), BF16) for oc in range(32)]
        hsl = [sbF5(f"hsl{oc}", (128, 512), BF16) for oc in range(32)]
        wE2_nd = []
        for oc in range(32):
            wE2_nd.append(em.dma("gpsimd", lambda e, oc=oc: e.dma_start(
                out=wE2[oc][:], in_=ins["W2T"][128 * oc:128 * (oc + 1), :]),
                deps=[wE1_nd[oc % 8], h_st[-1]], tag="wE2l"))
        ff_st = []
        for g in range(4):
            hsl_nd = []
            for oc in range(32):
                hsl_nd.append(em.dma("sync", lambda e, g=g, oc=oc:
                    e.dma_start(out=hsl[oc][:],
                                in_=d_h[128 * oc:128 * (oc + 1),
                                        512 * g:512 * (g + 1)]),
                    deps=[h_st[-1], wf.get(("hsl", oc))], tag="hsll"))
            for j in range(8):
                ps = ps1 if j % 2 == 0 else ps2
                mms = []
                for oc in range(32):
                    dep = [wE2_nd[oc], hsl_nd[oc]]
                    if oc == 0:
                        dep.append(wf.get(("ff2", id(ps))))
                    mms.append(em.n("tensor", lambda e, g=g, j=j, oc=oc,
                                    ps=ps: e.matmul(
                        ps[:, 0:512],
                        wE2[oc][:, 128 * j:128 * (j + 1)], hsl[oc][:],
                        start=(oc == 0), stop=(oc == 31)), deps=dep,
                        tag="ffmm", psb=[ps]))
                pki = j % 4
                ev = em.n("scalar", lambda e, j=j, ps=ps, pki=pki:
                    e.activation(pkt[pki][:], ps[:, 0:512], AF.Identity,
                                 bias=b2f_s[:, j:j + 1]),
                    deps=[mms[-1], cl0[8], wf.get(("pktf", pki))],
                    tag="ffev", psb=[ps])
                wf[("ff2", id(ps))] = ev
                st_ = em.dma("sync", lambda e, g=g, j=j, pki=pki:
                    e.dma_start(out=d_ffb[128 * j:128 * (j + 1),
                                          512 * g:512 * (g + 1)],
                                in_=pkt[pki][:]), deps=[ev], tag="ffpark")
                wf[("pktf", pki)] = st_
                ff_st.append(st_)
            for oc in range(32):
                wf[("hsl", oc)] = mms[-1]
        stack_f5.close()
        a1_st = []
        for t in range(NT):
            bi = t % 2
            lf = em.dma("sync", lambda e, t=t, bi=bi: e.dma_start_transpose(
                xc[bi][:], d_ffb[:, 128 * t:128 * (t + 1)]),
                deps=[ff_st[-1], wf.get(("a1c", bi))], tag="lff")
            lx2 = em.dma("sync", lambda e, t=t, bi=bi: e.dma_start(
                out=xa[bi][:], in_=d_x2[128 * t:128 * (t + 1), :]),
                deps=[x2_st[-2], wf.get(("a1a", bi))], tag="lx2")
            ad = em.n("vector", lambda e, bi=bi: e.tensor_add(
                xb_[bi][:], xa[bi][:], xc[bi][:]),
                deps=[lf, lx2, wf.get(("a1b", bi))], tag="a1add")
            wf[("a1c", bi)] = ad
            wf[("a1a", bi)] = ad
            b1 = em.n("vector", lambda e, bi=bi: e.bn_stats(
                st6[:, 0:6], xb_[bi][:, 0:512]), deps=[ad, wf.get("lnb")],
                tag="a1b1")
            b2 = em.n("vector", lambda e, bi=bi: e.bn_stats(
                st6[:, 6:12], xb_[bi][:, 512:1024]), deps=[ad], tag="a1b2")
            b3 = em.n("vector", lambda e: e.bn_aggr(mv[:], st6[:]),
                      deps=[b1, b2], tag="a1b3")
            sq = em.n("scalar", lambda e: e.activation(
                rstd[:], mv[:, 1:2], AF.Sqrt, bias=epsln[:]),
                deps=[b3], tag="a1sq")
            rc = em.n("vector", lambda e: e.reciprocal(rstd[:], rstd[:]),
                      deps=[sq], tag="a1rc")
            n1 = em.n("vector", lambda e, bi=bi: e.tensor_scalar(
                xb_[bi][:], xb_[bi][:], mv[:, 0:1], rstd[:],
                ALU.subtract, ALU.mult), deps=[rc], tag="a1n1")
            n2 = em.n("vector", lambda e, bi=bi: e.tensor_mul(
                xb_[bi][:], xb_[bi][:], gB[4][:]), deps=[n1, gB_nd[4]],
                tag="a1n2")
            n3 = em.n("vector", lambda e, bi=bi: e.tensor_add(
                xb_[bi][:], xb_[bi][:], gB[5][:]), deps=[n2, gB_nd[5]],
                tag="a1n3")
            wf["lnb"] = n3
            wf[("a1b", bi)] = n3
            s1_ = em.dma("sync", lambda e, t=t, bi=bi: e.dma_start(
                out=d_a1[128 * t:128 * (t + 1), :], in_=xb_[bi][:]),
                deps=[n3], tag="a1st")
            wf[("a1a", bi)] = s1_
            a1_st.append(s1_)
        if "a1" in dbg_t:
            em.dma("gpsimd", lambda e: e.dma_start(
                out=dbg_t["a1"].ap(), in_=d_a1.ap()), deps=a1_st[-1:],
                tag="dbga1")

        t_id128f = nc.inline_tensor(np.eye(128, dtype=np.float32), "t_id128f")
        id128f = sb("id128f", (128, 128), F32)
        cm1 = em.dma("sync", lambda e: e.dma_start(
            out=id128f[:], in_=t_id128f.ap()), deps=[ff_st[-1]], tag="cm1")
        cm2 = cm1
        lg_tm = [sb(f"lgtm{t}", (128, 8), F32) for t in range(NT)]
        oh_tm = [sb(f"ohtm{t}", (128, 8), F32) for t in range(NT)]
        wfin = [sb(f"wfin{t}", (128, 1), F32) for t in range(NT)]
        eix = [sb(f"eix{t}", (128, 1), F32) for t in range(NT)]
        idxu = [sb(f"idxu{t}", (128, 1), U32) for t in range(NT)]
        stack_m2 = ExitStack()
        sbM2 = lambda n, s, d: stack_m2.enter_context(
            nc.sbuf_tensor(n, list(s), d))
        ohf = sbM2("ohf", (8, T), F32)
        rkf = sbM2("rkf", (8, T), F32)
        zr8 = sbM2("zr8", (8, 16), F32)
        cnt_sb = sbM2("cnt_sb", (8, 8), F32)
        base8 = sbM2("base8", (8, 1), F32)
        brow = sbM2("brow", (1, 8), F32)
        baseB = sbM2("baseB", (128, 8), F32)
        mx8 = sbM2("mx8", (128, 8), F32)
        idx8 = sbM2("idx8", (128, 8), U32)
        nm2 = sbM2("nm2", (128, 1), F32)
        den2 = sbM2("den2", (128, 1), F32)
        exps = sbM2("exps", (128, 8), F32)
        idxf = sbM2("idxf", (128, 1), F32)
        tidt = sbM2("tidt", (128, 1), F32)
        tmp8 = sbM2("tmp8", (128, 8), F32)
        lrk = sbM2("lrk", (128, 1), F32)
        kpc = sbM2("kpc", (128, 1), F32)
        bsb = sbM2("bsb", (128, 1), F32)
        gw_sb = sbM2("gw_sb", (128, 64), BF16)
        zrow = sbM2("zrow", (128, D), BF16)
        xnt_m = [sbM2(f"xntm{i}", (128, D), BF16) for i in range(2)]

        stack_m1 = ExitStack()
        xnf3 = [stack_m1.enter_context(
            nc.sbuf_tensor(f"xnf3_{j}", [128, T], BF16)) for j in range(8)]
        lgf = stack_m1.enter_context(nc.sbuf_tensor("lgf", [8, T], F32))
        xnf3_nd = []
        for j in range(8):
            xnf3_nd.append(em.dma("sync", lambda e, j=j:
                e.dma_start_transpose(xnf3[j][:],
                                      d_xe[16:16 + T, 128 * j:128 * (j + 1)]),
                deps=xe_st + [ff_st[-1]], tag=f"xnf3{j}"))
        gw_nd = []
        for k in range(8):
            gw_nd.append(em.dma("gpsimd", lambda e, k=k: e.dma_start(
                out=gw_sb[:, 8 * k:8 * (k + 1)],
                in_=ins["gwT"][128 * k:128 * (k + 1), :]),
                deps=[wE2_nd[k], ff_st[-1]], tag="gwl"))
        lg_nd = []
        for g in range(4):
            mms = []
            for k in range(8):
                dep = [gw_nd[k], xnf3_nd[k]]
                if k == 0:
                    dep.append(wf.get("lgps"))
                mms.append(em.n("tensor", lambda e, g=g, k=k: e.matmul(
                    ps3[0:8, 0:512], gw_sb[:, 8 * k:8 * (k + 1)],
                    xnf3[k][:, 512 * g:512 * (g + 1)],
                    start=(k == 0), stop=(k == 7)), deps=dep, tag="lgmm",
                    psb=[ps3]))
            ev = em.n("vector", lambda e, g=g: e.tensor_copy(
                lgf[0:8, 512 * g:512 * (g + 1)], ps3[0:8, 0:512]),
                deps=[mms[-1]], tag="lgev", psb=[ps3])
            wf["lgps"] = ev
            lg_nd.append(ev)
        lgt_nd = []
        for t in range(NT):
            tp = em.n("tensor", lambda e, t=t: e.transpose(
                ps4[:, 0:8], lgf[0:8, 128 * t:128 * (t + 1)], id16[0:8, 0:8]),
                deps=[lg_nd[-1], cm1, wf.get("lgtp")], tag="lgtp", psb=[ps4])
            ev = em.n("vector", lambda e, t=t: e.tensor_copy(
                lg_tm[t][:], ps4[:, 0:8]), deps=[tp], tag="lgte", psb=[ps4])
            wf["lgtp"] = ev
            lgt_nd.append(ev)
        stack_m1.close()

        z8 = em.n("vector", lambda e: e.memset(zr8[:], 0.0),
                  deps=[lgt_nd[-1]], tag="z8")
        rt_nd = []
        oht_nd = {}
        for t in range(NT):
            m1 = em.n("vector", lambda e, t=t: e.max(mx8[:], lg_tm[t][:]),
                      deps=[lgt_nd[t], wf.get("mx8")], tag="rmax")
            m2 = em.n("vector", lambda e, t=t: e.max_index(
                idx8[:], mx8[:], lg_tm[t][:]), deps=[m1], tag="ridx")
            ng = em.n("vector", lambda e: e.tensor_scalar_mul(
                nm2[:], mx8[:, 0:1], -1.0), deps=[m1], tag="rng")
            ex = em.n("scalar", lambda e, t=t: e.activation(
                exps[:], lg_tm[t][:], AF.Exp, bias=nm2[:],
                accum_out=den2[:]), deps=[ng, wf.get("exps")], tag="rexp")
            rcp = em.n("vector", lambda e, t=t: e.reciprocal(
                wfin[t][:], den2[:]), deps=[ex], tag="rrcp")
            cv = em.n("vector", lambda e, t=t: e.tensor_copy(
                eix[t][:], idx8[:, 0:1]), deps=[m2], tag="rcv")
            oh = em.n("vector", lambda e, t=t: e.tensor_single_scalar(
                oh_tm[t][:], iota8[:], eix[t][:], ALU.is_equal),
                deps=[cv, cl0[16]], tag="roh")
            oht_nd[t] = oh
            wf["mx8"] = ex
            wf["exps"] = rcp
            wf["idxf"] = oh
            tp = em.n("tensor", lambda e, t=t: e.transpose(
                ps4[0:8, 0:128], oh_tm[t][:], id128f[:]),
                deps=[oh, cm1, wf.get("ohtp")], tag="ohtp", psb=[ps4])
            ev = em.n("vector", lambda e, t=t: e.tensor_copy(
                ohf[0:8, 128 * t:128 * (t + 1)], ps4[0:8, 0:128]),
                deps=[tp], tag="ohte", psb=[ps4])
            wf["ohtp"] = ev
            rt_nd.append(ev)
        sc_ = em.n("vector", lambda e: e.tensor_tensor_scan(
            rkf[:], ohf[:],
            zr8[:, 0:1].broadcast_to([8, T]), 0.0, ALU.add, ALU.add),
            deps=rt_nd + [z8], tag="rscan")
        cw = em.dma("sync", lambda e: e.dma_start(
            out=d_cnt.ap(), in_=rkf[0:8, T - 1:T]), deps=[sc_], tag="cw")
        agc2 = em.n("gpsimd", lambda e: e.collective_compute(
            "AllGather", ALU.bypass, replica_groups=[list(range(8))],
            ins=[d_cnt.ap().opt()], outs=[d_cnto.ap().opt()]),
            w=1, tag="agc2", deps=[cw], coll=True)
        exc = em.n("vector", lambda e: e.tensor_sub(
            rkf[:], rkf[:], ohf[:]), deps=[sc_, cw], tag="rexc")
        lcn = em.dma("sync", lambda e: e.dma_start(
            out=cnt_sb[:],
            in_=d_cnto.ap().rearrange("(r e) one -> r (e one)", e=8)),
            deps=[agc2], tag="lcn")
        bs1 = em.n("tensor", lambda e: e.matmul(
            ps3[0:8, 0:1], cnt_sb[:], pri_s[:], start=True, stop=True),
            deps=[lcn, cl0[11], wf.get("lgps")], tag="bs1", psb=[ps3])
        bs2 = em.n("vector", lambda e: e.tensor_copy(
            base8[:], ps3[0:8, 0:1]), deps=[bs1], tag="bs2", psb=[ps3])
        bs3 = em.n("tensor", lambda e: e.transpose(
            ps3[0:1, 0:8], base8[:], id16[0:8, 0:8]), deps=[bs2],
            tag="bs3", psb=[ps3])
        bs4 = em.n("vector", lambda e: e.tensor_copy(
            brow[:], ps3[0:1, 0:8]), deps=[bs3], tag="bs4", psb=[ps3])
        bs5 = em.n("tensor", lambda e: e.matmul(
            ps3[:, 0:8], ones1f2[:], brow[:], start=True, stop=True),
            deps=[bs4, cm2], tag="bs5", psb=[ps3])
        bs6 = em.n("vector", lambda e: e.tensor_copy(
            baseB[:], ps3[:, 0:8]), deps=[bs5], tag="bs6", psb=[ps3])
        zrn = em.n("vector", lambda e: e.memset(zrow[:], 0.0),
                   deps=[lgt_nd[-1]], tag="zrn")
        dz_st = []
        for r in range(24):
            dz_st.append(em.dma("sync", lambda e, r=r: e.dma_start(
                out=d_disp[128 * r:128 * (r + 1), :], in_=zrow[:]),
                deps=[zrn], tag="dz"))
        sc_st = []
        for t in range(NT):
            bi = t % 2
            tp = em.n("tensor", lambda e, t=t: e.transpose(
                ps4[:, 0:8], rkf[0:8, 128 * t:128 * (t + 1)],
                id16[0:8, 0:8]), deps=[exc, wf.get("rktp")], tag="rktp",
                psb=[ps4])
            ev = em.n("vector", lambda e, t=t: e.tensor_copy(
                tmp8[:], ps4[:, 0:8]), deps=[tp, wf.get("tmp8")],
                tag="rkte", psb=[ps4])
            wf["rktp"] = ev
            g1_ = em.n("vector", lambda e: e.tensor_add(
                tmp8[:], tmp8[:], baseB[:]), deps=[ev, bs6], tag="rgr")
            k1 = em.n("vector", lambda e: e.tensor_single_scalar(
                exps[:], tmp8[:], float(CAP), ALU.is_lt), deps=[g1_,
                wf.get("exps2")], tag="rkp1")
            k2 = em.n("vector", lambda e, t=t: e.tensor_mul(
                exps[:], exps[:], oh_tm[t][:]), deps=[k1], tag="rkp2")
            kc = em.n("vector", lambda e: e.reduce_sum(
                kpc[:], exps[:], axis=AXL.X), deps=[k2], tag="rkc")
            l1_ = em.n("vector", lambda e, t=t: e.tensor_mul(
                tmp8[:], tmp8[:], oh_tm[t][:]), deps=[g1_, k2], tag="rlr1")
            l2_ = em.n("vector", lambda e: e.reduce_sum(
                lrk[:], tmp8[:], axis=AXL.X), deps=[l1_], tag="rlr2")
            wf["tmp8"] = l2_
            wf["exps2"] = kc
            wm_ = em.n("vector", lambda e, t=t: e.tensor_mul(
                wfin[t][:], wfin[t][:], kpc[:]), deps=[kc], tag="rwm")
            b1_ = em.n("vector", lambda e, t=t: e.tensor_mul(
                exps[:], baseB[:], oh_tm[t][:]), deps=[wm_, kc], tag="rb1")
            b2_ = em.n("vector", lambda e: e.reduce_sum(
                bsb[:], exps[:], axis=AXL.X), deps=[b1_], tag="rb2")
            wf["exps2"] = b2_
            l3_ = em.n("vector", lambda e: e.tensor_sub(
                lrk[:], lrk[:], bsb[:]), deps=[l2_, b2_], tag="rl3")
            rt = em.n("gpsimd", lambda e, t=t: e.iota(
                tidt[:], [[0, 1]], base=4096 + 128 * t,
                channel_multiplier=1,
                allow_small_or_imprecise_dtypes=True),
                deps=[wf.get("tidt"), exc], tag="riota")
            s1_ = em.n("vector", lambda e, t=t: e.scalar_tensor_tensor(
                idxf[:], eix[t][:], float(CAPL), lrk[:], ALU.mult, ALU.add),
                deps=[l3_, rt, wf.get("idxf2")], tag="rs1")
            s2_ = em.n("vector", lambda e: e.tensor_sub(
                idxf[:], idxf[:], tidt[:]), deps=[s1_], tag="rs2")
            s3_ = em.n("vector", lambda e, t=t: e.scalar_tensor_tensor(
                idxf[:], idxf[:], kpc[:], tidt[:], ALU.mult, ALU.add),
                deps=[s2_], tag="rs3")
            wf["tidt"] = s3_
            wf["idxf2"] = s3_
            cu = em.n("vector", lambda e, t=t: e.tensor_copy(
                idxu[t][:], idxf[:]), deps=[s3_], tag="rcu")
            lxn = em.dma("sync", lambda e, t=t, bi=bi: e.dma_start(
                out=xnt_m[bi][:], in_=d_xe[16 + 128 * t:16 + 128 * (t + 1), :]),
                deps=[xe_st[-1], wf.get(("xnm", bi)), lgt_nd[-1]],
                tag="lxnm")
            sct = em.dma("gpsimd", lambda e, t=t, bi=bi:
                e.indirect_dma_start(
                    d_disp.ap(), IndirectOffsetOnAxis(idxu[t][:], 0),
                    xnt_m[bi][:], None, bounds_check=8 * CAPL - 1,
                    oob_is_err=False),
                deps=[cu, lxn, dz_st[-1]], tag="sct")
            wf[("xnm", bi)] = sct
            sc_st.append(sct)
        stack_m2.close()

        stack_m3 = ExitStack()
        sbM = lambda n, s, d: stack_m3.enter_context(
            nc.sbuf_tensor(n, list(s), d))
        wEx = [sbM(f"wEx{k}", (128, 4096), BF16) for k in range(8)]
        wE2x = [sbM(f"wE2x{oc}", (128, D), BF16) for oc in range(14)] \
            + [gB[4], gB[5]] + wbuf[0] + wbuf[1]
        dfm = csrc
        he = [sbM(f"he{oc}", (128, CAPL), BF16) for oc in range(16)] \
            + csc2 + pkt + gB[0:4]
        b1e_sb = sbM("b1e_sb", (128, 32), F32)
        b2row = rows6f
        b2B = xc[0]
        eo_t = [sbM(f"eo{i}", (128, 512), BF16) for i in range(2)]
        eout_st = []
        for ei in range(8):
            w1l, dfl = [], []
            for k in range(8):
                w1l.append(em.dma("gpsimd", lambda e, ei=ei, k=k:
                    e.dma_start(out=wEx[k][:],
                                in_=ins["w1e"][ei, 128 * k:128 * (k + 1), :]),
                    deps=[wf.get(("wEx", k)), gw_nd[-1], a1_st[-1],
                          sc_st[-1]],
                    tag="w1el"))
                dfl.append(em.dma("sync", lambda e, ei=ei, k=k:
                    e.dma_start_transpose(
                        dfm[k][:, 0:CAPL],
                        d_disp[CAPL * ei:CAPL * (ei + 1),
                               128 * k:128 * (k + 1)]),
                    deps=sc_st + [wf.get(("dfm", k)),
                                  a1_st[-1] if ei == 0 else None],
                    tag="dfml"))
            b1l = em.dma("sync", lambda e, ei=ei: e.dma_start(
                out=b1e_sb[:], in_=ins["b1e_c"][ei]),
                deps=[wf.get("b1e"), sc_st[-1] if ei == 0 else None],
                tag="b1el")
            he_nd = []
            for oc in range(32):
                ps = ps5 if oc % 2 == 0 else ps6
                mms = []
                for k in range(8):
                    dep = [w1l[k], dfl[k]]
                    if k == 0:
                        dep.append(wf.get(("eps", id(ps))))
                    mms.append(em.n("tensor", lambda e, oc=oc, k=k, ps=ps:
                        e.matmul(ps[:, 0:CAPL],
                                 wEx[k][:, 128 * oc:128 * (oc + 1)],
                                 dfm[k][:, 0:CAPL],
                                 start=(k == 0), stop=(k == 7)),
                        deps=dep, tag="e1mm", psb=[ps]))
                ev = em.n("vector", lambda e, oc=oc, ps=ps: e.tensor_scalar(
                    he[oc][:, 0:CAPL], ps[:, 0:CAPL], b1e_sb[:, oc:oc + 1],
                    0.0, ALU.add, ALU.max), deps=[mms[-1], b1l,
                    wf.get(("he", oc))], tag="e1ev", psb=[ps])
                wf[("eps", id(ps))] = ev
                he_nd.append(ev)
            w2l = []
            for oc in range(32):
                w2l.append(em.dma("gpsimd", lambda e, ei=ei, oc=oc:
                    e.dma_start(
                        out=wE2x[oc][:],
                        in_=ins["w2e"][ei, 128 * oc:128 * (oc + 1), :]),
                    deps=[wf.get(("wE2x", oc)), a1_st[-1] if ei == 0 else None,
                          sc_st[-1] if ei == 0 else None], tag="w2el"))
            b2l = em.dma("gpsimd", lambda e, ei=ei: e.dma_start(
                out=b2row[:], in_=ins["b2e_r"][ei:ei + 1, :]),
                deps=[wf.get("b2e"), a1_st[-1] if ei == 0 else None],
                tag="b2el")
            bb1 = em.n("tensor", lambda e: e.matmul(
                ps3[:, 0:512], one1b[:], b2row[0:1, 0:512],
                start=True, stop=True), deps=[b2l, wf.get("b2ps")],
                tag="bb1", psb=[ps3])
            bb2 = em.n("vector", lambda e: e.tensor_copy(
                b2B[:, 0:512], ps3[:, 0:512]), deps=[bb1,
                wf.get("b2B"), a1_st[-1] if ei == 0 else None],
                tag="bb2", psb=[ps3])
            bb3 = em.n("tensor", lambda e: e.matmul(
                ps3[:, 0:512], one1b[:], b2row[0:1, 512:1024],
                start=True, stop=True), deps=[bb2], tag="bb3", psb=[ps3])
            bb4 = em.n("vector", lambda e: e.tensor_copy(
                b2B[:, 512:1024], ps3[:, 0:512]), deps=[bb3], tag="bb4",
                psb=[ps3])
            wf["b2ps"] = bb4
            for rt_ in range(3):
                for hf in range(2):
                    ps = ps1 if hf == 0 else ps2
                    mms2 = []
                    for oc in range(32):
                        dep = [w2l[oc], he_nd[oc]]
                        if oc == 0:
                            dep.append(wf.get(("e2ps", id(ps))))
                        mms2.append(em.n("tensor", lambda e, rt_=rt_, hf=hf,
                                         oc=oc, ps=ps: e.matmul(
                            ps[:, 0:512],
                            he[oc][:, 128 * rt_:128 * (rt_ + 1)],
                            wE2x[oc][:, 512 * hf:512 * (hf + 1)],
                            start=(oc == 0), stop=(oc == 31)),
                            deps=dep, tag="e2mm", psb=[ps]))
                    bi3 = (rt_ * 2 + hf) % 2
                    ev = em.n("vector", lambda e, hf=hf, ps=ps, bi3=bi3:
                        e.tensor_add(eo_t[bi3][:], ps[:, 0:512],
                                     b2B[:, 512 * hf:512 * (hf + 1)]),
                        deps=[mms2[-1], bb4, wf.get(("eo", bi3))],
                        tag="e2ev", psb=[ps])
                    wf[("e2ps", id(ps))] = ev
                    st_ = em.dma("sync", lambda e, ei=ei, rt_=rt_, hf=hf,
                                 bi3=bi3: e.dma_start(
                        out=d_eout[CAPL * ei + 128 * rt_:
                                   CAPL * ei + 128 * (rt_ + 1),
                                   512 * hf:512 * (hf + 1)],
                        in_=eo_t[bi3][:]), deps=[ev], tag="eopark")
                    wf[("eo", bi3)] = st_
                    eout_st.append(st_)
            for k in range(8):
                wf[("wEx", k)] = mms[-1]
                wf[("dfm", k)] = mms[-1]
            for oc in range(32):
                wf[("wE2x", oc)] = mms2[-1]
            for oc in range(32):
                wf[("he", oc)] = mms2[-1]
            wf["b1e"] = he_nd[-1]
            wf["b2e"] = bb4
            wf["b2B"] = ev
        stack_m3.close()

        moeg = sb("moeg", (128, D), BF16)
        fin_st = []
        for t in range(NT):
            bi = t % 2
            gt = em.dma("gpsimd", lambda e, t=t: e.indirect_dma_start(
                moeg[:], None, d_eout.ap(),
                IndirectOffsetOnAxis(idxu[t][:], 0),
                bounds_check=8 * CAPL - 1, oob_is_err=False),
                deps=eout_st + [wf.get("moeg")], tag="gth")
            ct = em.n("vector", lambda e, t=t: e.tensor_scalar_mul(
                xb_[0][:], moeg[:], wfin[t][:]),
                deps=[gt, wf.get("fb0")], tag="fmul")
            wf["moeg"] = ct
            if "moe" in dbg_t:
                em.dma("sync", lambda e, t=t: e.dma_start(
                    out=dbg_t["moe"][128 * t:128 * (t + 1), :],
                    in_=xb_[0][:]), deps=[ct], tag="dbgmoe")
            la = em.dma("sync", lambda e, t=t: e.dma_start(
                out=xa[0][:], in_=ins["x1in"][128 * t:128 * (t + 1), :]),
                deps=[wf.get("fa0")], tag="fl1")
            a1 = em.n("vector", lambda e: e.tensor_add(
                xb_[0][:], xb_[0][:], xa[0][:]), deps=[la, ct], tag="fadd1")
            wf["fa0"] = a1
            lb = em.dma("sync", lambda e, t=t: e.dma_start(
                out=xa[0][:], in_=d_a1[128 * t:128 * (t + 1), :]),
                deps=[a1, a1_st[-1]], tag="fl2")
            a2 = em.n("vector", lambda e: e.tensor_add(
                xb_[0][:], xb_[0][:], xa[0][:]), deps=[lb], tag="fadd2")
            wf["fa0"] = a2
            st_ = em.dma("sync", lambda e, t=t: e.dma_start(
                out=yout[128 * t:128 * (t + 1), :], in_=xb_[0][:]),
                deps=[a2], tag="yst")
            wf["fb0"] = st_
            fin_st.append(st_)
        if "idx" in dbg_t:
            for t in range(NT):
                em.dma("sync", lambda e, t=t: e.dma_start(
                    out=dbg_t["idx"][128 * t:128 * (t + 1), :],
                    in_=idxu[t][:]), deps=fin_st, tag="dbgidx")

        if ctx is None:
            em.build(block, stack)
    return nc


def build_fused():
    nc = bass.Bass("TRN2", target_bir_lowering=False)
    em = Em(nc)
    stack = ExitStack()
    with stack:
        block = stack.enter_context(nc.Block())
        x1h = build_l0(ctx=(nc, em, block))
        build_l1(ctx=(nc, em, block), x1_src=x1h)
        em.build(block, stack)
    return nc



def kernel(**inputs):
    maps = stage_inputs(inputs)
    nc = build_fused()
    res = run_bass_kernel_spmd(nc, maps, core_ids=list(range(NCORE)))
    out = np.zeros((B, S, D), np.float32)
    for c in range(NCORE):
        b, hc = c // 2, c % 2
        out[b, hc * T:(hc + 1) * T] = res.results[c]["yout"]
    return out


# version 08 (final)
# speedup vs baseline: 1.0365x; 1.0365x over previous
import numpy as np
import ml_dtypes
from contextlib import ExitStack

import concourse.bass as bass
import concourse.mybir as mybir
from concourse.bass_utils import run_bass_kernel_spmd
from concourse.bass_core_rust import IndirectOffsetOnAxis

F32 = mybir.dt.float32
BF16 = mybir.dt.bfloat16
U32 = mybir.dt.uint32
AF = mybir.ActivationFunctionType
ALU = mybir.AluOpType
AXL = mybir.AxisListType

NCORE = 8
B, S, D = 4, 4096, 1024
T = 2048
NT = 16
NH = 16
HD = 64
C = 128
NCH = 16
CLW = 129
CLTOT = NCH * CLW
NEG = -1e30
EPS_LN = 1e-5
CAPL = 384
CAP = 2560

ENGINES = ("sync", "gpsimd", "vector", "scalar", "tensor")


class Node:
    __slots__ = ("i", "eng", "fn", "deps", "w", "signal", "sig", "tag",
                 "psb", "coll", "csem")

    def __init__(self, i, eng, fn, deps, w, tag=""):
        self.i = i
        self.eng = eng
        self.fn = fn
        self.deps = deps
        self.w = w
        self.signal = False
        self.sig = 0
        self.tag = tag


class Em:

    def __init__(self, nc):
        self.nc = nc
        self.nodes = []
        self.barrier = None

    def set_barrier(self):
        last = {}
        colls = []
        for nd in self.nodes:
            last[(nd.eng, nd.w)] = nd
            if nd.coll:
                colls.append(nd)
        self.barrier = list(last.values()) + colls

    def n(self, eng, fn, deps=(), w=1, tag="", psb=None, coll=False):
        deps = [d for d in deps if d is not None]
        if self.barrier is not None:
            deps = deps + self.barrier
        nd = Node(len(self.nodes), eng, fn, deps, w, tag)
        nd.psb = psb or []
        nd.coll = coll
        nd.csem = None
        for d in deps:
            assert d.i < nd.i, f"dep order violated at {tag}"
        self.nodes.append(nd)
        return nd

    def dma(self, eng, fn, deps=(), tag=""):
        return self.n(eng, fn, deps, w=16, tag=tag)

    def build(self, block, stack, limit=None):
        if limit is not None:
            self.nodes = self.nodes[:limit]
        last_ps = {}
        for nd in self.nodes:
            for h in getattr(nd, "psb", []):
                k = id(h)
                p = last_ps.get(k)
                if p is not None and p.eng != nd.eng and p not in nd.deps:
                    nd.deps.append(p)
                last_ps[k] = nd
        sems = {e: stack.enter_context(self.nc.semaphore(f"S_{e}"))
                for e in ENGINES}
        dsems = {e: stack.enter_context(self.nc.semaphore(f"D_{e}"))
                 for e in ("sync", "gpsimd", "scalar", "vector")}

        ncoll = sum(1 for nd in self.nodes if getattr(nd, "coll", False))
        csems = [stack.enter_context(self.nc.semaphore(f"C_{i}"))
                 for i in range(ncoll)]
        ci = [0]
        for nd in self.nodes:
            if getattr(nd, "coll", False):
                nd.csem = csems[ci[0]]
                ci[0] += 1

        def semof(nd):
            if nd.csem is not None:
                return nd.csem
            if nd.w == 16:
                return dsems[nd.eng]
            return sems[nd.eng]

        def semkey(nd):
            if nd.csem is not None:
                return ("coll", nd.i)
            return (nd.eng, "d" if nd.w == 16 else "c")

        for nd in self.nodes:
            if nd.w == 16:
                nd.signal = True
            for d in nd.deps:
                if d.eng != nd.eng or nd.eng != "tensor" or d.w == 16:
                    d.signal = True
        last = {}
        for nd in self.nodes:
            last[semkey(nd)] = nd
        for nd in last.values():
            nd.signal = True
        cnt = {}
        for nd in self.nodes:
            if nd.signal:
                k = semkey(nd)
                cnt[k] = cnt.get(k, 0) + nd.w
                nd.sig = cnt[k]
        totals = dict(cnt)
        consumed = set()
        for nd in self.nodes:
            for d in nd.deps:
                consumed.add(d.i)

        def emit_stream(e_obj, ename):
            hwm = {}
            for nd in self.nodes:
                if nd.eng != ename:
                    continue
                for d in nd.deps:
                    if not d.signal:
                        continue
                    k = semkey(d)
                    if k == (ename, "c") and ename == "tensor":
                        continue
                    if d.sig > hwm.get(k, 0):
                        e_obj.wait_ge(semof(d), d.sig)
                        hwm[k] = d.sig
                ins = nd.fn(e_obj)
                assert ins is not None, f"emit fn returned None: {nd.tag}"
                if nd.signal:
                    ins.then_inc(semof(nd), nd.w)
                    if nd.w == 16 and nd.i in consumed:
                        e_obj.wait_ge(semof(nd), nd.sig)
            if ename == "sync":
                for k, v in totals.items():
                    if v > 0 and k != ("sync", "d") and k[0] != "coll":
                        e_obj.wait_ge(sems[k[0]] if k[1] == "c"
                                      else dsems[k[0]], v)
                if totals.get(("sync", "d"), 0) > 0:
                    e_obj.wait_ge(dsems["sync"], totals[("sync", "d")])

        @block.sync
        def _(e):
            emit_stream(e, "sync")

        @block.gpsimd
        def _(e):
            emit_stream(e, "gpsimd")

        @block.vector
        def _(e):
            emit_stream(e, "vector")

        @block.scalar
        def _(e):
            emit_stream(e, "scalar")

        @block.tensor
        def _(e):
            emit_stream(e, "tensor")


class NCProxy:

    def __init__(self, nc, pf):
        self._nc = nc
        self._pf = pf

    def __getattr__(self, a):
        return getattr(self._nc, a)

    def sbuf_tensor(self, name, *a, **k):
        return self._nc.sbuf_tensor(self._pf + name, *a, **k)

    def psum_tensor(self, name, *a, **k):
        return self._nc.psum_tensor(self._pf + name, *a, **k)

    def inline_tensor(self, arr, name):
        return self._nc.inline_tensor(arr, self._pf + name)



def _t(a):
    return np.ascontiguousarray(np.asarray(a, np.float32).T)


def stage_inputs(inp):
    x = np.asarray(inp["L0_attn_x"], np.float32)
    maps = []
    r = np.arange(128)[:, None]
    j = np.arange(144)[None, :]
    band = np.where((j >= r) & (j <= r + 16), 0.0, NEG).astype(np.float32)
    Wab64 = np.zeros((D, 64), np.float32)
    Wab64[:, 0:16] = np.asarray(inp["L0_attn_Wa"], np.float32)
    Wab64[:, 32:48] = np.asarray(inp["L0_attn_Wb"], np.float32)
    bab64 = np.zeros((64, 1), np.float32)
    bab64[0:16, 0] = np.asarray(inp["L0_attn_ba"], np.float32)
    bab64[32:48, 0] = np.asarray(inp["L0_attn_bb"], np.float32)
    Wg = np.asarray(inp["L1_attn_Wg"], np.float32)
    P = np.asarray(inp["L1_attn_P"], np.float32)
    P16 = np.zeros((128, D), np.float32)
    P16[:10] = P
    b1e = np.asarray(inp["L1_ffn_b1"], np.float32)
    gm = np.zeros((1, 2176), np.float32)
    gm[0, :2048] = 1.0
    gm[0, 2048:2058] = 0.5
    for c in range(NCORE):
        b, hc = c // 2, c % 2
        t0 = hc * T
        m = {}
        xctx = np.zeros((2176, D), np.float32)
        if hc == 1:
            xctx[127] = x[b, t0 - 1]
        xctx[128:] = x[b, t0:t0 + T]
        m["xctx"] = xctx
        m["Wq"] = np.asarray(inp["L0_attn_Wq"], np.float32)
        m["Wk"] = np.asarray(inp["L0_attn_Wk"], np.float32)
        m["Wv"] = np.asarray(inp["L0_attn_Wv"], np.float32)
        m["Wab"] = Wab64
        m["bab"] = bab64
        m["Wg0"] = np.asarray(inp["L0_attn_Wg"], np.float32)
        m["Wo0"] = np.asarray(inp["L0_attn_Wo"], np.float32)
        m["gn_cols"] = _t(np.asarray(inp["L0_attn_g_norm"],
                                     np.float32).reshape(8, 128))
        m["WrT"] = _t(inp["L0_ffn_W_r"])
        m["WkT"] = _t(inp["L0_ffn_W_k"])
        m["WvT"] = _t(inp["L0_ffn_W_v"])
        m["mur_c"] = _t(np.asarray(inp["L0_ffn_mu_r"], np.float32).reshape(8, 128))
        m["muk_c"] = _t(np.asarray(inp["L0_ffn_mu_k"], np.float32).reshape(8, 128))
        m["state_mask"] = np.full((1, 1), float(hc), np.float32)
        m["memWT"] = _t(inp["L1_attn_mem_W"])
        m["WiT"] = _t(inp["L1_attn_Wi"])
        m["Wl1T"] = _t(inp["L1_attn_Wl1"])
        m["Wl2T"] = _t(inp["L1_attn_Wl2"])
        m["WoT1"] = _t(inp["L1_attn_Wo"])
        m["WgAT"] = _t(Wg[:, :D])
        m["WgBT"] = _t(Wg[:, D:])
        m["W1T"] = _t(inp["L1_attn_W1"])
        m["W2T"] = _t(inp["L1_attn_W2"])
        m["PT16"] = _t(P16)
        m["bi_c"] = _t(np.asarray(inp["L1_attn_bi"], np.float32).reshape(8, 128))
        m["bl1_c"] = _t(np.asarray(inp["L1_attn_bl1"], np.float32).reshape(8, 128))
        m["bl2_c"] = _t(np.asarray(inp["L1_attn_bl2"], np.float32).reshape(8, 128))
        m["memb_c"] = _t(np.asarray(inp["L1_attn_mem_b"], np.float32).reshape(8, 128))
        m["b1_c"] = _t(np.asarray(inp["L1_attn_b1"], np.float32).reshape(32, 128))
        m["rows6"] = np.stack([
            np.asarray(inp["L1_attn_g1"], np.float32),
            np.asarray(inp["L1_attn_be1"], np.float32),
            np.asarray(inp["L1_attn_g2"], np.float32),
            np.asarray(inp["L1_attn_be2"], np.float32),
            np.asarray(inp["L1_attn_g3"], np.float32),
            np.asarray(inp["L1_attn_be3"], np.float32)])
        m["bg_c"] = _t(np.asarray(inp["L1_attn_bg"], np.float32).reshape(8, 128))
        m["bo_c"] = _t(np.asarray(inp["L1_attn_bo"], np.float32).reshape(8, 128))
        m["b2f_c"] = _t(np.asarray(inp["L1_attn_b2"], np.float32).reshape(8, 128))
        m["fg_lr"] = np.array(
            [[float(np.asarray(inp["L1_attn_fgate"]).ravel()[0]),
              float(np.asarray(inp["L1_attn_lrate"]).ravel()[0])]],
            np.float32)
        m["gradmask"] = gm
        wm0 = band.copy()
        if hc == 0:
            wm0[:, :16] = NEG
        m["winmask0"] = wm0
        m["band"] = band
        m["gwT"] = _t(inp["L1_ffn_gate_w"])
        m["w1e"] = np.asarray(inp["L1_ffn_w1"], np.float32)
        m["w2e"] = np.asarray(inp["L1_ffn_w2"], np.float32)
        m["b1e_c"] = np.stack([_t(b1e[e].reshape(32, 128)) for e in range(8)])
        m["b2e_r"] = np.asarray(inp["L1_ffn_b2"], np.float32)
        pm = np.zeros((8, 1), np.float32)
        pm[:c] = 1.0
        m["prior_mask"] = pm
        maps.append(m)
    return maps



L0_INPUTS = [
    ("xctx", (2176, D), F32),
    ("Wq", (D, D), F32), ("Wk", (D, D), F32), ("Wv", (D, D), F32),
    ("Wab", (D, 64), F32), ("bab", (64, 1), F32),
    ("Wg0", (D, D), F32), ("Wo0", (D, D), F32), ("gn_cols", (128, 8), F32),
    ("WrT", (D, D), F32), ("WkT", (D, D), F32), ("WvT", (D, D), F32),
    ("mur_c", (128, 8), F32), ("muk_c", (128, 8), F32),
    ("state_mask", (1, 1), F32),
]


def build_l0(dbg=None, limit=None, ctx=None):
    dbg = dbg or []
    if ctx is None:
        nc = bass.Bass("TRN2", target_bir_lowering=False)
    else:
        nc, em, block = ctx
    ins = {n: nc.dram_tensor(n, list(s), d, kind="ExternalInput")
           for n, s, d in L0_INPUTS}
    ident16 = nc.inline_tensor(np.eye(16, dtype=np.float32), "ident16")
    r_ = np.arange(128)[:, None]
    c_ = np.arange(128)[None, :]
    mlo_np = np.where(c_ < r_, 0.0, NEG).astype(np.float32)
    mup_np = np.where(c_ > r_, 0.0, NEG).astype(np.float32)
    mui_np = np.where(c_ >= r_, 0.0, NEG).astype(np.float32)
    t_mlo = nc.inline_tensor(mlo_np, "t_mlo")
    t_mup = nc.inline_tensor(mup_np, "t_mup")
    t_mui = nc.inline_tensor(mui_np, "t_mui")
    t_ones1f = nc.inline_tensor(np.ones((65, 128), np.float32), "t_ones1f")
    t_id64f = nc.inline_tensor(np.eye(64, dtype=np.float32), "t_id64f")
    t_bd = []
    for jj in range(8):
        bd = np.zeros((128, 16), ml_dtypes.bfloat16)
        bd[0:64, 2 * jj] = 1
        bd[64:128, 2 * jj + 1] = 1
        t_bd.append(nc.inline_tensor(bd, f"t_bd{jj}"))

    if ctx is None:
        x1_out = nc.dram_tensor("x1", [T, D], F32, kind="ExternalOutput")
    else:
        x1_out = nc.dram_tensor("x1", [T, D], F32)
    d_xn = nc.dram_tensor("d_xn", [2176, D], BF16)
    d_qf = nc.dram_tensor("d_qf", [D, T], BF16)
    d_kf = nc.dram_tensor("d_kf", [D, T], BF16)
    d_ktm = nc.dram_tensor("d_ktm", [T, D], BF16)
    d_vtm = nc.dram_tensor("d_vtm", [T, D], BF16)
    d_cl = nc.dram_tensor("d_cl", [16, CLTOT], F32)
    d_nclk = nc.dram_tensor("d_nclk", [16, CLTOT], F32)
    d_clpb = nc.dram_tensor("d_clpb", [16, CLTOT], F32)
    d_clq = nc.dram_tensor("d_clq", [16, CLTOT], F32)
    d_bet = nc.dram_tensor("d_bet", [16, T], F32)
    d_attn = nc.dram_tensor("d_attn", [T, D], F32)
    d_f0 = nc.dram_tensor("d_f0", [T, D], F32)
    d_rtm = nc.dram_tensor("d_rtm", [T, D], BF16)
    d_pk = [nc.dram_tensor(f"d_pk{i}", [128, 16 * 384], BF16)
            for i in range(NCH)]
    d_sx = nc.dram_tensor("d_sx", [64, 1024], BF16)
    d_sxo = nc.dram_tensor("d_sxo", [128, 1024], BF16)
    dbg_t = {}
    for name in dbg:
        shp = {"cl": (16, CLTOT), "clpb": (16, CLTOT), "clq": (16, CLTOT),
               "nclk": (16, CLTOT), "ab": (64, T), "y": (D, T),
               "s1": (64, 1024), "z": (D, T)}[name]
        dbg_t[name] = nc.dram_tensor("dbg_" + name, list(shp), F32,
                                     kind="ExternalOutput")

    if ctx is None:
        em = Em(nc)
    stack = ExitStack()
    with stack:
        sb = lambda name, shape, dt: stack.enter_context(
            nc.sbuf_tensor(name, list(shape), dt))
        if ctx is None:
            block = stack.enter_context(nc.Block())

        s1 = sb("s1", (64, 1024), BF16)
        s2 = sb("s2", (64, 1024), BF16)
        sG = sb("sG", (64, 1024), BF16)
        yf = [sb(f"yf{j}", (128, T), BF16) for j in range(8)]
        clc = sb("clc", (1, 256), F32)
        mcol = sb("mcol", (64, 1), F32)
        m_lo = sb("m_lo", (128, 128), F32)
        m_up = sb("m_up", (128, 128), F32)
        m_ui = sb("m_ui", (128, 128), F32)
        one1f = sb("one1f", (65, 128), F32)
        id16 = sb("id16", (16, 16), F32)
        id64f = sb("id64f", (64, 64), F32)
        bd16 = [sb(f"bd16_{j}", (128, 16), BF16) for j in range(8)]
        bab_s = sb("bab_s", (64, 1), F32)
        gn_s = sb("gn_s", (128, 8), F32)
        mur_s = sb("mur_s", (128, 8), F32)
        muk_s = sb("muk_s", (128, 8), F32)
        smk = sb("smk", (1, 1), F32)

        cld = []
        for tile, src in ([(m_lo, t_mlo), (m_up, t_mup), (m_ui, t_mui),
                           (one1f, t_ones1f), (id16, ident16),
                           (id64f, t_id64f), (bab_s, ins["bab"]),
                           (gn_s, ins["gn_cols"]), (mur_s, ins["mur_c"]),
                           (muk_s, ins["muk_c"]), (smk, ins["state_mask"])] +
                          [(bd16[j], t_bd[j]) for j in range(8)]):
            cld.append(em.dma("sync", lambda e, t=tile, s=src: e.dma_start(
                out=t[:], in_=s.ap()), tag="cload"))
        cldall = cld[-1]

        wbuf = [[sb(f"wbuf{i}_{k}", (128, D), BF16) for k in range(8)]
                for i in range(2)]
        pkt = [sb(f"pkt{i}", (128, 512), BF16) for i in range(4)]
        sqs = sb("sqs", (128, 512), BF16)
        wabt = sb("wabt", (128, 64 * 8), BF16)
        xt = [sb(f"xt{i}", (128, D), F32) for i in range(2)]
        xnt = [sb(f"xnt{i}", (128, D), BF16) for i in range(2)]
        st6 = sb("st6", (128, 12), F32)
        mv = sb("mv", (128, 2), F32)
        rstd = sb("rstd", (128, 1), F32)
        eps_c = sb("eps_c", (128, 1), F32)
        stack_x1 = ExitStack()
        xnf = [stack_x1.enter_context(
            nc.sbuf_tensor(f"xnf{j}", [128, 2176], BF16)) for j in range(8)]
        epn = em.n("vector", lambda e: e.memset(eps_c[:], EPS_LN), tag="eps")
        xn_st = []
        prev_use = [None, None]
        prev_ln = None
        for t in range(17):
            bi = t % 2
            ld = em.dma("sync", lambda e, t=t, bi=bi: e.dma_start(
                out=xt[bi][:], in_=ins["xctx"][128 * t:128 * (t + 1), :]),
                deps=[prev_use[bi]], tag=f"xld{t}")
            b1 = em.n("vector", lambda e, bi=bi: e.bn_stats(
                st6[:, 0:6], xt[bi][:, 0:512]), deps=[ld, prev_ln], tag="bn1")
            b2 = em.n("vector", lambda e, bi=bi: e.bn_stats(
                st6[:, 6:12], xt[bi][:, 512:1024]), deps=[ld], tag="bn2")
            b3 = em.n("vector", lambda e: e.bn_aggr(mv[:], st6[:]),
                      deps=[b1, b2], tag="bn3")
            sq = em.n("scalar", lambda e: e.activation(
                rstd[:], mv[:, 1:2], AF.Sqrt, bias=eps_c[:]),
                deps=[b3, epn], tag="sqrt")
            rc = em.n("vector", lambda e: e.reciprocal(rstd[:], rstd[:]),
                      deps=[sq], tag="recip")
            xn = em.n("vector", lambda e, bi=bi: e.tensor_scalar(
                xnt[bi][:], xt[bi][:], mv[:, 0:1], rstd[:],
                ALU.subtract, ALU.mult), deps=[rc], tag="ln")
            prev_use[bi] = xn
            prev_ln = xn
            xn_st.append(em.dma("sync", lambda e, t=t, bi=bi: e.dma_start(
                out=d_xn[128 * t:128 * (t + 1), :], in_=xnt[bi][:]),
                deps=[xn], tag=f"xst{t}"))

        xnf_nd = []
        for j in range(8):
            xnf_nd.append(em.dma("sync", lambda e, j=j: e.dma_start_transpose(
                xnf[j][:], d_xn[:, 128 * j:128 * (j + 1)]),
                deps=xn_st, tag=f"xnf{j}"))

        stack_rows = ExitStack()
        sbr = lambda name, shape, dt: stack_rows.enter_context(
            nc.sbuf_tensor(name, list(shape), dt))
        ab = sbr("ab", (64, T), F32)
        cl = sbr("cl", (16, CLTOT), F32)
        nclk = sbr("nclk", (16, CLTOT), F32)
        clpb = sbr("clpb", (16, CLTOT), F32)
        clq = sbr("clq", (16, CLTOT), F32)
        lnq = sbr("lnq", (16, T), F32)
        lnk = sbr("lnk", (16, T), F32)
        lnb = sbr("lnb", (16, T), F32)
        lna = sbr("lna", (16, T), F32)
        nacc = sbr("nacc", (16, T), F32)
        zz = sbr("zz", (16, 128), F32)
        stack_pj = ExitStack()
        psA = stack_pj.enter_context(nc.psum_tensor("psA", [128, 512], F32))
        psB = stack_pj.enter_context(nc.psum_tensor("psB", [128, 512], F32))
        psN4 = [stack_pj.enter_context(nc.psum_tensor(f"psN{i}", [16, 512], F32))
                for i in range(4)]
        ps64 = stack_pj.enter_context(nc.psum_tensor("ps64", [64, 512], F32))

        wl_free = [[None] * 8, [None] * 8]

        def wload(idx, src, extra=(), fd=D):
            nds = []
            for k in range(8):
                nds.append(em.dma("gpsimd", lambda e, k=k, idx=idx, src=src,
                                  fd=fd: e.dma_start(
                    out=wbuf[idx][k][:, 0:fd],
                    in_=src[128 * k:128 * (k + 1), :]),
                    deps=[wl_free[idx][k]] + list(extra), tag=f"wl{idx}_{k}"))
                wl_free[idx][k] = nds[-1]
            return nds

        ps_free = {0: None, 1: None}
        pk_free = [None] * 4
        pidx = [0]

        wf_norm = {}
        last_na = {}
        norm_rd = [None]

        def fm_proj(widx, wnd, dst, normps, parks, extra_dep=None):
            last_nmm = []
            last_na.clear()
            for jj in range(8):
                for g in range(4):
                    pi = pidx[0] % 2
                    pidx[0] += 1
                    ps = psA if pi == 0 else psB
                    mms = []
                    for k in range(8):
                        dep = [wnd[k], xnf_nd[k]]
                        if k == 0:
                            pf = ps_free[pi]
                            dep += (pf if isinstance(pf, list) else [pf])
                            dep.append(extra_dep)
                        mms.append(em.n("tensor",
                            lambda e, jj=jj, g=g, k=k, ps=ps, widx=widx:
                            e.matmul(ps[:],
                                     wbuf[widx][k][:, 128 * jj:128 * (jj + 1)],
                                     xnf[k][:, 128 + 512 * g:128 + 512 * (g + 1)],
                                     start=(k == 0), stop=(k == 7)),
                            deps=dep, tag=f"fm{jj}_{g}_{k}", psb=[ps]))
                    pki = pidx[0] % 4
                    ev = em.n("vector", lambda e, ps=ps, pki=pki:
                        e.tensor_copy(pkt[pki][:], ps[:]),
                        deps=[mms[-1], pk_free[pki]], tag="fmev",
                        psb=[ps])
                    frees = [ev]
                    if normps is not None:
                        sqn = em.n("scalar", lambda e, ps=ps: e.activation(
                            sqs[:], ps[:], AF.Square), deps=[mms[-1]],
                            tag="fsq", psb=[ps])
                        noff = pidx[0] % 4
                        psn = psN4[noff]
                        nmm = em.n("tensor", lambda e, jj=jj, psn=psn:
                            e.matmul(psn[:], bd16[jj][:], sqs[:],
                                     start=True, stop=True),
                            deps=[sqn, wf_norm.get(noff)] + cld[:1],
                            tag="fnm", psb=[psn])
                        if jj == 0:
                            na = em.n("vector", lambda e, g=g, psn=psn:
                                e.tensor_copy(nacc[:, 512 * g:512 * (g + 1)],
                                              psn[:]),
                                deps=[nmm, norm_rd[0]], tag="fna0",
                                psb=[psn])
                        else:
                            na = em.n("vector", lambda e, g=g, psn=psn:
                                e.tensor_add(nacc[:, 512 * g:512 * (g + 1)],
                                             nacc[:, 512 * g:512 * (g + 1)],
                                             psn[:]),
                                deps=[nmm, last_na.get(g)], tag="fna",
                                psb=[psn])
                        last_na[g] = na
                        wf_norm[noff] = na
                        if jj == 7:
                            last_nmm.append(na)
                        frees.append(sqn)
                    ps_free[pi] = frees
                    st = em.dma("sync", lambda e, jj=jj, g=g, pki=pki,
                                dst=dst: e.dma_start(
                        out=dst[128 * jj:128 * (jj + 1),
                                512 * g:512 * (g + 1)],
                        in_=pkt[pki][:]), deps=[ev], tag="fpark")
                    pk_free[pki] = st
                    parks.append(st)
            return last_nmm

        def recip_norm(normps, out_row, dep_mms, tagp):
            n1 = em.n("scalar", lambda e: e.activation(
                out_row[:], nacc[:], AF.Sqrt), deps=dep_mms,
                tag=tagp + "sq")
            norm_rd[0] = n1
            n2 = em.n("vector", lambda e: e.tensor_scalar_max(
                out_row[:], out_row[:], 1e-12), deps=[n1], tag=tagp + "mx")
            n3 = em.n("vector", lambda e: e.reciprocal(out_row[:], out_row[:]),
                      deps=[n2], tag=tagp + "rc")
            n4 = em.n("scalar", lambda e: e.activation(
                out_row[:], out_row[:], AF.Ln), deps=[n3], tag=tagp + "ln")
            return n4

        q_parks, k_parks = [], []
        wq_nd = wload(0, ins["Wq"])
        qn_mm = fm_proj(0, wq_nd, d_qf, psN4, q_parks)
        lnq_nd = recip_norm(None, lnq, qn_mm, "rq")
        wk_nd = wload(1, ins["Wk"])
        kn_mm = fm_proj(1, wk_nd, d_kf, psN4, k_parks, extra_dep=lnq_nd)
        lnk_nd = recip_norm(None, lnk, kn_mm, "rk")

        wv_nd = wload(0, ins["Wv"], extra=[qn_mm[-1]])
        tm_parks = []
        for t in range(NT):
            for h2 in range(2):
                for wsel, dst in ((1, d_ktm), (0, d_vtm)):
                    pi = pidx[0] % 2
                    pidx[0] += 1
                    ps = psA if pi == 0 else psB
                    mms = []
                    wnd = wk_nd if wsel == 1 else wv_nd
                    for k in range(8):
                        dep = [wnd[k], xnf_nd[k]]
                        if k == 0:
                            pf = ps_free[pi]
                            dep += (pf if isinstance(pf, list) else [pf])
                        mms.append(em.n("tensor",
                            lambda e, t=t, h2=h2, k=k, ps=ps, wsel=wsel:
                            e.matmul(ps[:],
                                     xnf[k][:, 128 + 128 * t:256 + 128 * t],
                                     wbuf[wsel][k][:, 512 * h2:512 * (h2 + 1)],
                                     start=(k == 0), stop=(k == 7)),
                            deps=dep, tag=f"tm{t}_{h2}", psb=[ps]))
                    pki = pidx[0] % 4
                    ev = em.n("vector", lambda e, ps=ps, pki=pki:
                        e.tensor_copy(pkt[pki][:], ps[:]),
                        deps=[mms[-1], pk_free[pki]], tag="tmev", psb=[ps])
                    ps_free[pi] = ev
                    st = em.dma("sync", lambda e, t=t, h2=h2, dst=dst,
                                pki=pki: e.dma_start(
                        out=dst[128 * t:128 * (t + 1),
                                512 * h2:512 * (h2 + 1)],
                        in_=pkt[pki][:]), deps=[ev], tag="tmpark")
                    pk_free[pki] = st
                    tm_parks.append(st)

        wab_nd = []
        for k in range(8):
            wab_nd.append(em.dma("gpsimd", lambda e, k=k: e.dma_start(
                out=wabt[:, 64 * k:64 * (k + 1)],
                in_=ins["Wab"][128 * k:128 * (k + 1), :]), tag="wabl"))
        ab_nd = []
        prev_sig = None
        for g in range(4):
            mms = []
            for k in range(8):
                dep = [wab_nd[k], xnf_nd[k]]
                if k == 0:
                    dep.append(prev_sig)
                mms.append(em.n("tensor", lambda e, g=g, k=k: e.matmul(
                    ps64[:, 0:512], wabt[:, 64 * k:64 * (k + 1)],
                    xnf[k][:, 128 + 512 * g:128 + 512 * (g + 1)],
                    start=(k == 0), stop=(k == 7)), deps=dep, tag="abmm",
                    psb=[ps64]))
            prev_sig = em.n("scalar", lambda e, g=g: e.activation(
                ab[:, 512 * g:512 * (g + 1)], ps64[:, 0:512], AF.Sigmoid,
                bias=bab_s[:]), deps=[mms[-1]] + cld[:1], tag="absig",
                psb=[ps64])
            ab_nd.append(prev_sig)

        zzn = em.n("vector", lambda e: e.memset(zz[:], 0.0), tag="zz")
        cl0 = em.n("vector", lambda e: e.memset(cl[:], 0.0), tag="cl0")
        nk0 = em.n("vector", lambda e: e.memset(nclk[:], 0.0), tag="nk0")
        pb0 = em.n("vector", lambda e: e.memset(clpb[:], 0.0), tag="pb0")
        cq0 = em.n("vector", lambda e: e.memset(clq[:], 0.0), tag="cq0")
        la1 = em.n("vector", lambda e: e.tensor_scalar_max(
            lna[:], ab[0:16, :], 1e-30), deps=ab_nd, tag="lamax")
        la2 = em.n("scalar", lambda e: e.activation(
            lna[:], lna[:], AF.Ln), deps=[la1], tag="laln")
        lb1 = em.n("vector", lambda e: e.tensor_scalar_max(
            lnb[:], ab[32:48, :], 1e-30), deps=ab_nd, tag="lbmax")
        lb2 = em.n("scalar", lambda e: e.activation(
            lnb[:], lnb[:], AF.Ln), deps=[lb1], tag="lbln")
        cl_nd = []
        for i in range(NCH):
            cl_nd.append(em.n("vector", lambda e, i=i: e.tensor_tensor_scan(
                cl[:, CLW * i + 1:CLW * (i + 1)],
                lna[:, 128 * i:128 * (i + 1)], zz[:], 0.0, ALU.add, ALU.add),
                deps=[la2, cl0, zzn], tag=f"clscan{i}"))
        row_nd = []
        for i in range(NCH):
            a = em.n("vector", lambda e, i=i: e.scalar_tensor_tensor(
                nclk[:, CLW * i + 1:CLW * (i + 1)],
                cl[:, CLW * i + 1:CLW * (i + 1)], -1.0,
                lnk[:, 128 * i:128 * (i + 1)], ALU.mult, ALU.add),
                deps=[cl_nd[i], lnk_nd], tag=f"nclk{i}")
            b = em.n("vector", lambda e, i=i: e.tensor_add(
                clpb[:, CLW * i + 1:CLW * (i + 1)],
                cl[:, CLW * i:CLW * i + 128],
                lnb[:, 128 * i:128 * (i + 1)]), deps=[cl_nd[i], lb2, pb0],
                tag=f"clpb_a{i}")
            b2_ = em.n("vector", lambda e, i=i: e.tensor_add(
                clpb[:, CLW * i + 1:CLW * (i + 1)],
                clpb[:, CLW * i + 1:CLW * (i + 1)],
                lnk[:, 128 * i:128 * (i + 1)]), deps=[b, lnk_nd],
                tag=f"clpb_b{i}")
            cq = em.n("vector", lambda e, i=i: e.tensor_add(
                clq[:, CLW * i + 1:CLW * (i + 1)],
                cl[:, CLW * i + 1:CLW * (i + 1)],
                lnq[:, 128 * i:128 * (i + 1)]), deps=[cl_nd[i], lnq_nd, cq0],
                tag=f"clq{i}")
            row_nd += [a, b2_, cq]

        bounce = []
        for dst, src in ((d_cl, cl), (d_nclk, nclk), (d_clpb, clpb),
                         (d_clq, clq)):
            bounce.append(em.dma("sync", lambda e, dst=dst, src=src:
                e.dma_start(out=dst.ap(), in_=src[:]),
                deps=cl_nd + row_nd, tag="bounce"))
        bounce.append(em.dma("sync", lambda e: e.dma_start(
            out=d_bet.ap(), in_=ab[32:48, :]), deps=ab_nd + row_nd,
            tag="bounceb"))
        clc_nd = em.dma("sync", lambda e: e.dma_start(
            out=clc[:].rearrange("p (a b) -> p a b", a=16), in_=bass.AP(
                tensor=d_cl, offset=128,
                ap=[[1, 1], [CLTOT, 16], [CLW, 16]]),
            allow_slow_non_contiguous=True), deps=bounce[:1], tag="clc")

        for name, src in (("cl", cl), ("clpb", clpb), ("clq", clq),
                          ("nclk", nclk), ("ab", ab)):
            if name in dbg_t:
                em.dma("sync", lambda e, s=src, dd=dbg_t[name]: e.dma_start(
                    out=dd.ap(), in_=s[:]), deps=cl_nd + row_nd + ab_nd,
                    tag="dbg" + name)

        stack_pj.close()
        stack_rows.close()
        stack_x1.close()

        stack_scan = ExitStack()
        sb2 = lambda name, shape, dt: stack_scan.enter_context(
            nc.sbuf_tensor(name, list(shape), dt))
        fl = [sb2(f"fl{i}", (1, 6144), F32) for i in range(2)]
        rsl = [sb2(f"rsl{i}", (16, 384), F32) for i in range(2)]
        ktm_t = [sb2(f"ktm{i}", (128, D), BF16) for i in range(2)]
        vtm_t = [sb2(f"vtm{i}", (128, D), BF16) for i in range(2)]
        qfp = [[sb2(f"qfs{i}_{p}", (64, 1024), BF16) for p in range(2)]
               for i in range(2)]
        kfp = [[sb2(f"kfs{i}_{p}", (64, 1024), BF16) for p in range(2)]
               for i in range(2)]
        tmc = sb2("tmc", (128, 48), F32)
        kgx = sb2("kgx", (128, 16), F32)
        pbx = sb2("pbx", (128, 16), F32)
        gct = sb2("gct", (64, 16), F32)
        gram = sb2("gram", (128, 256), BF16)
        eSa = sb2("eSa", (128, 128), F32)
        eEa = sb2("eEa", (128, 128), BF16)
        LtP = [sb2(f"LtP{p}", (128, 128), BF16) for p in range(2)]
        LuP = [sb2(f"LuP{p}", (128, 128), BF16) for p in range(2)]
        MtP = [sb2(f"MtP{p}", (128, 128), BF16) for p in range(2)]
        gqP = [sb2(f"gqP{p}", (128, 128), BF16) for p in range(2)]
        RP = [sb2(f"RP{p}", (128, 128), BF16) for p in range(2)]
        XbP = [[sb2(f"XbP{p}_{i2}", (128, 128), BF16) for i2 in range(2)]
               for p in range(2)]
        TpP = [[sb2(f"TpP{p}_{i2}", (128, 128), BF16) for i2 in range(2)]
               for p in range(2)]
        UpP = [[sb2(f"UpP{p}_{i2}", (128, 128), BF16) for i2 in range(2)]
               for p in range(2)]
        qh0 = sb2("qh0", (64, 128), BF16)
        qgb = sb2("qgb", (64, 128), BF16)
        Kg_all = sb2("Kg_all", (128, 1024), BF16)
        X_all = sb2("X_all", (128, 2048), BF16)
        At_all = sb2("At_all", (64, 1024), BF16)
        G_all = sb2("G_all", (64, 2048), BF16)
        ihg = sb2("ihg", (64, 64), F32)
        pkr = [sb2(f"pkr{i}", (128, 3072), BF16) for i in range(2)]

        stack_sc = ExitStack()
        psE = stack_sc.enter_context(nc.psum_tensor("psE", [128, 512], F32))
        psG = stack_sc.enter_context(nc.psum_tensor("psG", [128, 512], F32))
        psT = stack_sc.enter_context(nc.psum_tensor("psT", [128, 128], F32))
        psU = stack_sc.enter_context(nc.psum_tensor("psU", [128, 128], F32))
        psX = stack_sc.enter_context(nc.psum_tensor("psX", [128, 128], F32))
        psO1 = stack_sc.enter_context(nc.psum_tensor("psO1", [64, 512], F32))
        psO2 = stack_sc.enter_context(nc.psum_tensor("psO2", [64, 512], F32))
        psM = stack_sc.enter_context(nc.psum_tensor("psM", [128, 512], F32))

        s1z = em.n("vector", lambda e: e.memset(s1[:], 0.0), tag="s1z")
        wf = {}
        chain1_prev = [s1z] * NH
        fl_free = [None, None]
        str_free = [None, None]
        park_nodes = {}

        for i in range(NCH):
            bi = i % 2
            l_fl = []
            for row, src in ((0, d_nclk), (1, d_clpb), (2, d_clq)):
                l_fl.append(em.dma("sync", lambda e, row=row, src=src, i=i,
                                   bi=bi: e.dma_start(
                    out=fl[bi][0:1, 2048 * row:2048 * (row + 1)].rearrange(
                        "p (h t) -> p h t", h=16),
                    in_=bass.AP(tensor=src, offset=CLW * i + 1,
                                ap=[[1, 1], [CLTOT, 16], [1, 128]])),
                    deps=[fl_free[bi]] + bounce, tag=f"flld{i}"))
            l_k = em.dma("sync", lambda e, i=i, bi=bi: e.dma_start(
                out=ktm_t[bi][:], in_=d_ktm[128 * i:128 * (i + 1), :]),
                deps=[str_free[bi], wf.get("gb_last"), tm_parks[-1]],
                tag=f"ktmld{i}")
            l_v = em.dma("sync", lambda e, i=i, bi=bi: e.dma_start(
                out=vtm_t[bi][:], in_=d_vtm[128 * i:128 * (i + 1), :]),
                deps=[str_free[bi], tm_parks[-1]], tag=f"vtmld{i}")
            l_q = None
            l_kf = None
            for p in range(2):
                l_q = em.dma("sync", lambda e, i=i, bi=bi, p=p: e.dma_start(
                    out=qfp[bi][p][:].rearrange("p2 (c t) -> p2 c t", c=8),
                    in_=bass.AP(tensor=d_qf, offset=128 * i + p * 64 * 2048,
                                ap=[[2048, 64], [128 * 2048, 8], [1, 128]])),
                    deps=[str_free[bi], q_parks[-1], l_q], tag=f"qfld{i}")
                l_kf = em.dma("sync", lambda e, i=i, bi=bi, p=p: e.dma_start(
                    out=kfp[bi][p][:].rearrange("p2 (c t) -> p2 c t", c=8),
                    in_=bass.AP(tensor=d_kf, offset=128 * i + p * 64 * 2048,
                                ap=[[2048, 64], [128 * 2048, 8], [1, 128]])),
                    deps=[str_free[bi], k_parks[-1], l_kf], tag=f"kfld{i}")

            rld_nd = []
            for ci, (dsrc, off) in enumerate(
                    ((d_nclk, CLW * i + 1), (d_clpb, CLW * i + 1),
                     (d_bet, 128 * i))):
                rld_nd.append(em.dma("sync", lambda e, ci=ci, dsrc=dsrc,
                                     off=off, bi=bi: e.dma_start(
                    out=rsl[bi][:, 128 * ci:128 * (ci + 1)],
                    in_=bass.AP(tensor=dsrc, offset=off,
                                ap=[[CLTOT if dsrc is not d_bet else T, 16],
                                    [1, 128]])),
                    deps=[wf.get(("rsl", bi)), bounce[ci + 1]
                          if ci < 2 else bounce[4], bounce[0]],
                    tag=f"rld{ci}_{i}"))
            tr_ev = []
            for ci in range(3):
                tr = em.n("tensor", lambda e, ci=ci, bi=bi: e.transpose(
                    psM[:, 16 * ci:16 * (ci + 1)],
                    rsl[bi][:, 128 * ci:128 * (ci + 1)], id16[:]),
                          deps=[rld_nd[ci], cld[4]],
                          tag=f"tr{ci}_{i}", psb=[psM])
                ev = em.n("vector", lambda e, ci=ci: e.tensor_copy(
                    tmc[:, 16 * ci:16 * (ci + 1)],
                    psM[:, 16 * ci:16 * (ci + 1)]),
                    deps=[tr, wf.get(("tmc", ci))], tag=f"trev{ci}_{i}",
                    psb=[psM])
                tr_ev.append(ev)
            wf[("rsl", bi)] = tr_ev[2]

            kg1 = em.n("tensor", lambda e, i=i: e.matmul(
                psM[:, 48:64], one1f[0:1, :],
                clc[:].rearrange("p (h i2) -> p h i2", h=16)[:, :, i:i + 1],
                start=True, stop=True),
                deps=[clc_nd, cld[3]], tag=f"kg1_{i}", psb=[psM])
            kg2 = em.n("vector", lambda e: e.tensor_add(
                kgx[:], psM[:, 48:64], tmc[:, 0:16]),
                deps=[kg1, tr_ev[0], wf.get("kgx")], tag="kg2", psb=[psM])
            kg3 = em.n("scalar", lambda e: e.activation(
                kgx[:], kgx[:], AF.Exp), deps=[kg2], tag="kg3")
            pb1 = em.n("scalar", lambda e: e.activation(
                pbx[:], tmc[:, 16:32], AF.Exp),
                deps=[tr_ev[1], wf.get("pbx")], tag="pb1")
            gc1 = em.n("tensor", lambda e, i=i: e.matmul(
                psM[0:64, 64:80], one1f[0:1, 0:64],
                clc[:].rearrange("p (h i2) -> p h i2", h=16)[:, :, i:i + 1],
                start=True, stop=True),
                deps=[clc_nd, cld[3]], tag=f"gc1_{i}", psb=[psM])
            gc2 = em.n("scalar", lambda e: e.activation(
                gct[:], psM[0:64, 64:80], AF.Exp),
                deps=[gc1, wf.get("gct")], tag="gc2", psb=[psM])

            kg_nodes, x_nodes, at_nodes, g_nodes = [], [], [], []
            pp = park_nodes.get(i - 1, [None] * 4)

            def phaseA(h):
                par = h % 2
                jj, hp = h // 2, h % 2
                bc = []
                for bidx in range(3):
                    bc.append(em.n("tensor", lambda e, bidx=bidx,
                                   h=h, bi=bi: e.matmul(
                        psE[:, 128 * bidx:128 * (bidx + 1)],
                        one1f[0:1, :],
                        fl[bi][0:1, 2048 * bidx + 128 * h:
                               2048 * bidx + 128 * (h + 1)],
                        start=True, stop=True),
                        deps=l_fl + [wf.get(("psE", bidx)), cld[3]],
                        tag=f"bc{bidx}_{i}_{h}", psb=[psE]))
                gkk = em.n("tensor", lambda e, jj=jj, hp=hp, bi=bi: e.matmul(
                    psG[:, 0:128],
                    kfp[bi][hp][:, 128 * jj:128 * (jj + 1)],
                    kfp[bi][hp][:, 128 * jj:128 * (jj + 1)],
                    start=True, stop=True),
                    deps=[l_kf, wf.get("psG")], tag="gkk", psb=[psG])
                gkq = em.n("tensor", lambda e, jj=jj, hp=hp, bi=bi: e.matmul(
                    psG[:, 128:256],
                    kfp[bi][hp][:, 128 * jj:128 * (jj + 1)],
                    qfp[bi][hp][:, 128 * jj:128 * (jj + 1)],
                    start=True, stop=True), deps=[l_kf, l_q, gkk],
                    tag="gkq", psb=[psG])
                grm = em.n("vector", lambda e: e.tensor_copy(
                    gram[:], psG[:, 0:256]),
                    deps=[gkq, wf.get("gram")], tag="grmev", psb=[psG])
                wf["psG"] = grm
                e_nodes = []
                prev_exp = wf.get("eEa")
                for which, bidx, col, msk, dst in (
                        ("lt", 1, 0, m_up, LtP[par]),
                        ("lu", 0, 16, m_lo, LuP[par]),
                        ("mt", 2, 0, m_ui, MtP[par])):
                    st_ = em.n("vector", lambda e, bidx=bidx, col=col, h=h,
                               msk=msk: e.scalar_tensor_tensor(
                        eSa[:], psE[:, 128 * bidx:128 * (bidx + 1)],
                        tmc[:, col + h:col + h + 1], msk[:],
                        ALU.add, ALU.add),
                        deps=[bc[bidx], tr_ev[0], tr_ev[1], prev_exp,
                              cld[0]], tag=f"es_{which}", psb=[psE])
                    ex_ = em.n("scalar", lambda e: e.activation(
                        eEa[:], eSa[:], AF.Exp),
                        deps=[st_, wf.get("eEa_rd")], tag=f"ex_{which}")
                    gcol = 0 if which != "mt" else 128
                    mu_ = em.n("gpsimd", lambda e, dst=dst, gcol=gcol:
                        e.tensor_tensor(dst[:], eEa[:],
                                        gram[:, gcol:gcol + 128], ALU.mult),
                        deps=[ex_, grm,
                              wf.get(("LtLu_rd", par)) if which != "mt"
                              else wf.get(("Mt_rd", par))],
                        tag=f"em_{which}")
                    prev_exp = ex_
                    wf["eEa_rd"] = mu_
                    e_nodes.append((st_, ex_, mu_))
                wf[("psE", 0)] = e_nodes[1][0]
                wf[("psE", 1)] = e_nodes[0][0]
                wf[("psE", 2)] = e_nodes[2][0]
                wf["eEa"] = e_nodes[2][1]
                gq_ = em.n("scalar", lambda e, par=par: e.activation(
                    gqP[par][:], psE[:, 256:384], AF.Exp),
                    deps=[bc[2], wf.get(("gqE", par))], tag="gqE", psb=[psE])
                wf[("psE", 2)] = gq_
                rv = em.n("vector", lambda e, h=h, bi=bi, par=par:
                    e.tensor_scalar_mul(
                        RP[par][:, 0:64], vtm_t[bi][:, 64 * h:64 * (h + 1)],
                        tmc[:, 32 + h:33 + h]),
                    deps=[l_v, tr_ev[2], wf.get(("R_s", par))], tag="rv")
                rk = em.n("vector", lambda e, h=h, bi=bi, par=par:
                    e.tensor_scalar_mul(
                        RP[par][:, 64:128], ktm_t[bi][:, 64 * h:64 * (h + 1)],
                        pbx[:, h:h + 1]), deps=[l_k, pb1], tag="rk")
                kg_ = em.n("vector", lambda e, h=h, bi=bi: e.tensor_scalar_mul(
                    Kg_all[:, 64 * h:64 * (h + 1)],
                    ktm_t[bi][:, 64 * h:64 * (h + 1)],
                    kgx[:, h:h + 1]),
                    deps=[l_k, kg3, pp[0], wf.get(("Kg", h))], tag="kg")
                kg_nodes.append(kg_)
                return dict(lt=e_nodes[0][2], lu=e_nodes[1][2],
                            mt=e_nodes[2][2], gq=gq_, rv=rv, rk=rk, kg=kg_,
                            lastbc=bc[2])

            xfin_map = {}

            def solve_gen(h, A):
                par = h % 2
                hT, cT = (psT, 0) if par == 0 else (psG, 256)
                hU, cU = (psU, 0) if par == 0 else (psG, 384)
                hX, cX = (psX, 0) if par == 0 else (psE, 384)
                Lt, Lu, R = LtP[par], LuP[par], RP[par]
                a0 = em.n("tensor", lambda e, hX=hX, cX=cX, Lt=Lt, R=R:
                    e.matmul(hX[:, cX:cX + 128], Lt[:], R[:],
                             start=True, stop=True),
                    deps=[A["lt"], A["rv"], A["rk"]], tag="a0", psb=[hX])
                yield
                x0 = em.n("vector", lambda e, hX=hX, cX=cX, R=R, par=par:
                    e.tensor_sub(XbP[par][0][:], R[:], hX[:, cX:cX + 128]),
                    deps=[a0, wf.get(("Xb", par, 0))], tag="x0", psb=[hX])
                wf[("R_s", par)] = x0
                yield
                tprev, uprev = Lt, Lu
                tprev_nd, uprev_nd = A["lt"], A["lu"]
                xcur, xcur_nd = XbP[par][0], x0
                xadd = None
                for k in range(1, 7):
                    tsq = em.n("tensor", lambda e, tprev=tprev, uprev=uprev,
                               hT=hT, cT=cT: e.matmul(
                        hT[:, cT:cT + 128], uprev[:], tprev[:],
                        start=True, stop=True),
                        deps=[tprev_nd, uprev_nd], tag="tsq", psb=[hT])
                    yield
                    tnew = TpP[par][k % 2]
                    tev = em.n("scalar", lambda e, tnew=tnew, hT=hT, cT=cT:
                        e.copy(tnew[:], hT[:, cT:cT + 128]),
                        deps=[tsq, wf.get(("Tp", par, k % 2))], tag="tev",
                        psb=[hT])
                    yield
                    if k < 6:
                        usq = em.n("tensor", lambda e, tprev=tprev,
                                   uprev=uprev, hU=hU, cU=cU: e.matmul(
                            hU[:, cU:cU + 128], tprev[:], uprev[:],
                            start=True, stop=True),
                            deps=[tprev_nd, uprev_nd], tag="usq", psb=[hU])
                        yield
                        unew = UpP[par][k % 2]
                        uev = em.n("scalar", lambda e, unew=unew, hU=hU,
                                   cU=cU: e.copy(unew[:], hU[:, cU:cU + 128]),
                            deps=[usq, wf.get(("Up", par, k % 2))], tag="uev",
                            psb=[hU])
                        yield
                        if k == 1:
                            wf[("LtLu_rd", par)] = usq
                        wf[("Up", par, k % 2)] = None
                        uprev, uprev_nd = unew, uev
                    app = em.n("tensor", lambda e, tnew=tnew, xcur=xcur,
                               hX=hX, cX=cX: e.matmul(
                        hX[:, cX:cX + 128], tnew[:], xcur[:],
                        start=True, stop=True),
                        deps=[tev, xcur_nd], tag="app", psb=[hX])
                    yield
                    if k < 6:
                        xnew = XbP[par][(k + 1) % 2]
                        xadd = em.n("vector", lambda e, xnew=xnew, xcur=xcur,
                                    hX=hX, cX=cX: e.tensor_add(
                            xnew[:], xcur[:], hX[:, cX:cX + 128]),
                            deps=[app, wf.get(("Xb", par, (k + 1) % 2))],
                            tag="xadd", psb=[hX])
                        wf[("Xb", par, (k + 1) % 2)] = None
                        xcur, xcur_nd = xnew, xadd
                    else:
                        xadd = em.n("vector", lambda e, h=h, xcur=xcur,
                                    hX=hX, cX=cX: e.tensor_add(
                            X_all[:, 128 * h:128 * (h + 1)],
                            xcur[:], hX[:, cX:cX + 128]),
                            deps=[app, pp[1], wf.get(("X", h))], tag="xfin",
                            psb=[hX])
                    wf[("Tp", par, k % 2)] = app
                    tprev, tprev_nd = tnew, tev
                    yield
                xfin_map[h] = xadd
                x_nodes.append(xadd)

            def phaseC(h, A):
                par = h % 2
                jj, r0 = h // 2, 64 * (h % 2)
                xfin = xfin_map[h]
                po1 = em.n("tensor", lambda e, h=h, par=par: e.matmul(
                    psO1[0:64, 0:128], X_all[:, 128 * h:128 * h + 64],
                    MtP[par][:], start=True, stop=True),
                    deps=[xfin, A["mt"]], tag="po1", psb=[psO1])
                po2 = em.n("tensor", lambda e, h=h, par=par: e.matmul(
                    psO2[0:64, 0:128],
                    X_all[:, 128 * h + 64:128 * (h + 1)],
                    MtP[par][:], start=True, stop=True),
                    deps=[xfin, A["mt"]], tag="po2", psb=[psO2])
                wf[("Mt_rd", par)] = po2
                yev = em.n("scalar", lambda e, jj=jj, r0=r0, i=i: e.copy(
                    yf[jj][r0:r0 + 64, 128 * i:128 * (i + 1)],
                    psO1[0:64, 0:128]), deps=[po1], tag="yev", psb=[psO1])
                wf[("yfw", jj, r0, i)] = yev
                qcp = em.n("vector", lambda e, jj=jj, hp=h % 2, bi=bi:
                    e.tensor_copy(qh0[:],
                                  qfp[bi][hp][:, 128 * jj:128 * (jj + 1)]),
                    deps=[l_q, wf.get("qh0")], tag="qcp")
                qg_ = em.n("vector", lambda e, h=h, par=par: e.tensor_mul(
                    qgb[:], qh0[:], gqP[par][0:64, :]),
                    deps=[qcp, A["gq"], wf.get("qgb")], tag="qgb")
                wf["qh0"] = qg_
                wf[("gqE", par)] = qg_
                gb = em.n("vector", lambda e, h=h: e.scalar_tensor_tensor(
                    G_all[0:64, 128 * h:128 * (h + 1)], psO2[0:64, 0:128],
                    -1.0, qgb[:], ALU.mult, ALU.add),
                    deps=[po2, qg_, pp[3], wf.get(("G", h))], tag="gb",
                    psb=[psO2])
                wf["gb_last"] = gb
                wf["qgb"] = gb
                g_nodes.append(gb)
                pc = em.n("tensor", lambda e, h=h: e.matmul(
                    psO2[0:64, 128:192],
                    X_all[:, 128 * h + 64:128 * (h + 1)],
                    Kg_all[:, 64 * h:64 * (h + 1)], start=True, stop=True),
                    deps=[xfin, A["kg"]], tag="pc", psb=[psO2])
                ih = em.n("vector", lambda e, h=h: e.tensor_scalar_mul(
                    ihg[:], id64f[:], gct[:, h:h + 1]),
                    deps=[gc2, cld[5], wf.get("ihg")], tag="ih")
                at = em.n("vector", lambda e, h=h: e.tensor_sub(
                    At_all[0:64, 64 * h:64 * (h + 1)], ihg[:],
                    psO2[0:64, 128:192]),
                    deps=[pc, ih, pp[2], wf.get(("At", h))], tag="at",
                    psb=[psO2])
                wf["ihg"] = at
                at_nodes.append(at)
                c1a = em.n("tensor", lambda e, h=h: e.matmul(
                    psM[0:64, 128:192], At_all[0:64, 64 * h:64 * (h + 1)],
                    s1[:, 64 * h:64 * (h + 1)], start=True, stop=False),
                    deps=[at, chain1_prev[h]], tag="c1a", psb=[psM])
                c1b = em.n("tensor", lambda e, h=h: e.matmul(
                    psM[0:64, 128:192], Kg_all[:, 64 * h:64 * (h + 1)],
                    X_all[:, 128 * h:128 * h + 64], start=False, stop=True),
                    deps=[c1a, A["kg"], xfin], tag="c1b", psb=[psM])
                c1e = em.n("vector", lambda e, h=h: e.tensor_copy(
                    s1[:, 64 * h:64 * (h + 1)], psM[0:64, 128:192]),
                    deps=[c1b], tag="c1e", psb=[psM])
                chain1_prev[h] = c1e
                wf[("Kg", h)] = None
                wf[("X", h)] = None
                wf[("At", h)] = None
                wf[("G", h)] = None

            for hq in range(NH // 2):
                h0, h1 = 2 * hq, 2 * hq + 1
                A0 = phaseA(h0)
                A1 = phaseA(h1)
                gens = [solve_gen(h0, A0), solve_gen(h1, A1)]
                while gens:
                    for g_ in list(gens):
                        try:
                            next(g_)
                        except StopIteration:
                            gens.remove(g_)
                phaseC(h0, A0)
                phaseC(h1, A1)
            pks = []
            pks.append(em.dma("sync", lambda e, i=i: e.dma_start(
                out=bass.AP(tensor=d_pk[i], offset=0,
                            ap=[[6144, 128], [384, 16], [1, 64]]),
                in_=Kg_all[:].rearrange("p (h w) -> p h w", h=16)),
                deps=kg_nodes, tag=f"pkKg{i}"))
            pks.append(em.dma("sync", lambda e, i=i: e.dma_start(
                out=bass.AP(tensor=d_pk[i], offset=64,
                            ap=[[6144, 128], [384, 16], [1, 128]]),
                in_=X_all[:].rearrange("p (h w) -> p h w", h=16)),
                deps=x_nodes, tag=f"pkX{i}"))
            pks.append(em.dma("sync", lambda e, i=i: e.dma_start(
                out=bass.AP(tensor=d_pk[i], offset=192,
                            ap=[[6144, 64], [384, 16], [1, 64]]),
                in_=At_all[:].rearrange("p (h w) -> p h w", h=16)),
                deps=at_nodes, tag=f"pkAt{i}"))
            pks.append(em.dma("sync", lambda e, i=i: e.dma_start(
                out=bass.AP(tensor=d_pk[i], offset=256,
                            ap=[[6144, 64], [384, 16], [1, 128]]),
                in_=G_all[:].rearrange("p (h w) -> p h w", h=16)),
                deps=g_nodes, tag=f"pkG{i}"))
            park_nodes[i] = pks
            fl_free[bi] = A1["lastbc"]
            str_free[bi] = chain1_prev[NH - 1]

        sxw = em.dma("sync", lambda e: e.dma_start(
            out=d_sx.ap(), in_=s1[:]), deps=[chain1_prev[h] for h in
                                             range(NH)], tag="sxw")
        agc = em.n("gpsimd", lambda e: e.collective_compute(
            "AllGather", ALU.bypass,
            replica_groups=[[0, 1], [2, 3], [4, 5], [6, 7]],
            ins=[d_sx.ap().opt()], outs=[d_sxo.ap().opt()]),
            w=1, tag="agc", deps=[sxw], coll=True)
        sgr = em.dma("sync", lambda e: e.dma_start(
            out=sG[:], in_=d_sxo[0:64, :]), deps=[agc], tag="sgr")
        mc1 = em.n("tensor", lambda e: e.matmul(
            psM[0:64, 192:193], one1f[0:1, 0:64], smk[:],
            start=True, stop=True), deps=[cld[3], cld[10]],
            tag="mc1", psb=[psM])
        mc2 = em.n("vector", lambda e: e.tensor_copy(
            mcol[:], psM[0:64, 192:193]), deps=[mc1], tag="mc2", psb=[psM])
        s2i = []
        for h in range(NH):
            s2i.append(em.n("vector", lambda e, h=h: e.tensor_scalar_mul(
                s2[:, 64 * h:64 * (h + 1)], sG[:, 64 * h:64 * (h + 1)],
                mcol[:]), deps=[sgr, mc2], tag=f"s2i{h}"))

        pkr_free = [None, None]
        chain2_prev = list(s2i)
        y_upd = {(jj, i): None for jj in range(8) for i in range(NCH)}
        yev_all = []
        for i in range(NCH):
            lpk_h = [None, None]
            for hh in range(2):
                la_ = em.dma("sync", lambda e, i=i, hh=hh: e.dma_start(
                    out=pkr[hh][0:64, :],
                    in_=d_pk[i][0:64, 3072 * hh:3072 * (hh + 1)]),
                    deps=[pkr_free[hh]] + park_nodes[i], tag=f"lpkA{i}_{hh}")
                lpk_h[hh] = em.dma("sync", lambda e, i=i, hh=hh: e.dma_start(
                    out=pkr[hh][64:128, :].rearrange(
                        "p (h w) -> p h w", h=8)[:, :, 0:192],
                    in_=bass.AP(tensor=d_pk[i], offset=64 * 6144 + 3072 * hh,
                                ap=[[6144, 64], [384, 8], [1, 192]])),
                    deps=[pkr_free[hh], la_] + park_nodes[i],
                    tag=f"lpkB{i}_{hh}")
            for h in range(NH):
                jj, r0 = h // 2, 64 * (h % 2)
                bi = h // 8
                lpk = lpk_h[bi]
                hm = h % 8
                psA_ = psX if h % 2 == 0 else psT
                psB_ = psM if h % 2 == 0 else psU
                oc = em.n("tensor", lambda e, h=h, bi=bi, hm=hm,
                          psA_=psA_: e.matmul(
                    psA_[0:64, 0:128],
                    s2[:, 64 * h:64 * (h + 1)],
                    pkr[bi][0:64, 384 * hm + 256:384 * (hm + 1)],
                    start=True, stop=True),
                    deps=[lpk, chain2_prev[h]], tag="c2oc", psb=[psA_])
                ya = em.n("vector", lambda e, jj=jj, r0=r0, i=i,
                          psA_=psA_: e.tensor_add(
                    yf[jj][r0:r0 + 64, 128 * i:128 * (i + 1)],
                    yf[jj][r0:r0 + 64, 128 * i:128 * (i + 1)],
                    psA_[0:64, 0:128]),
                    deps=[oc, wf.get(("yfw", jj, r0, i))], tag="c2ya",
                    psb=[psA_])
                yev_all.append(ya)
                c2a = em.n("tensor", lambda e, h=h, bi=bi, hm=hm,
                           psB_=psB_: e.matmul(
                    psB_[0:64, 0:64],
                    pkr[bi][0:64, 384 * hm + 192:384 * hm + 256],
                    s2[:, 64 * h:64 * (h + 1)], start=True, stop=False),
                    deps=[lpk, chain2_prev[h]], tag="c2a", psb=[psB_])
                c2b = em.n("tensor", lambda e, h=h, bi=bi, hm=hm,
                           psB_=psB_: e.matmul(
                    psB_[0:64, 0:64],
                    pkr[bi][:, 384 * hm:384 * hm + 64],
                    pkr[bi][:, 384 * hm + 64:384 * hm + 128],
                    start=False, stop=True), deps=[c2a], tag="c2b",
                    psb=[psB_])
                c2e = em.n("vector", lambda e, h=h, psB_=psB_: e.tensor_copy(
                    s2[:, 64 * h:64 * (h + 1)], psB_[0:64, 0:64]),
                    deps=[c2b, oc, c2a], tag="c2e", psb=[psB_])
                chain2_prev[h] = c2e
                pkr_free[bi] = c2e

        if "y" in dbg_t:
            for j in range(8):
                yev_all.append(em.dma("gpsimd", lambda e, j=j: e.dma_start(
                    out=dbg_t["y"][128 * j:128 * (j + 1), :], in_=yf[j][:]),
                    deps=list(yev_all), tag="dbgy"))
        if "s1" in dbg_t:
            em.dma("gpsimd", lambda e: e.dma_start(
                out=dbg_t["s1"].ap(), in_=s1[:]),
                deps=[chain1_prev[h] for h in range(NH)], tag="dbgs1")
        stack_sc.close()

        stack_scan.close()

        xnf2 = [sb(f"xnf2_{j}", (128, 2176), BF16) for j in range(8)]
        xnf2_nd = []
        for j in range(8):
            xnf2_nd.append(em.dma("sync", lambda e, j=j: e.dma_start_transpose(
                xnf2[j][:], d_xn[:, 128 * j:128 * (j + 1)]),
                deps=xn_st, tag=f"xnf2_{j}"))
        t_ones128b = nc.inline_tensor(np.ones((128, 1), ml_dtypes.bfloat16),
                                      "t_ones128b")
        t_ones1b = nc.inline_tensor(np.ones((1, 128), ml_dtypes.bfloat16),
                                    "t_ones1b")
        one128b = sb("one128b", (128, 1), BF16)
        one1b = sb("one1b", (1, 128), BF16)
        ones1f2 = sb("ones1f2", (1, 128), F32)
        c128b = em.dma("sync", lambda e: e.dma_start(
            out=one128b[:], in_=t_ones128b.ap()), tag="c128b")
        c1b = em.dma("sync", lambda e: e.dma_start(
            out=one1b[:], in_=t_ones1b.ap()), tag="c1b")
        sqY = sb("sqY", (128, T), BF16)
        rms_r = sb("rms_r", (1, T), F32)
        rms_b = sb("rms_b", (1, T), BF16)
        gsil = sb("gsil", (128, 512), BF16)
        t1g = sb("t1g", (128, 512), BF16)
        pkf = [sb(f"pkf{i}", (128, 512), F32) for i in range(2)]

        stack_p5 = ExitStack()
        psY4 = [stack_p5.enter_context(nc.psum_tensor(f"psY{i}", [1, 512], F32))
                for i in range(4)]
        psC = stack_p5.enter_context(nc.psum_tensor("psC", [128, 512], F32))
        psD = stack_p5.enter_context(nc.psum_tensor("psD", [128, 512], F32))
        psR = stack_p5.enter_context(nc.psum_tensor("psR", [128, 512], F32))

        yacc = sb("yacc", (1, T), F32)
        ysq_mm = []
        yps_wf = {}
        ylast = {}
        prev_sq = None
        for j in range(8):
            sq = em.n("scalar", lambda e, j=j: e.activation(
                sqY[:], yf[j][:], AF.Square),
                deps=yev_all + [prev_sq], tag=f"ysq{j}")
            for g in range(4):
                off = (j * 4 + g) % 4
                psy = psY4[off]
                nm = em.n("tensor", lambda e, g=g, psy=psy: e.matmul(
                    psy[:], one128b[:],
                    sqY[:, 512 * g:512 * (g + 1)],
                    start=True, stop=True),
                    deps=[sq, c128b, yps_wf.get(off)], tag=f"ynm{j}_{g}",
                    psb=[psy])
                if j == 0:
                    na = em.n("vector", lambda e, g=g, psy=psy: e.tensor_copy(
                        yacc[0:1, 512 * g:512 * (g + 1)],
                        psy[:]), deps=[nm], tag="yna0", psb=[psy])
                else:
                    na = em.n("vector", lambda e, g=g, psy=psy: e.tensor_add(
                        yacc[0:1, 512 * g:512 * (g + 1)],
                        yacc[0:1, 512 * g:512 * (g + 1)],
                        psy[:]),
                        deps=[nm, ylast.get(g)], tag="yna", psb=[psy])
                ylast[g] = na
                yps_wf[off] = na
                if j == 7:
                    ysq_mm.append(na)
            prev_sq = nm
        rr1 = em.n("vector", lambda e: e.tensor_scalar(
            rms_r[:], yacc[0:1, :], 1.0 / D, EPS_LN, ALU.mult, ALU.add),
            deps=ysq_mm, tag="rr1")
        rr2 = em.n("scalar", lambda e: e.activation(
            rms_r[:], rms_r[:], AF.Sqrt), deps=[rr1], tag="rr2")
        rr3 = em.n("vector", lambda e: e.reciprocal(rms_r[:], rms_r[:]),
                   deps=[rr2], tag="rr3")
        rr4 = em.n("vector", lambda e: e.tensor_copy(rms_b[:], rms_r[:]),
                   deps=[rr3], tag="rr4")

        wg_nd = wload(1, ins["Wg0"], extra=[tm_parks[-1]])
        z_nd = {}
        prev_ps = {0: None, 1: None}
        for j in range(8):
            for g in range(4):
                mms = []
                for k in range(8):
                    dep = [wg_nd[k], xnf2_nd[k]]
                    if k == 0:
                        dep.append(prev_ps[0])
                    mms.append(em.n("tensor", lambda e, j=j, g=g, k=k:
                        e.matmul(psC[:],
                                 wbuf[1][k][:, 128 * j:128 * (j + 1)],
                                 xnf2[k][:, 128 + 512 * g:128 + 512 * (g + 1)],
                                 start=(k == 0), stop=(k == 7)),
                        deps=dep, tag=f"gmm{j}_{g}_{k}", psb=[psC]))
                sil = em.n("scalar", lambda e: e.activation(
                    gsil[:], psC[:], AF.Silu), deps=[mms[-1],
                    z_nd.get((j, g - 1) if g else (j - 1, 3))], tag="gsil",
                    psb=[psC])
                prev_ps[0] = sil
                rb = em.n("tensor", lambda e, g=g: e.matmul(
                    psR[:], one1b[:], rms_b[0:1, 512 * g:512 * (g + 1)],
                    start=True, stop=True),
                    deps=[rr4, c1b], tag="rbz", psb=[psR])
                tz = em.n("vector", lambda e, j=j: e.scalar_tensor_tensor(
                    t1g[:], gsil[:], gn_s[:, j:j + 1], psR[:],
                    ALU.mult, ALU.mult), deps=[sil, rb, cld[7],
                    wf.get("t1g")], tag="tz", psb=[psR])
                zz_ = em.n("vector", lambda e, j=j, g=g: e.tensor_mul(
                    yf[j][:, 512 * g:512 * (g + 1)],
                    yf[j][:, 512 * g:512 * (g + 1)], t1g[:]),
                    deps=[tz] + yev_all[-1:], tag="zmul")
                wf["t1g"] = zz_
                z_nd[(j, g)] = zz_
        if "z" in dbg_t:
            for j in range(8):
                em.dma("gpsimd", lambda e, j=j: e.dma_start(
                    out=dbg_t["z"][128 * j:128 * (j + 1), :], in_=yf[j][:]),
                    deps=[z_nd[(j, g)] for g in range(4)], tag="dbgz")

        wo_nd = wload(0, ins["Wo0"], extra=[tm_parks[-1], z_nd[(7, 3)]])
        attn_st = []
        pkf_free = [None, None]
        prev_wops = None
        for t in range(NT):
            for h2 in range(2):
                mms = []
                for k in range(8):
                    dep = [wo_nd[k], z_nd[(k, t // 4)]]
                    if k == 0:
                        dep.append(prev_wops)
                    mms.append(em.n("tensor", lambda e, t=t, h2=h2, k=k:
                        e.matmul(psD[:],
                                 yf[k][:, 128 * t:128 * (t + 1)],
                                 wbuf[0][k][:, 512 * h2:512 * (h2 + 1)],
                                 start=(k == 0), stop=(k == 7)),
                        deps=dep, tag=f"womm{t}_{h2}_{k}", psb=[psD]))
                fi = (2 * t + h2) % 2
                ev = em.n("vector", lambda e, fi=fi: e.tensor_copy(
                    pkf[fi][:], psD[:]), deps=[mms[-1], pkf_free[fi]],
                    tag="woev", psb=[psD])
                prev_wops = ev
                st = em.dma("sync", lambda e, t=t, h2=h2, fi=fi: e.dma_start(
                    out=d_attn[128 * t:128 * (t + 1),
                               512 * h2:512 * (h2 + 1)], in_=pkf[fi][:]),
                    deps=[ev], tag="wopark")
                pkf_free[fi] = st
                attn_st.append(st)

        mur1m = sb("mur1m", (128, 8), F32)
        muk1m = sb("muk1m", (128, 8), F32)
        n1m_r = em.n("vector", lambda e: e.tensor_scalar(
            mur1m[:], mur_s[:], -1.0, 1.0, ALU.mult, ALU.add),
            deps=[cld[8]], tag="mur1m")
        n1m_k = em.n("vector", lambda e: e.tensor_scalar(
            muk1m[:], muk_s[:], -1.0, 1.0, ALU.mult, ALU.add),
            deps=[cld[9]], tag="muk1m")
        xr = [sb(f"xr{j}", (128, T), BF16) for j in range(8)]
        xr_t1 = sb("xr_t1", (128, T), BF16)

        def mix(mu_s, mu1m, mdep, extra_deps):
            nds = []
            for j in range(8):
                a = em.n("vector", lambda e, j=j, mu_s=mu_s: e.tensor_scalar_mul(
                    xr_t1[:], xnf2[j][:, 128:2176], mu_s[:, j:j + 1]),
                    deps=[xnf_nd[j], mdep, wf.get("xr_t1")] + extra_deps,
                    tag=f"mixa{j}")
                b = em.n("vector", lambda e, j=j, mu1m=mu1m:
                    e.scalar_tensor_tensor(
                        xr[j][:], xnf2[j][:, 127:2175], mu1m[:, j:j + 1],
                        xr_t1[:], ALU.mult, ALU.add),
                    deps=[a, wf.get(("xrw", j))], tag=f"mixb{j}")
                wf["xr_t1"] = b
                nds.append(b)
            return nds

        xr_nd = mix(mur_s, mur1m, n1m_r, [])
        wr_nd = wload(1, ins["WrT"], extra=[z_nd[(7, 3)], prev_wops])
        r_st = {}
        prev_rps = None
        for t in range(NT):
            for h2 in range(2):
                mms = []
                for k in range(8):
                    dep = [wr_nd[k], xr_nd[k]]
                    if k == 0:
                        dep.append(prev_rps)
                    mms.append(em.n("tensor", lambda e, t=t, h2=h2, k=k:
                        e.matmul(psC[:],
                                 xr[k][:, 128 * t:128 * (t + 1)],
                                 wbuf[1][k][:, 512 * h2:512 * (h2 + 1)],
                                 start=(k == 0), stop=(k == 7)),
                        deps=dep, tag=f"rmm{t}_{h2}", psb=[psC]))
                pki = (2 * t + h2) % 4
                sg = em.n("scalar", lambda e, pki=pki: e.activation(
                    pkt[pki][:], psC[:], AF.Sigmoid),
                    deps=[mms[-1], pk_free[pki]], tag="rsig", psb=[psC])
                prev_rps = sg
                st = em.dma("sync", lambda e, t=t, h2=h2, pki=pki:
                    e.dma_start(out=d_rtm[128 * t:128 * (t + 1),
                                          512 * h2:512 * (h2 + 1)],
                                in_=pkt[pki][:]), deps=[sg], tag="rpark")
                pk_free[pki] = st
                r_st[(t, h2)] = st

        xk_nd = mix(muk_s, muk1m, n1m_k, [prev_rps])
        wk2_nd = wload(0, ins["WkT"], extra=[prev_wops])
        rel_s = sb("rel_s", (128, 512), BF16)
        ka_nd = {}
        prev_kps = None
        for j in range(8):
            for g in range(4):
                mms = []
                for k in range(8):
                    dep = [wk2_nd[k], xk_nd[k]]
                    if k == 0:
                        dep.append(prev_kps)
                    mms.append(em.n("tensor", lambda e, j=j, g=g, k=k:
                        e.matmul(psC[:],
                                 wbuf[0][k][:, 128 * j:128 * (j + 1)],
                                 xr[k][:, 512 * g:512 * (g + 1)],
                                 start=(k == 0), stop=(k == 7)),
                        deps=dep, tag=f"kamm{j}_{g}", psb=[psC]))
                rl = em.n("vector", lambda e: e.tensor_relu(
                    rel_s[:], psC[:]), deps=[mms[-1], wf.get("rel_s")],
                    tag="karelu", psb=[psC])
                prev_kps = rl
                kq = em.n("scalar", lambda e, j=j, g=g: e.activation(
                    xnf2[j][:, 512 * g:512 * (g + 1)], rel_s[:], AF.Square),
                    deps=[rl, wf.get(("kaw", j, g))], tag="kasq")
                wf["rel_s"] = kq
                ka_nd[(j, g)] = kq
        wv2_nd = wload(1, ins["WvT"], extra=[prev_rps])
        rld = sb("rld", (128, 512), BF16)
        f0_st = []
        prev_fps = None
        for t in range(NT):
            for h2 in range(2):
                lr = em.dma("sync", lambda e, t=t, h2=h2: e.dma_start(
                    out=rld[:], in_=d_rtm[128 * t:128 * (t + 1),
                                          512 * h2:512 * (h2 + 1)]),
                    deps=[r_st[(t, h2)], wf.get("rld")], tag="rldld")
                mms = []
                for k in range(8):
                    dep = [wv2_nd[k], ka_nd[(k, t // 4)]]
                    if k == 0:
                        dep.append(prev_fps)
                    mms.append(em.n("tensor", lambda e, t=t, h2=h2, k=k:
                        e.matmul(psD[:],
                                 xnf2[k][:, 128 * t:128 * (t + 1)],
                                 wbuf[1][k][:, 512 * h2:512 * (h2 + 1)],
                                 start=(k == 0), stop=(k == 7)),
                        deps=dep, tag=f"fmm{t}_{h2}", psb=[psD]))
                fi = (2 * t + h2) % 2
                fz = em.n("vector", lambda e, fi=fi: e.tensor_mul(
                    pkf[fi][:], psD[:], rld[:]),
                    deps=[mms[-1], lr, pkf_free[fi]], tag="f0mul",
                    psb=[psD])
                wf["rld"] = fz
                prev_fps = fz
                st = em.dma("sync", lambda e, t=t, h2=h2, fi=fi: e.dma_start(
                    out=d_f0[128 * t:128 * (t + 1),
                             512 * h2:512 * (h2 + 1)], in_=pkf[fi][:]),
                    deps=[fz], tag="f0park")
                pkf_free[fi] = st
                f0_st.append(st)

        res_a = sb("res_a", (128, D), F32)
        res_b = sb("res_b", (128, D), F32)
        for t in range(NT):
            la = em.dma("sync", lambda e, t=t: e.dma_start(
                out=res_a[:], in_=ins["xctx"][128 + 128 * t:256 + 128 * t, :]),
                deps=[wf.get("res_st")], tag="resx")
            lb = em.dma("sync", lambda e, t=t: e.dma_start(
                out=res_b[:], in_=d_attn[128 * t:128 * (t + 1), :]),
                deps=attn_st + [wf.get("res_st")], tag="resat")
            s1_ = em.n("vector", lambda e: e.tensor_add(
                res_a[:], res_a[:], res_b[:]), deps=[la, lb], tag="resadd1")
            lb2 = em.dma("sync", lambda e, t=t: e.dma_start(
                out=res_b[:], in_=d_f0[128 * t:128 * (t + 1), :]),
                deps=f0_st + [s1_], tag="resf0")
            s2_ = em.n("vector", lambda e: e.tensor_add(
                res_a[:], res_a[:], res_b[:]), deps=[lb2], tag="resadd2")
            st = em.dma("sync", lambda e, t=t: e.dma_start(
                out=x1_out[128 * t:128 * (t + 1), :], in_=res_a[:]),
                deps=[s2_], tag="x1st")
            wf["res_st"] = st

        stack_p5.close()
        if ctx is None:
            em.build(block, stack, limit=limit)
    if ctx is not None:
        return x1_out
    return nc



L1_INPUTS = [
    ("x1in", (T, D), F32),
    ("memWT", (D, D), F32), ("WiT", (D, D), F32), ("Wl1T", (D, D), F32),
    ("Wl2T", (D, D), F32), ("WoT1", (D, D), F32),
    ("WgAT", (D, D), F32), ("WgBT", (D, D), F32),
    ("W1T", (D, 4096), F32), ("W2T", (4096, D), F32),
    ("PT16", (D, 128), F32),
    ("bi_c", (128, 8), F32), ("bl1_c", (128, 8), F32),
    ("bl2_c", (128, 8), F32), ("memb_c", (128, 8), F32),
    ("b1_c", (128, 32), F32), ("rows6", (6, D), F32),
    ("bg_c", (128, 8), F32), ("bo_c", (128, 8), F32),
    ("b2f_c", (128, 8), F32),
    ("fg_lr", (1, 2), F32), ("gradmask", (1, 2176), F32),
    ("winmask0", (128, 144), F32), ("band", (128, 144), F32),
    ("gwT", (D, 8), F32), ("w1e", (8, D, 4096), F32),
    ("w2e", (8, 4096, D), F32), ("b1e_c", (8, 128, 32), F32),
    ("b2e_r", (8, D), F32), ("prior_mask", (8, 1), F32),
]
NKV = 2176
GRPS = [(0, 512), (512, 512), (1024, 512), (1536, 512), (2048, 128)]


def build_l1(dbg=None, ctx=None, x1_src=None):
    dbg = dbg or []
    if ctx is None:
        nc = bass.Bass("TRN2", target_bir_lowering=False)
    else:
        nc0, em, block = ctx
        nc = NCProxy(nc0, "q")
        em.set_barrier()
    ins = {n: nc.dram_tensor(n, list(s), d, kind="ExternalInput")
           for n, s, d in L1_INPUTS
           if ctx is None or n != "x1in"}
    if ctx is not None:
        ins["x1in"] = x1_src
    yout = nc.dram_tensor("yout", [T, D], F32, kind="ExternalOutput")
    ident16 = nc.inline_tensor(np.eye(16, dtype=np.float32), "ident16")
    t_id128b = nc.inline_tensor(np.eye(128, dtype=ml_dtypes.bfloat16),
                                "t_id128b")
    t_ones1b = nc.inline_tensor(np.ones((1, 128), ml_dtypes.bfloat16),
                                "t_ones1b")
    t_ones128b = nc.inline_tensor(np.ones((128, 1), ml_dtypes.bfloat16),
                                  "t_ones128b")
    t_iota8 = nc.inline_tensor(
        np.tile(np.arange(8, dtype=np.float32), (128, 1)), "t_iota8")
    t_ones1f2 = nc.inline_tensor(np.ones((1, 128), np.float32), "t_ones1f2")

    d_hx = nc.dram_tensor("d_hx", [16, D], F32)
    d_hxo = nc.dram_tensor("d_hxo", [32, D], F32)
    d_xe = nc.dram_tensor("d_xe", [NKV, D], BF16)
    d_x1s = nc.dram_tensor("d_x1s", [T, D], F32)
    d_x1sb = nc.dram_tensor("d_x1sb", [T, D], BF16)
    d_kp = nc.dram_tensor("d_kp", [D, NKV], BF16)
    d_hdd = nc.dram_tensor("d_hdd", [D, NKV], BF16)
    d_diff = nc.dram_tensor("d_diff", [D, NKV], BF16)
    d_grad = nc.dram_tensor("d_grad", [D, D], F32)
    d_grado = nc.dram_tensor("d_grado", [D, D], F32)
    d_s = nc.dram_tensor("d_s", [D, T], BF16)
    d_x2 = nc.dram_tensor("d_x2", [T, D], F32)
    d_x2b = nc.dram_tensor("d_x2b", [T, D], BF16)
    d_h = nc.dram_tensor("d_h", [4096, T], BF16)
    d_ffb = nc.dram_tensor("d_ffb", [D, T], BF16)
    d_a1 = nc.dram_tensor("d_a1", [T, D], F32)
    d_disp = nc.dram_tensor("d_disp", [8 * CAPL, D], BF16)
    d_eout = nc.dram_tensor("d_eout", [8 * CAPL, D], BF16)
    d_cnt = nc.dram_tensor("d_cnt", [8, 1], F32)
    d_cnto = nc.dram_tensor("d_cnto", [64, 1], F32)
    dbg_t = {}
    for name in dbg:
        shp = {"x1s": (T, D), "hdd": (D, NKV), "grad": (D, D),
               "mo": (D, T), "x2": (T, D), "a1": (T, D),
               "moe": (T, D), "idx": (T, 1)}[name]
        dt_ = U32 if name == "idx" else F32
        dbg_t[name] = nc.dram_tensor("dbg_" + name, list(shp), dt_,
                                     kind="ExternalOutput")

    if ctx is None:
        em = Em(nc)
    stack = ExitStack()
    with stack:
        sb = lambda name, shape, dt: stack.enter_context(
            nc.sbuf_tensor(name, list(shape), dt))
        if ctx is None:
            block = stack.enter_context(nc.Block())

        wm0 = sb("wm0", (128, 144), F32)
        wmb = sb("wmb", (128, 144), F32)
        rows6f = sb("rows6f", (1, D), BF16)
        gB = [sb(f"gB{r}", (128, D), BF16) for r in range(6)]
        gmB = sb("gmB", (128, NKV), BF16)
        gmrow = sb("gmrow", (1, NKV), BF16)
        bi_s = sb("bi_s", (128, 8), F32)
        bl1_s = sb("bl1_s", (128, 8), F32)
        bl2_s = sb("bl2_s", (128, 8), F32)
        memb_s = sb("memb_s", (128, 8), F32)
        dfb_s = sb("dfb_s", (128, 8), F32)
        bg_s = sb("bg_s", (128, 8), F32)
        bo_s = sb("bo_s", (128, 8), F32)
        b2f_s = sb("b2f_s", (128, 8), F32)
        b1_s = sb("b1_s", (128, 32), F32)
        fglr = sb("fglr", (1, 2), F32)
        fgc = sb("fgc", (128, 2), F32)
        pri_s = sb("pri_s", (8, 1), F32)
        one1b = sb("one1b", (1, 128), BF16)
        ones1f2 = sb("ones1f2", (1, 128), F32)
        one128b = sb("one128b", (128, 1), BF16)
        id16 = sb("id16", (16, 16), F32)
        id128b = sb("id128b", (128, 128), BF16)
        iota8 = sb("iota8", (128, 8), F32)
        eps12 = sb("eps12", (128, 1), F32)
        wbuf = [[sb(f"wbuf{i}_{k}", (128, D), BF16) for k in range(8)]
                for i in range(2)]
        pkt = [sb(f"pkt{i}", (128, 512), BF16) for i in range(4)]
        xa = [sb(f"xa{i}", (128, D), F32) for i in range(2)]
        xb_ = [sb(f"xb{i}", (128, D), F32) for i in range(2)]
        xc = [sb(f"xc{i}", (128, D), BF16) for i in range(2)]
        st6 = sb("st6", (128, 12), F32)
        mv = sb("mv", (128, 2), F32)
        rstd = sb("rstd", (128, 1), F32)

        ps1 = stack.enter_context(nc.psum_tensor("ps1", [128, 512], F32))
        ps2 = stack.enter_context(nc.psum_tensor("ps2", [128, 512], F32))
        ps3 = stack.enter_context(nc.psum_tensor("ps3", [128, 512], F32))
        ps4 = stack.enter_context(nc.psum_tensor("ps4", [128, 512], F32))
        ps5 = stack.enter_context(nc.psum_tensor("ps5", [128, 512], F32))
        ps6 = stack.enter_context(nc.psum_tensor("ps6", [128, 512], F32))

        wf = {}
        cl0 = []
        for tile, srcd in ((wm0, ins["winmask0"]), (wmb, ins["band"]),
                           (bi_s, ins["bi_c"]), (bl1_s, ins["bl1_c"]),
                           (bl2_s, ins["bl2_c"]), (memb_s, ins["memb_c"]),
                           (bg_s, ins["bg_c"]), (bo_s, ins["bo_c"]),
                           (b2f_s, ins["b2f_c"]), (b1_s, ins["b1_c"]),
                           (fglr, ins["fg_lr"]), (pri_s, ins["prior_mask"]),
                           (one1b, t_ones1b), (one128b, t_ones128b),
                           (id16, ident16), (id128b, t_id128b),
                           (iota8, t_iota8), (ones1f2, t_ones1f2)):
            cl0.append(em.dma("sync", lambda e, t=tile, s=srcd: e.dma_start(
                out=t[:], in_=s.ap()), tag="cl0"))
        r6l = None
        gml = em.dma("gpsimd", lambda e: e.dma_start(
            out=gmrow[:], in_=ins["gradmask"].ap()), tag="gml")
        ep0 = em.n("vector", lambda e: e.memset(eps12[:], 1e-12), tag="ep0")
        dfb = em.n("vector", lambda e: e.tensor_sub(
            dfb_s[:], memb_s[:], bi_s[:]), deps=[cl0[5], cl0[2]], tag="dfb")
        fg1 = em.n("tensor", lambda e: e.matmul(
            ps3[:, 0:2], ones1f2[:], fglr[0:1, :],
            start=True, stop=True), deps=[cl0[10], cl0[-1]], tag="fg1",
            psb=[ps3])
        fg2 = em.n("vector", lambda e: e.tensor_scalar(
            fgc[:, 0:1], ps3[:, 0:1], -1.0, 1.0, ALU.mult, ALU.add),
            deps=[fg1], tag="fg2", psb=[ps3])
        fg3 = em.n("vector", lambda e: e.tensor_copy(
            fgc[:, 1:2], ps3[:, 1:2]), deps=[fg1], tag="fg3", psb=[ps3])
        gB_nd = []
        prev_r6 = None
        for r in range(6):
            r6l = em.dma("gpsimd", lambda e, r=r: e.dma_start(
                out=rows6f[:], in_=ins["rows6"][r:r + 1, :]),
                deps=[prev_r6], tag=f"r6l{r}")
            mm1 = em.n("tensor", lambda e, r=r: e.matmul(
                ps3[:, 0:512], one1b[:], rows6f[0:1, 0:512],
                start=True, stop=True), deps=[r6l, cl0[12], fg2, fg3],
                tag=f"gB{r}a", psb=[ps3])
            c1 = em.n("vector", lambda e, r=r: e.tensor_copy(
                gB[r][:, 0:512], ps3[:, 0:512]), deps=[mm1], tag="gBc",
                psb=[ps3])
            mm2 = em.n("tensor", lambda e, r=r: e.matmul(
                ps3[:, 0:512], one1b[:], rows6f[0:1, 512:1024],
                start=True, stop=True), deps=[c1], tag=f"gB{r}b", psb=[ps3])
            c2 = em.n("vector", lambda e, r=r: e.tensor_copy(
                gB[r][:, 512:1024], ps3[:, 0:512]), deps=[mm2], tag="gBc2",
                psb=[ps3])
            prev_r6 = mm2
            gB_nd.append(c2)
        gm_nd = []
        prev = gB_nd[-1]
        for off, gsz in GRPS:
            mm1 = em.n("tensor", lambda e, off=off, gsz=gsz: e.matmul(
                ps3[:, 0:gsz], one1b[:], gmrow[0:1, off:off + gsz],
                start=True, stop=True), deps=[gml, cl0[12], prev],
                tag="gmBa", psb=[ps3])
            c1 = em.n("vector", lambda e, off=off, gsz=gsz: e.tensor_copy(
                gmB[:, off:off + gsz], ps3[:, 0:gsz]), deps=[mm1],
                tag="gmBc", psb=[ps3])
            gm_nd.append(c1)
            prev = c1

        hxw = em.dma("sync", lambda e: e.dma_start(
            out=d_hx.ap(), in_=ins["x1in"][T - 16:T, :]), tag="hxw")
        agh = em.n("gpsimd", lambda e: e.collective_compute(
            "AllGather", ALU.bypass,
            replica_groups=[[0, 1], [2, 3], [4, 5], [6, 7]],
            ins=[d_hx.ap().opt()], outs=[d_hxo.ap().opt()]),
            w=1, tag="agh", deps=[hxw], coll=True)
        hl = em.dma("sync", lambda e: e.dma_start(
            out=xa[0][0:16, :], in_=d_hxo[0:16, :]), deps=[agh], tag="hl")
        hb1 = em.n("vector", lambda e: e.bn_stats(
            st6[0:16, 0:6], xa[0][0:16, 0:512]), deps=[hl], tag="hb1")
        hb2 = em.n("vector", lambda e: e.bn_stats(
            st6[0:16, 6:12], xa[0][0:16, 512:1024]), deps=[hl], tag="hb2")
        hb3 = em.n("vector", lambda e: e.bn_aggr(
            mv[0:16, :], st6[0:16, :]), deps=[hb1, hb2], tag="hb3")
        epsln = sb("epsln", (128, 1), F32)
        epl = em.n("vector", lambda e: e.memset(epsln[:], EPS_LN), tag="epl")
        hsq = em.n("scalar", lambda e: e.activation(
            rstd[0:16, :], mv[0:16, 1:2], AF.Sqrt, bias=epsln[0:16, :]),
            deps=[hb3, epl], tag="hsq")
        hrc = em.n("vector", lambda e: e.reciprocal(
            rstd[0:16, :], rstd[0:16, :]), deps=[hsq], tag="hrc")
        hxn = em.n("vector", lambda e: e.tensor_scalar(
            xc[0][0:16, :], xa[0][0:16, :], mv[0:16, 0:1], rstd[0:16, :],
            ALU.subtract, ALU.mult), deps=[hrc], tag="hxn")
        hst = em.dma("sync", lambda e: e.dma_start(
            out=d_xe[0:16, :], in_=xc[0][0:16, :]), deps=[hxn], tag="hst")
        xe_st = [hst]
        zxe0 = em.n("vector", lambda e: e.memset(xc[1][:], 0.0), tag="zxe0")
        zxe1 = em.dma("sync", lambda e: e.dma_start(
            out=d_xe[2064:2176, :], in_=xc[1][0:112, :]), deps=[zxe0],
            tag="zxe1")
        xe_st.append(zxe1)
        prevs = {("xc", 1): zxe1}
        for t in range(NT):
            bi = t % 2
            ld = em.dma("sync", lambda e, t=t, bi=bi: e.dma_start(
                out=xa[bi][:], in_=ins["x1in"][128 * t:128 * (t + 1), :]),
                deps=[prevs.get(("xa", bi)), hxn if t < 2 else None],
                tag=f"xld{t}")
            b1 = em.n("vector", lambda e, bi=bi: e.bn_stats(
                st6[:, 0:6], xa[bi][:, 0:512]), deps=[ld, prevs.get("ln")],
                tag="b1")
            b2 = em.n("vector", lambda e, bi=bi: e.bn_stats(
                st6[:, 6:12], xa[bi][:, 512:1024]), deps=[ld], tag="b2")
            b3 = em.n("vector", lambda e: e.bn_aggr(mv[:], st6[:]),
                      deps=[b1, b2], tag="b3")
            sq = em.n("scalar", lambda e: e.activation(
                rstd[:], mv[:, 1:2], AF.Sqrt, bias=epsln[:]),
                deps=[b3, epl], tag="sq")
            rc = em.n("vector", lambda e: e.reciprocal(rstd[:], rstd[:]),
                      deps=[sq], tag="rc")
            xn = em.n("vector", lambda e, bi=bi: e.tensor_scalar(
                xc[bi][:], xa[bi][:], mv[:, 0:1], rstd[:],
                ALU.subtract, ALU.mult), deps=[rc, prevs.get(("xc", bi))],
                tag="xn")
            prevs["ln"] = xn
            prevs[("xa", bi)] = xn
            st_ = em.dma("sync", lambda e, t=t, bi=bi: e.dma_start(
                out=d_xe[16 + 128 * t:16 + 128 * (t + 1), :],
                in_=xc[bi][:]), deps=[xn], tag=f"xest{t}")
            prevs[("xc", bi)] = st_
            xe_st.append(st_)

        stack_b = ExitStack()
        sbB = lambda n, s, d: stack_b.enter_context(
            nc.sbuf_tensor(n, list(s), d))
        xef = [sbB(f"xef{j}", (128, NKV), BF16) for j in range(8)]
        xef_nd = []
        for j in range(8):
            xef_nd.append(em.dma("sync", lambda e, j=j: e.dma_start_transpose(
                xef[j][:], d_xe[:, 128 * j:128 * (j + 1)]),
                deps=xe_st, tag=f"xef{j}"))
        ka_t = [sbB(f"ka{i}", (128, D), BF16) for i in range(2)]
        kb_t = [sbB(f"kb{i}", (16, D), BF16) for i in range(2)]
        praw = sbB("praw", (128, 144), F32)
        pexp = sbB("pexp", (128, 144), BF16)
        pat = sbB("pat", (128, 128), BF16)
        pbt = sbB("pbt", (16, 128), BF16)
        nmx = sbB("nmx", (128, 1), F32)
        dnm = sbB("dnm", (128, 1), F32)
        x1s_st = []
        for t in range(NT):
            bi = t % 2
            mms = []
            for k in range(8):
                mms.append(em.n("tensor", lambda e, t=t, k=k: e.matmul(
                    ps1[:, 0:144],
                    xef[k][:, 16 + 128 * t:16 + 128 * (t + 1)],
                    xef[k][:, 128 * t:128 * t + 144],
                    start=(k == 0), stop=(k == 7)),
                    deps=[xef_nd[k], wf.get("ps1w")], tag=f"ws{t}_{k}",
                    psb=[ps1]))
            msk = wm0 if t == 0 else wmb
            sc = em.n("vector", lambda e, msk=msk: e.scalar_tensor_tensor(
                praw[:], ps1[:, 0:144], 0.03125, msk[:], ALU.mult, ALU.add),
                deps=[mms[-1], cl0[0], cl0[1], wf.get("praw")], tag="wmask",
                psb=[ps1])
            wf["ps1w"] = sc
            mx = em.n("vector", lambda e: e.reduce_max(
                nmx[:], praw[:], axis=AXL.X, negate=True),
                deps=[sc], tag="wmax")
            ex = em.n("scalar", lambda e: e.activation(
                pexp[:], praw[:], AF.Exp, bias=nmx[:], accum_out=dnm[:]),
                deps=[mx, wf.get("pexp")], tag="wexp")
            wf["praw"] = ex
            rcp = em.n("vector", lambda e: e.reciprocal(dnm[:], dnm[:]),
                       deps=[ex], tag="wrcp")
            tp1 = em.n("tensor", lambda e: e.transpose(
                ps2[:, 0:64].bitcast(BF16), pexp[:, 0:128], id128b[:]),
                deps=[ex, wf.get("ps2w")], tag="wtp1", psb=[ps2])
            ev1 = em.n("vector", lambda e: e.tensor_copy(
                pat[:], ps2[:, 0:64].bitcast(BF16)), deps=[tp1],
                tag="wev1", psb=[ps2])
            tp2 = em.n("tensor", lambda e: e.transpose(
                ps2[0:16, 64:128].bitcast(BF16), pexp[:, 128:144],
                id128b[:]), deps=[ev1], tag="wtp2", psb=[ps2])
            ev2 = em.n("vector", lambda e: e.tensor_copy(
                pbt[:], ps2[0:16, 64:128].bitcast(BF16)), deps=[tp2],
                tag="wev2", psb=[ps2])
            wf["pexp"] = tp2
            wf["ps2w"] = ev2
            lka = em.dma("sync", lambda e, t=t, bi=bi: e.dma_start(
                out=ka_t[bi][:], in_=d_xe[128 * t:128 * (t + 1), :]),
                deps=[xe_st[-1], wf.get(("ka", bi))], tag="lka")
            lkb = em.dma("sync", lambda e, t=t, bi=bi: e.dma_start(
                out=kb_t[bi][0:16, :],
                in_=d_xe[128 * t + 128:128 * t + 144, :]),
                deps=[xe_st[-1], wf.get(("ka", bi))], tag="lkb")
            avn = []
            for h2 in range(2):
                m1 = em.n("tensor", lambda e, h2=h2, bi=bi: e.matmul(
                    ps4[:, 0:512], pat[:],
                    ka_t[bi][:, 512 * h2:512 * (h2 + 1)],
                    start=True, stop=False),
                    deps=[ev1, lka, wf.get("ps4w")], tag="av1", psb=[ps4])
                m2 = em.n("tensor", lambda e, h2=h2, bi=bi: e.matmul(
                    ps4[:, 0:512], pbt[:],
                    kb_t[bi][0:16, 512 * h2:512 * (h2 + 1)],
                    start=False, stop=True),
                    deps=[ev2, lkb, m1], tag="av2", psb=[ps4])
                a1_ = em.n("vector", lambda e, h2=h2, bi=bi:
                    e.tensor_scalar_mul(
                        xb_[bi][:, 512 * h2:512 * (h2 + 1)], ps4[:, 0:512],
                        dnm[:]), deps=[m2, rcp, wf.get(("xb", bi))],
                    tag="avs", psb=[ps4])
                wf["ps4w"] = a1_
                avn.append(a1_)
            wf[("ka", bi)] = avn[-1]
            lx = em.dma("sync", lambda e, t=t, bi=bi: e.dma_start(
                out=xa[bi][:], in_=ins["x1in"][128 * t:128 * (t + 1), :]),
                deps=[wf.get(("xaB", bi))], tag="lx1")
            ad = em.n("vector", lambda e, bi=bi: e.tensor_add(
                xb_[bi][:], xb_[bi][:], xa[bi][:]), deps=[lx] + avn,
                tag="wadd")
            wf[("xaB", bi)] = ad
            b1 = em.n("vector", lambda e, bi=bi: e.bn_stats(
                st6[:, 0:6], xb_[bi][:, 0:512]), deps=[ad, wf.get("lnb")],
                tag="wb1")
            b2 = em.n("vector", lambda e, bi=bi: e.bn_stats(
                st6[:, 6:12], xb_[bi][:, 512:1024]), deps=[ad], tag="wb2")
            b3 = em.n("vector", lambda e: e.bn_aggr(mv[:], st6[:]),
                      deps=[b1, b2], tag="wb3")
            sq = em.n("scalar", lambda e: e.activation(
                rstd[:], mv[:, 1:2], AF.Sqrt, bias=epsln[:]),
                deps=[b3, epl], tag="wsq")
            rc = em.n("vector", lambda e: e.reciprocal(rstd[:], rstd[:]),
                      deps=[sq], tag="wrc")
            n1 = em.n("vector", lambda e, bi=bi: e.tensor_scalar(
                xb_[bi][:], xb_[bi][:], mv[:, 0:1], rstd[:],
                ALU.subtract, ALU.mult), deps=[rc], tag="wn1")
            n2 = em.n("vector", lambda e, bi=bi: e.tensor_mul(
                xb_[bi][:], xb_[bi][:], gB[0][:]), deps=[n1, gB_nd[0]],
                tag="wn2")
            n3 = em.n("vector", lambda e, bi=bi: e.tensor_add(
                xb_[bi][:], xb_[bi][:], gB[1][:]), deps=[n2, gB_nd[1]],
                tag="wn3")
            wf["lnb"] = n3
            c_ = em.n("vector", lambda e, bi=bi: e.tensor_copy(
                xc[bi][:], xb_[bi][:]), deps=[n3, wf.get(("xcB", bi))],
                tag="wcb")
            s1_ = em.dma("sync", lambda e, t=t, bi=bi: e.dma_start(
                out=d_x1s[128 * t:128 * (t + 1), :], in_=xb_[bi][:]),
                deps=[n3], tag="x1sst")
            s2_ = em.dma("sync", lambda e, t=t, bi=bi: e.dma_start(
                out=d_x1sb[128 * t:128 * (t + 1), :], in_=xc[bi][:]),
                deps=[c_], tag="x1sbst")
            wf[("xb", bi)] = s1_
            wf[("xcB", bi)] = s2_
            x1s_st.append(s1_)
            x1s_st.append(s2_)
        if "x1s" in dbg_t:
            em.dma("gpsimd", lambda e: e.dma_start(
                out=dbg_t["x1s"].ap(), in_=d_x1s.ap()), deps=x1s_st,
                tag="dbgx1s")

        pfm = [sbB(f"pfm{j}", (128, 128), BF16) for j in range(8)]
        nrm = sbB("nrm", (1, NKV), F32)
        recb = sbB("recb", (1, NKV), BF16)
        sqv = sbB("sqv", (128, 512), BF16)
        knt = sbB("knt", (128, 512), BF16)

        pf_nd = []
        for j in range(8):
            pf_nd.append(em.dma("gpsimd", lambda e, j=j: e.dma_start(
                out=pfm[j][:], in_=ins["PT16"][128 * j:128 * (j + 1), :]),
                tag="pfl"))

        def kv_src(k, off, gsz):
            if off >= T:
                return pfm[k][:, 0:128]
            return xef[k][:, 16 + off:16 + off + gsz]

        nz = em.n("vector", lambda e: e.memset(nrm[:], 0.0), tag="nz")
        nrm_nd = [nz]
        prev_sq = None
        for k in range(8):
            for off, gsz in GRPS:
                sq_ = em.n("scalar", lambda e, k=k, off=off, gsz=gsz:
                    e.activation(sqv[:, 0:gsz], kv_src(k, off, gsz),
                                 AF.Square),
                    deps=[xef_nd[k], pf_nd[k], prev_sq], tag="ksq")
                nm_ = em.n("tensor", lambda e, gsz=gsz: e.matmul(
                    ps3[0:1, 0:gsz], one128b[:], sqv[:, 0:gsz],
                    start=True, stop=True),
                    deps=[sq_, cl0[13], wf.get("ps3n")], tag="knm",
                    psb=[ps3])
                ad_ = em.n("vector", lambda e, off=off, gsz=gsz: e.tensor_add(
                    nrm[0:1, off:off + gsz], nrm[0:1, off:off + gsz],
                    ps3[0:1, 0:gsz]), deps=[nm_, nrm_nd[-1]], tag="kna",
                    psb=[ps3])
                wf["ps3n"] = ad_
                nrm_nd.append(ad_)
                prev_sq = nm_
        kr1 = em.n("scalar", lambda e: e.activation(
            nrm[:], nrm[:], AF.Sqrt), deps=nrm_nd[-1:], tag="kr1")
        kr2 = em.n("vector", lambda e: e.tensor_scalar_max(
            nrm[:], nrm[:], 1e-12), deps=[kr1], tag="kr2")
        kr3 = em.n("vector", lambda e: e.reciprocal(nrm[:], nrm[:]),
                   deps=[kr2], tag="kr3")
        kr4 = em.n("vector", lambda e: e.tensor_copy(recb[:], nrm[:]),
                   deps=[kr3], tag="kr4")
        d_kn = nc.dram_tensor("d_kn", [D, NKV], BF16)
        kn_nd = []
        for k in range(8):
            for off, gsz in GRPS:
                rb_ = em.n("tensor", lambda e, off=off, gsz=gsz: e.matmul(
                    ps3[:, 0:gsz], one1b[:], recb[0:1, off:off + gsz],
                    start=True, stop=True),
                    deps=[kr4, cl0[12], wf.get("ps3n")], tag="krb",
                    psb=[ps3])
                kn_ = em.n("vector", lambda e, k=k, off=off, gsz=gsz:
                    e.tensor_mul(knt[:, 0:gsz], kv_src(k, off, gsz),
                                 ps3[:, 0:gsz]),
                    deps=[rb_, wf.get("knt")], tag="knmul", psb=[ps3])
                wf["ps3n"] = kn_
                pk_ = em.dma("sync", lambda e, k=k, off=off, gsz=gsz:
                    e.dma_start(out=d_kn[128 * k:128 * (k + 1),
                                         off:off + gsz],
                                in_=knt[:, 0:gsz]), deps=[kn_], tag="knpk")
                wf["knt"] = pk_
                kn_nd.append(pk_)
        stack_b.close()

        csrc = [sb(f"csrc{k}", (128, 512), BF16) for k in range(8)]
        csc2 = [sb(f"csc2{k}", (128, 512), BF16) for k in range(8)]

        def wloadL(idx, srcd, dep=(), fd=D):
            nds = []
            for k in range(8):
                nds.append(em.dma("gpsimd", lambda e, k=k, idx=idx,
                                  srcd=srcd, fd=fd: e.dma_start(
                    out=wbuf[idx][k][:, 0:fd],
                    in_=srcd[128 * k:128 * (k + 1), :]),
                    deps=[wf.get(("wl", idx, k))] + list(dep),
                    tag=f"wlL{idx}_{k}"))
                wf[("wl", idx, k)] = nds[-1]
            return nds

        def fm_chain2(wnd, widx, src_d, dst_d, func, bias_s, tagp,
                      deps_extra=(), tok_only=False, wtiles=None,
                      src2_d=None, wnd2=None, widx2=None, sig=AF.Identity,
                      src_park_of=None):
            grps = GRPS[:4] if tok_only else GRPS
            last = []
            for gi, (off, gsz) in enumerate(grps):
                l_nd = []
                for k in range(8):
                    pdep = ([src_park_of(gi, k)] if src_park_of is not None
                            else [])
                    if isinstance(src_d, tuple):
                        srcT = src_d[1]
                        l_nd.append(em.dma("sync", lambda e, k=k, off=off,
                                           gsz=gsz, srcT=srcT:
                            e.dma_start_transpose(
                                csrc[k][:, 0:gsz],
                                srcT[off:off + gsz,
                                     128 * k:128 * (k + 1)]),
                            deps=[wf.get(("cs", k))] + list(deps_extra)
                                 + pdep,
                            tag=f"{tagp}lT"))
                    else:
                        l_nd.append(em.dma("sync", lambda e, k=k, off=off,
                                           gsz=gsz, src_d=src_d:
                            e.dma_start(
                                out=csrc[k][:, 0:gsz],
                                in_=src_d[128 * k:128 * (k + 1),
                                          off:off + gsz]),
                            deps=[wf.get(("cs", k))] + list(deps_extra)
                                 + pdep,
                            tag=f"{tagp}l"))
                l2_nd = []
                if src2_d is not None:
                    for k in range(8):
                        l2_nd.append(em.dma("sync", lambda e, k=k, off=off,
                                            gsz=gsz: e.dma_start(
                            out=csc2[k][:, 0:gsz],
                            in_=src2_d[128 * k:128 * (k + 1),
                                       off:off + gsz]),
                            deps=[wf.get(("cs2", k))] + list(deps_extra),
                            tag=f"{tagp}l2"))
                for j in range(8):
                    ps = ps1 if j % 2 == 0 else ps2
                    mms = []
                    for k in range(8):
                        wtile = (wtiles[k] if wtiles is not None
                                 else wbuf[widx][k])
                        dep = [wnd[k], l_nd[k]]
                        if k == 0:
                            dep.append(wf.get((tagp, id(ps))))
                        mms.append(em.n("tensor",
                            lambda e, j=j, k=k, gsz=gsz, ps=ps,
                            wtile=wtile: e.matmul(
                                ps[:, 0:gsz],
                                wtile[:, 128 * j:128 * (j + 1)],
                                csrc[k][:, 0:gsz],
                                start=(k == 0),
                                stop=(k == 7 and src2_d is None)),
                            deps=dep, tag=f"{tagp}mm", psb=[ps]))
                    if src2_d is not None:
                        for k in range(8):
                            mms.append(em.n("tensor",
                                lambda e, j=j, k=k, gsz=gsz, ps=ps,
                                widx2=widx2: e.matmul(
                                    ps[:, 0:gsz],
                                    wbuf[widx2][k][:, 128 * j:128 * (j + 1)],
                                    csc2[k][:, 0:gsz],
                                    start=False, stop=(k == 7)),
                                deps=[wnd2[k], l2_nd[k]], tag=f"{tagp}m2",
                                psb=[ps]))
                    pki = j % 4
                    ev = em.n("scalar", lambda e, j=j, gsz=gsz, ps=ps,
                              func=func, pki=pki: e.activation(
                        pkt[pki][:, 0:gsz], ps[:, 0:gsz], func,
                        bias=bias_s[:, j:j + 1]),
                        deps=[mms[-1], wf.get(("pk", pki))],
                        tag=f"{tagp}ev", psb=[ps])
                    wf[(tagp, id(ps))] = ev
                    pk_ = em.dma("sync", lambda e, j=j, off=off, gsz=gsz,
                                 dst_d=dst_d, pki=pki: e.dma_start(
                        out=dst_d[128 * j:128 * (j + 1), off:off + gsz],
                        in_=pkt[pki][:, 0:gsz]), deps=[ev], tag=f"{tagp}pk")
                    wf[("pk", pki)] = pk_
                    last.append(pk_)
                for k in range(8):
                    wf[("cs", k)] = mms[7] if src2_d is None else mms[7]
                    if src2_d is not None:
                        wf[("cs2", k)] = mms[-1]
            return last

        d_h1 = nc.dram_tensor("d_h1", [D, NKV], BF16)
        d_pred = nc.dram_tensor("d_pred", [D, NKV], BF16)
        d_mem = nc.dram_tensor("d_mem", [D, NKV], BF16)
        d_mo = nc.dram_tensor("d_mo", [D, T], BF16)
        d_gate = nc.dram_tensor("d_gate", [D, T], BF16)
        wi_nd = wloadL(0, ins["WiT"])
        kp_nd = fm_chain2(wi_nd, 0, d_kn, d_kp, AF.Identity, bi_s, "kp",
                          src_park_of=lambda gi, k: kn_nd[5 * k + gi])
        wl1_nd = wloadL(1, ins["Wl1T"])
        h1_nd = fm_chain2(wl1_nd, 1, d_kp, d_h1, AF.Silu, bl1_s, "h1",
                          src_park_of=lambda gi, k: kp_nd[8 * gi + k])
        wl2_nd = wloadL(0, ins["Wl2T"], dep=kp_nd[-1:])
        hdd_nd = fm_chain2(wl2_nd, 0, d_h1, d_hdd, AF.Silu, bl2_s, "hd",
                           src_park_of=lambda gi, k: h1_nd[8 * gi + k])
        wm_nd = wloadL(1, ins["memWT"], dep=h1_nd[-1:])
        pred_nd = fm_chain2(wm_nd, 1, d_hdd, d_pred, AF.Identity, memb_s,
                            "pr", deps_extra=[cl0[5]],
                            src_park_of=lambda gi, k: hdd_nd[8 * gi + k])
        diff_nd = []
        prev = None
        for j in range(8):
            for off, gsz in GRPS:
                gi_ = GRPS.index((off, gsz))
                l1_ = em.dma("sync", lambda e, j=j, off=off, gsz=gsz:
                    e.dma_start(out=csc2[0][:, 0:gsz],
                                in_=d_pred[128 * j:128 * (j + 1),
                                           off:off + gsz]),
                    deps=[pred_nd[8 * gi_ + j], wf.get("df0")], tag="dfl1")
                l2_ = em.dma("sync", lambda e, j=j, off=off, gsz=gsz:
                    e.dma_start(out=csc2[1][:, 0:gsz],
                                in_=d_kp[128 * j:128 * (j + 1),
                                         off:off + gsz]),
                    deps=[kp_nd[8 * gi_ + j], wf.get("df1")], tag="dfl2")
                d1 = em.n("vector", lambda e, gsz=gsz: e.tensor_sub(
                    csc2[0][:, 0:gsz], csc2[0][:, 0:gsz],
                    csc2[1][:, 0:gsz]), deps=[l1_, l2_, prev], tag="dfs")
                wf["df1"] = d1
                d2 = em.n("vector", lambda e, off=off, gsz=gsz:
                    e.tensor_mul(csc2[0][:, 0:gsz], csc2[0][:, 0:gsz],
                                 gmB[:, off:off + gsz]),
                    deps=[d1, gm_nd[-1]], tag="dfm")
                pk_ = em.dma("sync", lambda e, j=j, off=off, gsz=gsz:
                    e.dma_start(out=d_diff[128 * j:128 * (j + 1),
                                           off:off + gsz],
                                in_=csc2[0][:, 0:gsz]),
                    deps=[d2], tag="dfpk")
                wf["df0"] = pk_
                diff_nd.append(pk_)
                prev = d2
        if "hdd" in dbg_t:
            em.dma("gpsimd", lambda e: e.dma_start(
                out=dbg_t["hdd"].ap(), in_=d_hdd.ap()),
                deps=hdd_nd[-1:], tag="dbghdd")

        stack_g = ExitStack()
        sbG = lambda n, s, d: stack_g.enter_context(
            nc.sbuf_tensor(n, list(s), d))
        hddT = [sbG(f"hddT{t}", (128, D), BF16) for t in range(17)]
        difT = [sbG(f"difT{t}", (128, D), BF16) for t in range(17)]
        wnew = [sbG(f"wnew{k}", (128, D), BF16) for k in range(8)]
        grl = sbG("grl", (128, 512), F32)
        gev = [sbG(f"gev{i}", (128, 512), F32) for i in range(2)]
        ht_nd, dt_nd = [], []
        for t in range(17):
            rows = 128
            ht_nd.append(em.dma("sync", lambda e, t=t, rows=rows:
                e.dma_start_transpose(hddT[t][0:rows, :],
                                      d_hdd[:, 128 * t:128 * t + rows]),
                deps=hdd_nd[-1:], tag=f"htl{t}"))
            dt_nd.append(em.dma("sync", lambda e, t=t, rows=rows:
                e.dma_start_transpose(difT[t][0:rows, :],
                                      d_diff[:, 128 * t:128 * t + rows]),
                deps=diff_nd[-1:], tag=f"dtl{t}"))
        gw_st = []
        for jj in range(8):
            for hf in range(2):
                ps = ps5 if (jj * 2 + hf) % 2 == 0 else ps6
                bi4 = (jj * 2 + hf) % 2
                mms = []
                for t in range(17):
                    rows = 128
                    dep = [ht_nd[t], dt_nd[t]]
                    if t == 0:
                        dep.append(wf.get(("gps", id(ps))))
                    mms.append(em.n("tensor", lambda e, jj=jj, hf=hf, t=t,
                                    rows=rows, ps=ps: e.matmul(
                        ps[:, 0:512],
                        hddT[t][0:rows, 128 * jj:128 * (jj + 1)],
                        difT[t][0:rows, 512 * hf:512 * (hf + 1)],
                        start=(t == 0), stop=(t == 16)),
                        deps=dep, tag="grmm", psb=[ps]))
                ev = em.n("vector", lambda e, ps=ps, bi4=bi4:
                    e.tensor_scalar_mul(gev[bi4][:], ps[:, 0:512],
                                        2.0 / (16424.0 * 1024.0)),
                    deps=[mms[-1], wf.get(("gevb", bi4))], tag="grev",
                    psb=[ps])
                wf[("gps", id(ps))] = ev
                st_ = em.dma("sync", lambda e, jj=jj, hf=hf, bi4=bi4:
                    e.dma_start(out=d_grad[128 * jj:128 * (jj + 1),
                                           512 * hf:512 * (hf + 1)],
                                in_=gev[bi4][:]), deps=[ev], tag="gwst")
                wf[("gevb", bi4)] = st_
                gw_st.append(st_)
        agg = em.n("gpsimd", lambda e: e.collective_compute(
            "AllReduce", ALU.add, replica_groups=[list(range(8))],
            ins=[d_grad.ap().opt()], outs=[d_grado.ap().opt()]),
            w=1, tag="agg", deps=gw_st, coll=True)
        if "grad" in dbg_t:
            em.dma("gpsimd", lambda e: e.dma_start(
                out=dbg_t["grad"].ap(), in_=d_grado.ap()), deps=[agg],
                tag="dbggrad")
        wmw_nd = wloadL(0, ins["memWT"], dep=pred_nd[-1:])
        wn_nd = []
        for k in range(8):
            for hf in range(2):
                lg = em.dma("sync", lambda e, k=k, hf=hf: e.dma_start(
                    out=grl[:, 0:512],
                    in_=d_grado[128 * k:128 * (k + 1),
                                512 * hf:512 * (hf + 1)]),
                    deps=[agg, wf.get("grl")], tag="grll")
                t1 = em.n("vector", lambda e: e.tensor_scalar_mul(
                    grl[:, 0:512], grl[:, 0:512], fgc[:, 1:2]),
                    deps=[lg, fg3], tag="wnt1")
                t2 = em.n("vector", lambda e, k=k, hf=hf:
                    e.scalar_tensor_tensor(
                        wnew[k][:, 512 * hf:512 * (hf + 1)],
                        wbuf[0][k][:, 512 * hf:512 * (hf + 1)],
                        fgc[:, 0:1], grl[:, 0:512], ALU.mult, ALU.add),
                    deps=[t1, wmw_nd[k], fg2], tag="wnt2")
                wf["grl"] = t2
                wn_nd.append(t2)
        wnall = [wn_nd[2 * k + 1] for k in range(8)]
        mem_nd = fm_chain2(wnall, None, d_hdd, d_mem, AF.Identity, memb_s,
                           "mo0", wtiles=wnew,
                           src_park_of=lambda gi, k: hdd_nd[8 * gi + k])
        wo1_nd = wloadL(1, ins["WoT1"], dep=pred_nd[-1:] + [wn_nd[-1]])
        mo_nd = fm_chain2(wo1_nd, 1, d_mem, d_mo, AF.Identity, bo_s, "mo",
                          tok_only=True,
                          src_park_of=lambda gi, k: mem_nd[8 * gi + k])
        stack_g.close()
        if "mo" in dbg_t:
            em.dma("gpsimd", lambda e: e.dma_start(
                out=dbg_t["mo"].ap(), in_=d_mo.ap()), deps=mo_nd[-1:],
                tag="dbgmo")

        wga_nd = wloadL(0, ins["WgAT"], dep=mem_nd[-1:])
        wgb_nd = wloadL(1, ins["WgBT"], dep=mo_nd[-1:])
        gate_nd = fm_chain2(wga_nd, 0, ("T", d_x1sb), d_gate, AF.Sigmoid,
                            bg_s, "gt",
                            deps_extra=x1s_st[-1:] + mo_nd[-1:],
                            tok_only=True, src2_d=d_mo, wnd2=wgb_nd,
                            widx2=1)
        s_st = []
        prev = gate_nd[-1]
        for j in range(8):
            for g in range(4):
                off, gsz = 512 * g, 512
                lg_ = em.dma("sync", lambda e, j=j, off=off: e.dma_start(
                    out=csrc[0][:, 0:512],
                    in_=d_gate[128 * j:128 * (j + 1), off:off + 512]),
                    deps=[gate_nd[-1], wf.get("sg0")], tag="sgl")
                lxs = em.dma("sync", lambda e, j=j, off=off:
                    e.dma_start_transpose(
                        csrc[1][:, 0:512],
                        d_x1sb[off:off + 512, 128 * j:128 * (j + 1)]),
                    deps=[x1s_st[-1], wf.get("sg1")], tag="sxl")
                lmo = em.dma("sync", lambda e, j=j, off=off: e.dma_start(
                    out=csrc[2][:, 0:512],
                    in_=d_mo[128 * j:128 * (j + 1), off:off + 512]),
                    deps=[mo_nd[-1], wf.get("sg2")], tag="sml")
                v1 = em.n("vector", lambda e: e.tensor_mul(
                    csrc[3][:, 0:512], csrc[0][:, 0:512],
                    csrc[1][:, 0:512]), deps=[lg_, lxs, wf.get("sg3")],
                    tag="cu1")
                v2 = em.n("vector", lambda e: e.tensor_mul(
                    csrc[0][:, 0:512], csrc[0][:, 0:512],
                    csrc[2][:, 0:512]), deps=[v1, lmo], tag="cu2")
                wf["sg0"] = v2
                v3 = em.n("vector", lambda e: e.tensor_add(
                    csrc[1][:, 0:512], csrc[1][:, 0:512],
                    csrc[3][:, 0:512]), deps=[v1, v2], tag="cu3")
                wf["sg3"] = v3
                v4 = em.n("vector", lambda e: e.tensor_add(
                    csrc[1][:, 0:512], csrc[1][:, 0:512],
                    csrc[2][:, 0:512]), deps=[v3], tag="cu4")
                wf["sg2"] = v4
                v5 = em.n("vector", lambda e: e.tensor_sub(
                    csrc[1][:, 0:512], csrc[1][:, 0:512],
                    csrc[0][:, 0:512]), deps=[v4], tag="cu5")
                st_ = em.dma("sync", lambda e, j=j, off=off: e.dma_start(
                    out=d_s[128 * j:128 * (j + 1), off:off + 512],
                    in_=csrc[1][:, 0:512]), deps=[v5], tag="spark")
                wf["sg1"] = st_
                s_st.append(st_)
        x2_st = []
        for t in range(NT):
            bi = t % 2
            ls = em.dma("sync", lambda e, t=t, bi=bi: e.dma_start_transpose(
                xc[bi][:], d_s[:, 128 * t:128 * (t + 1)]),
                deps=[s_st[-1], wf.get(("x2c", bi))], tag="lstm")
            cp = em.n("vector", lambda e, bi=bi: e.tensor_copy(
                xb_[bi][:], xc[bi][:]), deps=[ls, wf.get(("x2b", bi))],
                tag="x2cp")
            b1 = em.n("vector", lambda e, bi=bi: e.bn_stats(
                st6[:, 0:6], xb_[bi][:, 0:512]), deps=[cp, wf.get("lnb")],
                tag="x2b1")
            b2 = em.n("vector", lambda e, bi=bi: e.bn_stats(
                st6[:, 6:12], xb_[bi][:, 512:1024]), deps=[cp], tag="x2b2")
            b3 = em.n("vector", lambda e: e.bn_aggr(mv[:], st6[:]),
                      deps=[b1, b2], tag="x2b3")
            sq = em.n("scalar", lambda e: e.activation(
                rstd[:], mv[:, 1:2], AF.Sqrt, bias=epsln[:]),
                deps=[b3], tag="x2sq")
            rc = em.n("vector", lambda e: e.reciprocal(rstd[:], rstd[:]),
                      deps=[sq], tag="x2rc")
            n1 = em.n("vector", lambda e, bi=bi: e.tensor_scalar(
                xb_[bi][:], xb_[bi][:], mv[:, 0:1], rstd[:],
                ALU.subtract, ALU.mult), deps=[rc], tag="x2n1")
            n2 = em.n("vector", lambda e, bi=bi: e.tensor_mul(
                xb_[bi][:], xb_[bi][:], gB[2][:]), deps=[n1, gB_nd[2]],
                tag="x2n2")
            n3 = em.n("vector", lambda e, bi=bi: e.tensor_add(
                xb_[bi][:], xb_[bi][:], gB[3][:]), deps=[n2, gB_nd[3]],
                tag="x2n3")
            wf["lnb"] = n3
            c_ = em.n("vector", lambda e, bi=bi: e.tensor_copy(
                xc[bi][:], xb_[bi][:]), deps=[n3], tag="x2cb")
            wf[("x2c", bi)] = c_
            s1_ = em.dma("sync", lambda e, t=t, bi=bi: e.dma_start(
                out=d_x2[128 * t:128 * (t + 1), :], in_=xb_[bi][:]),
                deps=[n3], tag="x2st")
            s2_ = em.dma("sync", lambda e, t=t, bi=bi: e.dma_start(
                out=d_x2b[128 * t:128 * (t + 1), :], in_=xc[bi][:]),
                deps=[c_], tag="x2bst")
            wf[("x2b", bi)] = s1_
            x2_st += [s1_, s2_]
        if "x2" in dbg_t:
            em.dma("gpsimd", lambda e: e.dma_start(
                out=dbg_t["x2"].ap(), in_=d_x2.ap()), deps=x2_st[-2:],
                tag="dbgx2")

        stack_f4 = ExitStack()
        sbF = lambda n, s, d: stack_f4.enter_context(
            nc.sbuf_tensor(n, list(s), d))
        wE1 = [sbF(f"wE1_{k}", (128, 4096), BF16) for k in range(8)]
        wE1_nd = []
        for k in range(8):
            wE1_nd.append(em.dma("gpsimd", lambda e, k=k: e.dma_start(
                out=wE1[k][:], in_=ins["W1T"][128 * k:128 * (k + 1), :]),
                deps=[wf.get(("wl", 0, k)), wf.get(("wl", 1, k))],
                tag="wE1l"))
        h_st = []
        for g in range(4):
            x2l = []
            for k in range(8):
                x2l.append(em.dma("sync", lambda e, g=g, k=k:
                    e.dma_start_transpose(
                        csrc[k][:, 0:512],
                        d_x2b[512 * g:512 * (g + 1),
                              128 * k:128 * (k + 1)]),
                    deps=[x2_st[-1], wf.get(("cs", k))], tag="ffxl"))
            for oc in range(32):
                ps = ps1 if oc % 2 == 0 else ps2
                mms = []
                for k in range(8):
                    dep = [wE1_nd[k], x2l[k]]
                    if k == 0:
                        dep.append(wf.get(("ff", id(ps))))
                    mms.append(em.n("tensor", lambda e, oc=oc, k=k,
                                    ps=ps: e.matmul(
                        ps[:, 0:512],
                        wE1[k][:, 128 * oc:128 * (oc + 1)],
                        csrc[k][:, 0:512],
                        start=(k == 0), stop=(k == 7)), deps=dep,
                        tag="h1mm", psb=[ps]))
                pki = oc % 4
                ev = em.n("scalar", lambda e, oc=oc, ps=ps, pki=pki:
                    e.activation(pkt[pki][:], ps[:, 0:512], AF.Silu,
                                 bias=b1_s[:, oc:oc + 1]),
                    deps=[mms[-1], cl0[9], wf.get(("pk", pki))],
                    tag="h1ev", psb=[ps])
                wf[("ff", id(ps))] = ev
                st_ = em.dma("sync", lambda e, g=g, oc=oc, pki=pki:
                    e.dma_start(out=d_h[128 * oc:128 * (oc + 1),
                                        512 * g:512 * (g + 1)],
                                in_=pkt[pki][:]), deps=[ev], tag="hpark")
                wf[("pk", pki)] = st_
                h_st.append(st_)
            for k in range(8):
                wf[("cs", k)] = mms[7]
        stack_f4.close()
        stack_f5 = ExitStack()
        sbF5 = lambda n, s, d: stack_f5.enter_context(
            nc.sbuf_tensor(n, list(s), d))
        wE2 = [sbF5(f"wE2_{oc}", (128, D), BF16) for oc in range(32)]
        hsl = [sbF5(f"hsl{oc}", (128, 512), BF16) for oc in range(32)]
        wE2_nd = []
        for oc in range(32):
            wE2_nd.append(em.dma("gpsimd", lambda e, oc=oc: e.dma_start(
                out=wE2[oc][:], in_=ins["W2T"][128 * oc:128 * (oc + 1), :]),
                deps=[wE1_nd[oc % 8], h_st[-1]], tag="wE2l"))
        ff_st = []
        for g in range(4):
            hsl_nd = []
            for oc in range(32):
                hsl_nd.append(em.dma("sync", lambda e, g=g, oc=oc:
                    e.dma_start(out=hsl[oc][:],
                                in_=d_h[128 * oc:128 * (oc + 1),
                                        512 * g:512 * (g + 1)]),
                    deps=[h_st[-1], wf.get(("hsl", oc))], tag="hsll"))
            for j in range(8):
                ps = ps1 if j % 2 == 0 else ps2
                mms = []
                for oc in range(32):
                    dep = [wE2_nd[oc], hsl_nd[oc]]
                    if oc == 0:
                        dep.append(wf.get(("ff2", id(ps))))
                    mms.append(em.n("tensor", lambda e, g=g, j=j, oc=oc,
                                    ps=ps: e.matmul(
                        ps[:, 0:512],
                        wE2[oc][:, 128 * j:128 * (j + 1)], hsl[oc][:],
                        start=(oc == 0), stop=(oc == 31)), deps=dep,
                        tag="ffmm", psb=[ps]))
                pki = j % 4
                ev = em.n("scalar", lambda e, j=j, ps=ps, pki=pki:
                    e.activation(pkt[pki][:], ps[:, 0:512], AF.Identity,
                                 bias=b2f_s[:, j:j + 1]),
                    deps=[mms[-1], cl0[8], wf.get(("pktf", pki))],
                    tag="ffev", psb=[ps])
                wf[("ff2", id(ps))] = ev
                st_ = em.dma("sync", lambda e, g=g, j=j, pki=pki:
                    e.dma_start(out=d_ffb[128 * j:128 * (j + 1),
                                          512 * g:512 * (g + 1)],
                                in_=pkt[pki][:]), deps=[ev], tag="ffpark")
                wf[("pktf", pki)] = st_
                ff_st.append(st_)
            for oc in range(32):
                wf[("hsl", oc)] = mms[-1]
        stack_f5.close()
        a1_st = []
        for t in range(NT):
            bi = t % 2
            lf = em.dma("sync", lambda e, t=t, bi=bi: e.dma_start_transpose(
                xc[bi][:], d_ffb[:, 128 * t:128 * (t + 1)]),
                deps=[ff_st[-1], wf.get(("a1c", bi))], tag="lff")
            lx2 = em.dma("sync", lambda e, t=t, bi=bi: e.dma_start(
                out=xa[bi][:], in_=d_x2[128 * t:128 * (t + 1), :]),
                deps=[x2_st[-2], wf.get(("a1a", bi))], tag="lx2")
            ad = em.n("vector", lambda e, bi=bi: e.tensor_add(
                xb_[bi][:], xa[bi][:], xc[bi][:]),
                deps=[lf, lx2, wf.get(("a1b", bi))], tag="a1add")
            wf[("a1c", bi)] = ad
            wf[("a1a", bi)] = ad
            b1 = em.n("vector", lambda e, bi=bi: e.bn_stats(
                st6[:, 0:6], xb_[bi][:, 0:512]), deps=[ad, wf.get("lnb")],
                tag="a1b1")
            b2 = em.n("vector", lambda e, bi=bi: e.bn_stats(
                st6[:, 6:12], xb_[bi][:, 512:1024]), deps=[ad], tag="a1b2")
            b3 = em.n("vector", lambda e: e.bn_aggr(mv[:], st6[:]),
                      deps=[b1, b2], tag="a1b3")
            sq = em.n("scalar", lambda e: e.activation(
                rstd[:], mv[:, 1:2], AF.Sqrt, bias=epsln[:]),
                deps=[b3], tag="a1sq")
            rc = em.n("vector", lambda e: e.reciprocal(rstd[:], rstd[:]),
                      deps=[sq], tag="a1rc")
            n1 = em.n("vector", lambda e, bi=bi: e.tensor_scalar(
                xb_[bi][:], xb_[bi][:], mv[:, 0:1], rstd[:],
                ALU.subtract, ALU.mult), deps=[rc], tag="a1n1")
            n2 = em.n("vector", lambda e, bi=bi: e.tensor_mul(
                xb_[bi][:], xb_[bi][:], gB[4][:]), deps=[n1, gB_nd[4]],
                tag="a1n2")
            n3 = em.n("vector", lambda e, bi=bi: e.tensor_add(
                xb_[bi][:], xb_[bi][:], gB[5][:]), deps=[n2, gB_nd[5]],
                tag="a1n3")
            wf["lnb"] = n3
            wf[("a1b", bi)] = n3
            s1_ = em.dma("sync", lambda e, t=t, bi=bi: e.dma_start(
                out=d_a1[128 * t:128 * (t + 1), :], in_=xb_[bi][:]),
                deps=[n3], tag="a1st")
            wf[("a1a", bi)] = s1_
            a1_st.append(s1_)
        if "a1" in dbg_t:
            em.dma("gpsimd", lambda e: e.dma_start(
                out=dbg_t["a1"].ap(), in_=d_a1.ap()), deps=a1_st[-1:],
                tag="dbga1")

        t_id128f = nc.inline_tensor(np.eye(128, dtype=np.float32), "t_id128f")
        id128f = sb("id128f", (128, 128), F32)
        cm1 = em.dma("sync", lambda e: e.dma_start(
            out=id128f[:], in_=t_id128f.ap()), deps=[ff_st[-1]], tag="cm1")
        cm2 = cm1
        lg_tm = [sb(f"lgtm{t}", (128, 8), F32) for t in range(NT)]
        oh_tm = [sb(f"ohtm{t}", (128, 8), F32) for t in range(NT)]
        wfin = [sb(f"wfin{t}", (128, 1), F32) for t in range(NT)]
        eix = [sb(f"eix{t}", (128, 1), F32) for t in range(NT)]
        idxu = [sb(f"idxu{t}", (128, 1), U32) for t in range(NT)]
        stack_m2 = ExitStack()
        sbM2 = lambda n, s, d: stack_m2.enter_context(
            nc.sbuf_tensor(n, list(s), d))
        ohf = sbM2("ohf", (8, T), F32)
        rkf = sbM2("rkf", (8, T), F32)
        zr8 = sbM2("zr8", (8, 16), F32)
        cnt_sb = sbM2("cnt_sb", (8, 8), F32)
        base8 = sbM2("base8", (8, 1), F32)
        brow = sbM2("brow", (1, 8), F32)
        baseB = sbM2("baseB", (128, 8), F32)
        mx8 = sbM2("mx8", (128, 8), F32)
        idx8 = sbM2("idx8", (128, 8), U32)
        nm2 = sbM2("nm2", (128, 1), F32)
        den2 = sbM2("den2", (128, 1), F32)
        exps = sbM2("exps", (128, 8), F32)
        idxf = sbM2("idxf", (128, 1), F32)
        tidt = sbM2("tidt", (128, 1), F32)
        tmp8 = sbM2("tmp8", (128, 8), F32)
        lrk = sbM2("lrk", (128, 1), F32)
        kpc = sbM2("kpc", (128, 1), F32)
        bsb = sbM2("bsb", (128, 1), F32)
        gw_sb = sbM2("gw_sb", (128, 64), BF16)
        zrow = sbM2("zrow", (128, D), BF16)
        xnt_m = [sbM2(f"xntm{i}", (128, D), BF16) for i in range(2)]

        stack_m1 = ExitStack()
        xnf3 = [stack_m1.enter_context(
            nc.sbuf_tensor(f"xnf3_{j}", [128, T], BF16)) for j in range(8)]
        lgf = stack_m1.enter_context(nc.sbuf_tensor("lgf", [8, T], F32))
        xnf3_nd = []
        for j in range(8):
            xnf3_nd.append(em.dma("sync", lambda e, j=j:
                e.dma_start_transpose(xnf3[j][:],
                                      d_xe[16:16 + T, 128 * j:128 * (j + 1)]),
                deps=xe_st + [ff_st[-1]], tag=f"xnf3{j}"))
        gw_nd = []
        for k in range(8):
            gw_nd.append(em.dma("gpsimd", lambda e, k=k: e.dma_start(
                out=gw_sb[:, 8 * k:8 * (k + 1)],
                in_=ins["gwT"][128 * k:128 * (k + 1), :]),
                deps=[wE2_nd[k], ff_st[-1]], tag="gwl"))
        lg_nd = []
        for g in range(4):
            mms = []
            for k in range(8):
                dep = [gw_nd[k], xnf3_nd[k]]
                if k == 0:
                    dep.append(wf.get("lgps"))
                mms.append(em.n("tensor", lambda e, g=g, k=k: e.matmul(
                    ps3[0:8, 0:512], gw_sb[:, 8 * k:8 * (k + 1)],
                    xnf3[k][:, 512 * g:512 * (g + 1)],
                    start=(k == 0), stop=(k == 7)), deps=dep, tag="lgmm",
                    psb=[ps3]))
            ev = em.n("vector", lambda e, g=g: e.tensor_copy(
                lgf[0:8, 512 * g:512 * (g + 1)], ps3[0:8, 0:512]),
                deps=[mms[-1]], tag="lgev", psb=[ps3])
            wf["lgps"] = ev
            lg_nd.append(ev)
        lgt_nd = []
        for t in range(NT):
            tp = em.n("tensor", lambda e, t=t: e.transpose(
                ps4[:, 0:8], lgf[0:8, 128 * t:128 * (t + 1)], id16[0:8, 0:8]),
                deps=[lg_nd[-1], cm1, wf.get("lgtp")], tag="lgtp", psb=[ps4])
            ev = em.n("vector", lambda e, t=t: e.tensor_copy(
                lg_tm[t][:], ps4[:, 0:8]), deps=[tp], tag="lgte", psb=[ps4])
            wf["lgtp"] = ev
            lgt_nd.append(ev)
        stack_m1.close()

        z8 = em.n("vector", lambda e: e.memset(zr8[:], 0.0),
                  deps=[lgt_nd[-1]], tag="z8")
        rt_nd = []
        oht_nd = {}
        for t in range(NT):
            m1 = em.n("vector", lambda e, t=t: e.max(mx8[:], lg_tm[t][:]),
                      deps=[lgt_nd[t], wf.get("mx8")], tag="rmax")
            m2 = em.n("vector", lambda e, t=t: e.max_index(
                idx8[:], mx8[:], lg_tm[t][:]), deps=[m1], tag="ridx")
            ng = em.n("vector", lambda e: e.tensor_scalar_mul(
                nm2[:], mx8[:, 0:1], -1.0), deps=[m1], tag="rng")
            ex = em.n("scalar", lambda e, t=t: e.activation(
                exps[:], lg_tm[t][:], AF.Exp, bias=nm2[:],
                accum_out=den2[:]), deps=[ng, wf.get("exps")], tag="rexp")
            rcp = em.n("vector", lambda e, t=t: e.reciprocal(
                wfin[t][:], den2[:]), deps=[ex], tag="rrcp")
            cv = em.n("vector", lambda e, t=t: e.tensor_copy(
                eix[t][:], idx8[:, 0:1]), deps=[m2], tag="rcv")
            oh = em.n("vector", lambda e, t=t: e.tensor_single_scalar(
                oh_tm[t][:], iota8[:], eix[t][:], ALU.is_equal),
                deps=[cv, cl0[16]], tag="roh")
            oht_nd[t] = oh
            wf["mx8"] = ex
            wf["exps"] = rcp
            wf["idxf"] = oh
            tp = em.n("tensor", lambda e, t=t: e.transpose(
                ps4[0:8, 0:128], oh_tm[t][:], id128f[:]),
                deps=[oh, cm1, wf.get("ohtp")], tag="ohtp", psb=[ps4])
            ev = em.n("vector", lambda e, t=t: e.tensor_copy(
                ohf[0:8, 128 * t:128 * (t + 1)], ps4[0:8, 0:128]),
                deps=[tp], tag="ohte", psb=[ps4])
            wf["ohtp"] = ev
            rt_nd.append(ev)
        sc_ = em.n("vector", lambda e: e.tensor_tensor_scan(
            rkf[:], ohf[:],
            zr8[:, 0:1].broadcast_to([8, T]), 0.0, ALU.add, ALU.add),
            deps=rt_nd + [z8], tag="rscan")
        cw = em.dma("sync", lambda e: e.dma_start(
            out=d_cnt.ap(), in_=rkf[0:8, T - 1:T]), deps=[sc_], tag="cw")
        agc2 = em.n("gpsimd", lambda e: e.collective_compute(
            "AllGather", ALU.bypass, replica_groups=[list(range(8))],
            ins=[d_cnt.ap().opt()], outs=[d_cnto.ap().opt()]),
            w=1, tag="agc2", deps=[cw], coll=True)
        exc = em.n("vector", lambda e: e.tensor_sub(
            rkf[:], rkf[:], ohf[:]), deps=[sc_, cw], tag="rexc")
        lcn = em.dma("sync", lambda e: e.dma_start(
            out=cnt_sb[:],
            in_=d_cnto.ap().rearrange("(r e) one -> r (e one)", e=8)),
            deps=[agc2], tag="lcn")
        bs1 = em.n("tensor", lambda e: e.matmul(
            ps3[0:8, 0:1], cnt_sb[:], pri_s[:], start=True, stop=True),
            deps=[lcn, cl0[11], wf.get("lgps")], tag="bs1", psb=[ps3])
        bs2 = em.n("vector", lambda e: e.tensor_copy(
            base8[:], ps3[0:8, 0:1]), deps=[bs1], tag="bs2", psb=[ps3])
        bs3 = em.n("tensor", lambda e: e.transpose(
            ps3[0:1, 0:8], base8[:], id16[0:8, 0:8]), deps=[bs2],
            tag="bs3", psb=[ps3])
        bs4 = em.n("vector", lambda e: e.tensor_copy(
            brow[:], ps3[0:1, 0:8]), deps=[bs3], tag="bs4", psb=[ps3])
        bs5 = em.n("tensor", lambda e: e.matmul(
            ps3[:, 0:8], ones1f2[:], brow[:], start=True, stop=True),
            deps=[bs4, cm2], tag="bs5", psb=[ps3])
        bs6 = em.n("vector", lambda e: e.tensor_copy(
            baseB[:], ps3[:, 0:8]), deps=[bs5], tag="bs6", psb=[ps3])
        zrn = em.n("vector", lambda e: e.memset(zrow[:], 0.0),
                   deps=[lgt_nd[-1]], tag="zrn")
        dz_st = []
        for r in range(24):
            dz_st.append(em.dma("sync", lambda e, r=r: e.dma_start(
                out=d_disp[128 * r:128 * (r + 1), :], in_=zrow[:]),
                deps=[zrn], tag="dz"))
        sc_st = []
        for t in range(NT):
            bi = t % 2
            tp = em.n("tensor", lambda e, t=t: e.transpose(
                ps4[:, 0:8], rkf[0:8, 128 * t:128 * (t + 1)],
                id16[0:8, 0:8]), deps=[exc, wf.get("rktp")], tag="rktp",
                psb=[ps4])
            ev = em.n("vector", lambda e, t=t: e.tensor_copy(
                tmp8[:], ps4[:, 0:8]), deps=[tp, wf.get("tmp8")],
                tag="rkte", psb=[ps4])
            wf["rktp"] = ev
            g1_ = em.n("vector", lambda e: e.tensor_add(
                tmp8[:], tmp8[:], baseB[:]), deps=[ev, bs6], tag="rgr")
            k1 = em.n("vector", lambda e: e.tensor_single_scalar(
                exps[:], tmp8[:], float(CAP), ALU.is_lt), deps=[g1_,
                wf.get("exps2")], tag="rkp1")
            k2 = em.n("vector", lambda e, t=t: e.tensor_mul(
                exps[:], exps[:], oh_tm[t][:]), deps=[k1], tag="rkp2")
            kc = em.n("vector", lambda e: e.reduce_sum(
                kpc[:], exps[:], axis=AXL.X), deps=[k2], tag="rkc")
            l1_ = em.n("vector", lambda e, t=t: e.tensor_mul(
                tmp8[:], tmp8[:], oh_tm[t][:]), deps=[g1_, k2], tag="rlr1")
            l2_ = em.n("vector", lambda e: e.reduce_sum(
                lrk[:], tmp8[:], axis=AXL.X), deps=[l1_], tag="rlr2")
            wf["tmp8"] = l2_
            wf["exps2"] = kc
            wm_ = em.n("vector", lambda e, t=t: e.tensor_mul(
                wfin[t][:], wfin[t][:], kpc[:]), deps=[kc], tag="rwm")
            b1_ = em.n("vector", lambda e, t=t: e.tensor_mul(
                exps[:], baseB[:], oh_tm[t][:]), deps=[wm_, kc], tag="rb1")
            b2_ = em.n("vector", lambda e: e.reduce_sum(
                bsb[:], exps[:], axis=AXL.X), deps=[b1_], tag="rb2")
            wf["exps2"] = b2_
            l3_ = em.n("vector", lambda e: e.tensor_sub(
                lrk[:], lrk[:], bsb[:]), deps=[l2_, b2_], tag="rl3")
            rt = em.n("gpsimd", lambda e, t=t: e.iota(
                tidt[:], [[0, 1]], base=4096 + 128 * t,
                channel_multiplier=1,
                allow_small_or_imprecise_dtypes=True),
                deps=[wf.get("tidt"), exc], tag="riota")
            s1_ = em.n("vector", lambda e, t=t: e.scalar_tensor_tensor(
                idxf[:], eix[t][:], float(CAPL), lrk[:], ALU.mult, ALU.add),
                deps=[l3_, rt, wf.get("idxf2")], tag="rs1")
            s2_ = em.n("vector", lambda e: e.tensor_sub(
                idxf[:], idxf[:], tidt[:]), deps=[s1_], tag="rs2")
            s3_ = em.n("vector", lambda e, t=t: e.scalar_tensor_tensor(
                idxf[:], idxf[:], kpc[:], tidt[:], ALU.mult, ALU.add),
                deps=[s2_], tag="rs3")
            wf["tidt"] = s3_
            wf["idxf2"] = s3_
            cu = em.n("vector", lambda e, t=t: e.tensor_copy(
                idxu[t][:], idxf[:]), deps=[s3_], tag="rcu")
            lxn = em.dma("sync", lambda e, t=t, bi=bi: e.dma_start(
                out=xnt_m[bi][:], in_=d_xe[16 + 128 * t:16 + 128 * (t + 1), :]),
                deps=[xe_st[-1], wf.get(("xnm", bi)), lgt_nd[-1]],
                tag="lxnm")
            sct = em.dma("gpsimd", lambda e, t=t, bi=bi:
                e.indirect_dma_start(
                    d_disp.ap(), IndirectOffsetOnAxis(idxu[t][:], 0),
                    xnt_m[bi][:], None, bounds_check=8 * CAPL - 1,
                    oob_is_err=False),
                deps=[cu, lxn, dz_st[-1]], tag="sct")
            wf[("xnm", bi)] = sct
            sc_st.append(sct)
        stack_m2.close()

        stack_m3 = ExitStack()
        sbM = lambda n, s, d: stack_m3.enter_context(
            nc.sbuf_tensor(n, list(s), d))
        wEx = [sbM(f"wEx{k}", (128, 4096), BF16) for k in range(8)]
        wE2x = [sbM(f"wE2x{oc}", (128, D), BF16) for oc in range(14)] \
            + [gB[4], gB[5]] + wbuf[0] + wbuf[1]
        dfm = csrc
        he = [sbM(f"he{oc}", (128, CAPL), BF16) for oc in range(16)] \
            + csc2 + pkt + gB[0:4]
        b1e_sb = sbM("b1e_sb", (128, 32), F32)
        b2row = rows6f
        b2B = xc[0]
        eo_t = [sbM(f"eo{i}", (128, 512), BF16) for i in range(2)]
        eout_st = []
        for ei in range(8):
            w1l, dfl = [], []
            for k in range(8):
                w1l.append(em.dma("gpsimd", lambda e, ei=ei, k=k:
                    e.dma_start(out=wEx[k][:],
                                in_=ins["w1e"][ei, 128 * k:128 * (k + 1), :]),
                    deps=[wf.get(("wEx", k)), gw_nd[-1], a1_st[-1],
                          sc_st[-1]],
                    tag="w1el"))
                dfl.append(em.dma("sync", lambda e, ei=ei, k=k:
                    e.dma_start_transpose(
                        dfm[k][:, 0:CAPL],
                        d_disp[CAPL * ei:CAPL * (ei + 1),
                               128 * k:128 * (k + 1)]),
                    deps=sc_st + [wf.get(("dfm", k)),
                                  a1_st[-1] if ei == 0 else None],
                    tag="dfml"))
            b1l = em.dma("sync", lambda e, ei=ei: e.dma_start(
                out=b1e_sb[:], in_=ins["b1e_c"][ei]),
                deps=[wf.get("b1e"), sc_st[-1] if ei == 0 else None],
                tag="b1el")
            he_nd = []
            for oc in range(32):
                ps = ps5 if oc % 2 == 0 else ps6
                mms = []
                for k in range(8):
                    dep = [w1l[k], dfl[k]]
                    if k == 0:
                        dep.append(wf.get(("eps", id(ps))))
                    mms.append(em.n("tensor", lambda e, oc=oc, k=k, ps=ps:
                        e.matmul(ps[:, 0:CAPL],
                                 wEx[k][:, 128 * oc:128 * (oc + 1)],
                                 dfm[k][:, 0:CAPL],
                                 start=(k == 0), stop=(k == 7)),
                        deps=dep, tag="e1mm", psb=[ps]))
                ev = em.n("vector", lambda e, oc=oc, ps=ps: e.tensor_scalar(
                    he[oc][:, 0:CAPL], ps[:, 0:CAPL], b1e_sb[:, oc:oc + 1],
                    0.0, ALU.add, ALU.max), deps=[mms[-1], b1l,
                    wf.get(("he", oc))], tag="e1ev", psb=[ps])
                wf[("eps", id(ps))] = ev
                he_nd.append(ev)
            w2l = []
            for oc in range(32):
                w2l.append(em.dma("gpsimd", lambda e, ei=ei, oc=oc:
                    e.dma_start(
                        out=wE2x[oc][:],
                        in_=ins["w2e"][ei, 128 * oc:128 * (oc + 1), :]),
                    deps=[wf.get(("wE2x", oc)), a1_st[-1] if ei == 0 else None,
                          sc_st[-1] if ei == 0 else None], tag="w2el"))
            b2l = em.dma("gpsimd", lambda e, ei=ei: e.dma_start(
                out=b2row[:], in_=ins["b2e_r"][ei:ei + 1, :]),
                deps=[wf.get("b2e"), a1_st[-1] if ei == 0 else None],
                tag="b2el")
            bb1 = em.n("tensor", lambda e: e.matmul(
                ps3[:, 0:512], one1b[:], b2row[0:1, 0:512],
                start=True, stop=True), deps=[b2l, wf.get("b2ps")],
                tag="bb1", psb=[ps3])
            bb2 = em.n("vector", lambda e: e.tensor_copy(
                b2B[:, 0:512], ps3[:, 0:512]), deps=[bb1,
                wf.get("b2B"), a1_st[-1] if ei == 0 else None],
                tag="bb2", psb=[ps3])
            bb3 = em.n("tensor", lambda e: e.matmul(
                ps3[:, 0:512], one1b[:], b2row[0:1, 512:1024],
                start=True, stop=True), deps=[bb2], tag="bb3", psb=[ps3])
            bb4 = em.n("vector", lambda e: e.tensor_copy(
                b2B[:, 512:1024], ps3[:, 0:512]), deps=[bb3], tag="bb4",
                psb=[ps3])
            wf["b2ps"] = bb4
            for rt_ in range(3):
                for hf in range(2):
                    ps = ps1 if hf == 0 else ps2
                    mms2 = []
                    for oc in range(32):
                        dep = [w2l[oc], he_nd[oc]]
                        if oc == 0:
                            dep.append(wf.get(("e2ps", id(ps))))
                        mms2.append(em.n("tensor", lambda e, rt_=rt_, hf=hf,
                                         oc=oc, ps=ps: e.matmul(
                            ps[:, 0:512],
                            he[oc][:, 128 * rt_:128 * (rt_ + 1)],
                            wE2x[oc][:, 512 * hf:512 * (hf + 1)],
                            start=(oc == 0), stop=(oc == 31)),
                            deps=dep, tag="e2mm", psb=[ps]))
                    bi3 = (rt_ * 2 + hf) % 2
                    ev = em.n("vector", lambda e, hf=hf, ps=ps, bi3=bi3:
                        e.tensor_add(eo_t[bi3][:], ps[:, 0:512],
                                     b2B[:, 512 * hf:512 * (hf + 1)]),
                        deps=[mms2[-1], bb4, wf.get(("eo", bi3))],
                        tag="e2ev", psb=[ps])
                    wf[("e2ps", id(ps))] = ev
                    st_ = em.dma("sync", lambda e, ei=ei, rt_=rt_, hf=hf,
                                 bi3=bi3: e.dma_start(
                        out=d_eout[CAPL * ei + 128 * rt_:
                                   CAPL * ei + 128 * (rt_ + 1),
                                   512 * hf:512 * (hf + 1)],
                        in_=eo_t[bi3][:]), deps=[ev], tag="eopark")
                    wf[("eo", bi3)] = st_
                    eout_st.append(st_)
            for k in range(8):
                wf[("wEx", k)] = mms[-1]
                wf[("dfm", k)] = mms[-1]
            for oc in range(32):
                wf[("wE2x", oc)] = mms2[-1]
            for oc in range(32):
                wf[("he", oc)] = mms2[-1]
            wf["b1e"] = he_nd[-1]
            wf["b2e"] = bb4
            wf["b2B"] = ev
        stack_m3.close()

        moeg = sb("moeg", (128, D), BF16)
        fin_st = []
        for t in range(NT):
            bi = t % 2
            gt = em.dma("gpsimd", lambda e, t=t: e.indirect_dma_start(
                moeg[:], None, d_eout.ap(),
                IndirectOffsetOnAxis(idxu[t][:], 0),
                bounds_check=8 * CAPL - 1, oob_is_err=False),
                deps=eout_st + [wf.get("moeg")], tag="gth")
            ct = em.n("vector", lambda e, t=t: e.tensor_scalar_mul(
                xb_[0][:], moeg[:], wfin[t][:]),
                deps=[gt, wf.get("fb0")], tag="fmul")
            wf["moeg"] = ct
            if "moe" in dbg_t:
                em.dma("sync", lambda e, t=t: e.dma_start(
                    out=dbg_t["moe"][128 * t:128 * (t + 1), :],
                    in_=xb_[0][:]), deps=[ct], tag="dbgmoe")
            la = em.dma("sync", lambda e, t=t: e.dma_start(
                out=xa[0][:], in_=ins["x1in"][128 * t:128 * (t + 1), :]),
                deps=[wf.get("fa0")], tag="fl1")
            a1 = em.n("vector", lambda e: e.tensor_add(
                xb_[0][:], xb_[0][:], xa[0][:]), deps=[la, ct], tag="fadd1")
            wf["fa0"] = a1
            lb = em.dma("sync", lambda e, t=t: e.dma_start(
                out=xa[0][:], in_=d_a1[128 * t:128 * (t + 1), :]),
                deps=[a1, a1_st[-1]], tag="fl2")
            a2 = em.n("vector", lambda e: e.tensor_add(
                xb_[0][:], xb_[0][:], xa[0][:]), deps=[lb], tag="fadd2")
            wf["fa0"] = a2
            st_ = em.dma("sync", lambda e, t=t: e.dma_start(
                out=yout[128 * t:128 * (t + 1), :], in_=xb_[0][:]),
                deps=[a2], tag="yst")
            wf["fb0"] = st_
            fin_st.append(st_)
        if "idx" in dbg_t:
            for t in range(NT):
                em.dma("sync", lambda e, t=t: e.dma_start(
                    out=dbg_t["idx"][128 * t:128 * (t + 1), :],
                    in_=idxu[t][:]), deps=fin_st, tag="dbgidx")

        if ctx is None:
            em.build(block, stack)
    return nc


def build_fused():
    nc = bass.Bass("TRN2", target_bir_lowering=False)
    em = Em(nc)
    stack = ExitStack()
    with stack:
        block = stack.enter_context(nc.Block())
        x1h = build_l0(ctx=(nc, em, block))
        build_l1(ctx=(nc, em, block), x1_src=x1h)
        em.build(block, stack)
    return nc



def kernel(**inputs):
    maps = stage_inputs(inputs)
    nc = build_fused()
    res = run_bass_kernel_spmd(nc, maps, core_ids=list(range(NCORE)))
    out = np.zeros((B, S, D), np.float32)
    for c in range(NCORE):
        b, hc = c // 2, c % 2
        out[b, hc * T:(hc + 1) * T] = res.results[c]["yout"]
    return out
